# version 05
# baseline (speedup 1.0000x reference)
import numpy as np
import ml_dtypes
import concourse.bass as bass
import concourse.mybir as mybir
from concourse.bass_utils import run_bass_kernel_spmd
from contextlib import ExitStack

F32 = mybir.dt.float32
BF16 = mybir.dt.bfloat16
AF = mybir.ActivationFunctionType
AL = mybir.AluOpType
AX = mybir.AxisListType

MAGIC = 12582912.0
P = 128
TL = 2048
NT = TL // 512
TH = 512
NTH = TL // TH
D = 1024
ND = D // P
SEQ = 4096
NCORES = 8
LN_EPS = 1e-5
RMS_EPS = 1.1920929e-07
H_FFN = 2048
NH = H_FFN // P
NE = 8
NS = 16
I_FFN = 3072
NHEAD = 16
NKV = 4
DH = 64
SEQT = SEQ // 512
NPT = 4

BUILD_STAGE = 6
DBG_ZERO_SINI = False


class Eng:
    def __init__(self, sem, dma_step=1):
        self.sem = sem
        self.n = 0
        self.step = dma_step
        self.waited = {}

    def wait(self, g, *toks):
        for t in toks:
            if t is None:
                continue
            sem, v = t
            if self.waited.get(sem, -1) >= v:
                continue
            g.wait_ge(sem, v)
            self.waited[sem] = v

    def tok(self, instr):
        instr.then_inc(self.sem, self.step)
        self.n += self.step
        return (self.sem, self.n)


def build(stage=BUILD_STAGE):
    nc = bass.Bass("TRN2", target_bir_lowering=False)

    xin = nc.dram_tensor("xin", [D, TL], F32, kind="ExternalInput")
    xout = nc.dram_tensor("xout", [D, TL], F32, kind="ExternalOutput")
    ident_in = nc.dram_tensor("ident_in", [P, P], F32, kind="ExternalInput")
    halfm_in = nc.dram_tensor("halfm_in", [P, 1], F32, kind="ExternalInput")

    w1 = nc.dram_tensor("l0w1", [NE, D, H_FFN], F32, kind="ExternalInput")
    w2 = nc.dram_tensor("l0w2", [NE, H_FFN, D], F32, kind="ExternalInput")
    wr = nc.dram_tensor("l0wr", [D, NE], F32, kind="ExternalInput")
    brd = nc.dram_tensor("l0br", [NE, 1], F32, kind="ExternalInput")
    b1t = nc.dram_tensor("l0b1t", [NE, P, NH], F32, kind="ExternalInput")
    b2d = nc.dram_tensor("l0b2", [NE, D], F32, kind="ExternalInput")

    wdt_t = nc.dram_tensor("wdt_t", [D, D], F32, kind="ExternalInput")
    wb_t = nc.dram_tensor("wb_t", [D, NS], F32, kind="ExternalInput")
    wc_t = nc.dram_tensor("wc_t", [D, NS], F32, kind="ExternalInput")
    bdt_d = nc.dram_tensor("bdt_d", [P, ND], F32, kind="ExternalInput")
    loga_d = nc.dram_tensor("loga_d", [ND, P, NS], F32, kind="ExternalInput")
    dpar_d = nc.dram_tensor("dpar_d", [P, ND], F32, kind="ExternalInput")

    w1bf = nc.dram_tensor("w1bf", [NE, NH, ND, P, P], BF16, kind="Internal")
    w2bf = nc.dram_tensor("w2bf", [NE, ND, NH, P, P], BF16, kind="Internal")
    wdtbf = nc.dram_tensor("wdtbf", [ND, ND, P, P], BF16, kind="Internal")
    wg_d = nc.dram_tensor("wg_d", [I_FFN, D], F32, kind="ExternalInput")
    wu_d = nc.dram_tensor("wu_d", [I_FFN, D], F32, kind="ExternalInput")
    wd_d = nc.dram_tensor("wd_d", [D, I_FFN], F32, kind="ExternalInput")
    wqg = nc.dram_tensor("wqg", [I_FFN, D], BF16, kind="Internal")
    wqu = nc.dram_tensor("wqu", [I_FFN, D], BF16, kind="Internal")
    wqd = nc.dram_tensor("wqd", [D, I_FFN], BF16, kind="Internal")
    upd = nc.dram_tensor("upd", [TL, I_FFN], BF16, kind="Internal")
    gated = nc.dram_tensor("gated", [TL, I_FFN], BF16, kind="Internal")
    g2d = nc.dram_tensor("g2d", [TL, I_FFN], BF16, kind="Internal")
    ord_ = nc.dram_tensor("ord_", [TL, I_FFN], BF16, kind="Internal")
    ar_in = nc.dram_tensor("ar_in", [1, 64], F32, kind="Internal")
    ar_out = nc.dram_tensor("ar_out", [1, 64], F32, kind="Internal")
    yd = nc.dram_tensor("yd", [ND, P, TL], F32, kind="Internal")
    dtd = nc.dram_tensor("dtd", [ND, P, TL], F32, kind="Internal")
    ag_s_in = nc.dram_tensor("ag_s_in", [P, P], F32, kind="Internal")
    ag_s_out = nc.dram_tensor("ag_s_out", [2 * P, P], F32, kind="Internal")

    wq_td = nc.dram_tensor("wq_td", [D, D], F32, kind="ExternalInput")
    wk_td = nc.dram_tensor("wk_td", [D, 2 * P], F32, kind="ExternalInput")
    wv_td = nc.dram_tensor("wv_td", [D, 2 * P], F32, kind="ExternalInput")
    wp_td = nc.dram_tensor("wp_td", [D, D], F32, kind="ExternalInput")
    qg_in = nc.dram_tensor("qg_in", [1, NHEAD], F32, kind="ExternalInput")
    cc_in = nc.dram_tensor("cc_in", [P, SEQ], F32, kind="ExternalInput")
    ss_in = nc.dram_tensor("ss_in", [P, SEQ], F32, kind="ExternalInput")
    msk_in = nc.dram_tensor("msk_in", [P, 4 * 512], BF16, kind="ExternalInput")
    agg_in = nc.dram_tensor("agg_in", [1, TL], F32, kind="Internal")
    agg_out = nc.dram_tensor("agg_out", [2, TL], F32, kind="Internal")
    agx_in = nc.dram_tensor("agx_in", [P, ND, TL], BF16, kind="Internal")
    agx_out = nc.dram_tensor("agx_out", [2, P, ND, TL], BF16, kind="Internal")
    y_hbm = nc.dram_tensor("y_hbm", [4 * P, SEQ], BF16, kind="Internal")
    ary_in = nc.dram_tensor("ary_in", [1, SEQ], F32, kind="Internal")
    ary_out = nc.dram_tensor("ary_out", [1, SEQ], F32, kind="Internal")
    pps_hbm = nc.dram_tensor("pps_hbm", [2, D, TL], F32, kind="Internal")
    rsy_out = nc.dram_tensor("rsy_out", [D, TL], F32, kind="Internal")

    es = ExitStack()
    with es:
        block = es.enter_context(nc.Block())

        class C:
            pass
        c = C()
        c.V = Eng(es.enter_context(nc.semaphore("sV")))
        c.S = Eng(es.enter_context(nc.semaphore("sS")))
        c.T = Eng(es.enter_context(nc.semaphore("sT")))
        c.G = Eng(es.enter_context(nc.semaphore("sG")))
        c.Dg = Eng(es.enter_context(nc.semaphore("sDg")), 16)
        c.Ds = Eng(es.enter_context(nc.semaphore("sDs")), 16)
        c.CC = Eng(es.enter_context(nc.semaphore("sCC")))

        def sbg(name, shape, dtype):
            return es.enter_context(nc.sbuf_tensor(name, shape, dtype))

        def psg(name, shape, dtype=F32):
            return es.enter_context(nc.psum_tensor(name, shape, dtype))

        VS, SS, TS, GS, YS = (block.vector, block.scalar, block.tensor,
                              block.gpsimd, block.sync)

        X = sbg("X", [P, ND, TL], F32)
        rows = sbg("rows", [1, 2 * TL], F32)
        consts = sbg("consts", [P, 4], F32)
        onesf = sbg("onesf", [P, 1], F32)
        identf = sbg("identf", [P, P], F32)
        halfm = sbg("halfm", [P, 1], F32)

        ps_su = psg("ps_su", [1, 512])
        ps_sq = psg("ps_sq", [1, 512])
        ps_misc = psg("ps_misc", [P, 512])
        ps_tr = psg("ps_tr", [P, P])
        ps_a = [psg(f"ps_a{i}", [P, 512]) for i in range(2)]
        ps_b = [psg(f"ps_b{i}", [P, 512]) for i in range(2)]

        st = {}

        def fence():
            toks = [(e.sem, e.n) for e in (c.V, c.S, c.T, c.G, c.Dg, c.Ds)]

            def _(g):
                c.V.wait(g, *toks)
            VS(_)

            def _(g):
                c.S.wait(g, *toks)
            SS(_)

            def _(g):
                c.T.wait(g, *toks)
            TS(_)

            def _(g):
                c.G.wait(g, *toks)
            GS(_)

            def _(g):
                c.Ds.wait(g, *toks)
            YS(_)

        def _(g):
            st["xld"] = c.Ds.tok(g.dma_start(
                out=X[:, :, :],
                in_=bass.AP(xin, 0, [[TL, P], [P * TL, ND], [1, TL]])))
            st["idld"] = c.Ds.tok(g.dma_start(out=identf[:, :], in_=ident_in[:, :]))
            st["hmld"] = c.Ds.tok(g.dma_start(out=halfm[:, :], in_=halfm_in[:, :]))
        YS(_)

        def _(g):
            g.memset(consts[:, 0:1], 1.0)
            g.memset(consts[:, 1:2], LN_EPS)
            st["consts"] = c.V.tok(g.memset(onesf[:, 0:1], 1.0))
        VS(_)

        def _(g):
            for e in range(NE):
                for ht in range(NH):
                    c.Dg.tok(g.dma_start(
                        out=bass.AP(w1bf, (e * NH + ht) * ND * P * P,
                                    [[1, 1], [1, ND * P * P]]),
                        in_=bass.AP(w1, e * D * H_FFN + ht * P,
                                    [[1, 1], [P * H_FFN, ND], [H_FFN, P], [1, P]])))
                for dt in range(ND):
                    t2 = c.Dg.tok(g.dma_start(
                        out=bass.AP(w2bf, (e * ND + dt) * NH * P * P,
                                    [[1, 1], [1, NH * P * P]]),
                        in_=bass.AP(w2, e * H_FFN * D + dt * P,
                                    [[1, 1], [P * D, NH], [D, P], [1, P]])))
                st[f"wcast{e}"] = t2
            if stage >= 2:
                for et in range(ND):
                    t = c.Dg.tok(g.dma_start(
                        out=bass.AP(wdtbf, et * ND * P * P,
                                    [[1, 1], [1, ND * P * P]]),
                        in_=bass.AP(wdt_t, et * P,
                                    [[1, 1], [P * D, ND], [D, P], [1, P]])))
                st["wdtcast"] = t
        GS(_)

        def emit_ln(tag, xdep, ubt, router=None, rstate=None):
            with ExitStack() as ls:
                sq = ls.enter_context(nc.sbuf_tensor(f"{tag}_sq", [P, 2, 512], F32))
                u32c = ls.enter_context(nc.sbuf_tensor(f"{tag}_u32c", [P, 2, 512], F32))
                rstdb = ls.enter_context(nc.sbuf_tensor(f"{tag}_rstdb", [P, 512], F32))
                mrsb = ls.enter_context(nc.sbuf_tensor(f"{tag}_mrsb", [P, 512], F32))
                logit = (ls.enter_context(nc.sbuf_tensor(f"{tag}_lgt", [NE, 2, 512], F32))
                         if router else None)

                def row(i):
                    return rows[0:1, i * TL:(i + 1) * TL]

                for ch in range(NT):
                    sl = slice(ch * 512, (ch + 1) * 512)
                    for dt in range(ND):
                        par = (ch * ND + dt) % 2

                        def _(g, ch=ch, dt=dt, par=par, sl=sl):
                            c.V.wait(g, xdep)
                            c.V.wait(g, st.get(f"{tag}_smm{ch * ND + dt - 2}"))
                            st[f"{tag}_sq_{ch}_{dt}"] = c.V.tok(g.tensor_mul(
                                sq[:, par, :], X[:, dt, sl], X[:, dt, sl]))
                        VS(_)

                        def _(g, ch=ch, dt=dt, par=par, sl=sl):
                            c.T.wait(g, st["consts"], xdep)
                            c.T.wait(g, st.get(f"{tag}_evac{ch - 1}"))
                            c.T.wait(g, st[f"{tag}_sq_{ch}_{dt}"])
                            g.matmul(ps_su[:, :], onesf[:, 0:1], X[:, dt, sl],
                                     start=(dt == 0), stop=(dt == ND - 1))
                            mm2 = g.matmul(ps_sq[:, :], onesf[:, 0:1], sq[:, par, :],
                                           start=(dt == 0), stop=(dt == ND - 1))
                            st[f"{tag}_smm{ch * ND + dt}"] = c.T.tok(mm2)
                        TS(_)

                    def _(g, ch=ch, sl=sl):
                        c.V.wait(g, st[f"{tag}_smm{ch * ND + ND - 1}"])
                        g.tensor_scalar(row(0)[:, sl], ps_su[:, :], 1.0 / D,
                                        None, AL.mult)
                        st[f"{tag}_evac{ch}"] = c.V.tok(g.tensor_scalar(
                            row(1)[:, sl], ps_sq[:, :], 1.0 / D, None, AL.mult))
                    VS(_)

                def _(g):
                    c.V.wait(g, st[f"{tag}_evac{NT - 1}"])
                    for ch2 in range(NT):
                        sl2 = slice(ch2 * 512, (ch2 + 1) * 512)
                        t1 = c.V.tok(g.tensor_mul(ps_misc[0:1, :], row(0)[:, sl2],
                                                  row(0)[:, sl2]))
                        c.V.wait(g, t1)
                        t2 = c.V.tok(g.tensor_sub(row(1)[:, sl2], row(1)[:, sl2],
                                                  ps_misc[0:1, :]))
                    st[f"{tag}_var"] = t2
                VS(_)

                def _(g):
                    c.S.wait(g, st[f"{tag}_var"])
                    t = c.S.tok(g.activation(row(1)[:, :], row(1)[:, :], AF.Ln,
                                             bias=consts[0:1, 1:2]))
                    c.S.wait(g, t)
                    st[f"{tag}_lnv"] = c.S.tok(
                        g.activation(row(1)[:, :], row(1)[:, :], AF.Exp, scale=-0.5))
                SS(_)

                def _(g):
                    c.V.wait(g, st[f"{tag}_lnv"])
                    st[f"{tag}_mrs"] = c.V.tok(
                        g.tensor_mul(row(0)[:, :], row(0)[:, :], row(1)[:, :]))
                VS(_)

                if ubt is None:
                    st[f"{tag}_u"] = st[f"{tag}_mrs"]
                    return

                for ch in range(NT):
                    def _(g, ch=ch):
                        c.G.wait(g, st[f"{tag}_mrs"])
                        c.G.wait(g, st.get(f"{tag}_ubc{ch - 1}"))
                        g.dma_start(out=rstdb[:, :],
                                    in_=bass.AP(rows, TL + ch * 512,
                                                [[2 * TL, 1], [0, P], [1, 512]])
                                    ).then_inc(c.Dg.sem, 16)
                        c.Dg.n += 16
                        st[f"{tag}_bc{ch}"] = c.Dg.tok(g.dma_start(
                            out=mrsb[:, :],
                            in_=bass.AP(rows, ch * 512,
                                        [[2 * TL, 1], [0, P], [1, 512]])))
                    GS(_)

                    for dt in range(ND):
                        def _(g, ch=ch, dt=dt):
                            sl = slice(ch * 512, (ch + 1) * 512)
                            par = (ch * ND + dt) % 2
                            upar = dt % 2
                            c.V.wait(g, st[f"{tag}_bc{ch}"])
                            c.V.wait(g, st.get(f"{tag}_rmm{ch * ND + dt - 2}"))
                            t = c.V.tok(g.tensor_mul(sq[:, par, :], X[:, dt, sl],
                                                     rstdb[:, :]))
                            c.V.wait(g, t)
                            tu = c.V.tok(g.tensor_sub(ubt[:, dt, sl], sq[:, par, :],
                                                      mrsb[:, :]))
                            st[f"{tag}_u_{ch}_{dt}"] = tu
                            if router:
                                st[f"{tag}_u32_{ch}_{dt}"] = c.V.tok(g.tensor_sub(
                                    u32c[:, upar, :], sq[:, par, :], mrsb[:, :]))
                            if dt == ND - 1:
                                st[f"{tag}_ubc{ch}"] = st.get(
                                    f"{tag}_u32_{ch}_{dt}", tu)
                        VS(_)

                        if router:
                            def _(g, ch=ch, dt=dt):
                                upar = dt % 2
                                c.T.wait(g, st[f"{tag}_u32_{ch}_{dt}"],
                                         st[f"{router}_wrld"])
                                if dt == 0:
                                    c.T.wait(g, st.get(f"{router}_lgev{ch - 1}"))
                                mm = g.matmul(ps_misc[0:NE, :],
                                              rstate["wrt"][:, dt * NE:(dt + 1) * NE],
                                              u32c[:, upar, :],
                                              start=(dt == 0), stop=(dt == ND - 1))
                                t = c.T.tok(mm)
                                st[f"{tag}_rmm{ch * ND + dt}"] = t
                                if dt == ND - 1:
                                    st[f"{router}_lg{ch}"] = t
                            TS(_)

                    if router:
                        par2 = ch % 2

                        def _(g, ch=ch, par2=par2):
                            c.S.wait(g, st[f"{router}_brld"], st[f"{router}_lg{ch}"])
                            c.S.wait(g, st.get(f"{router}_trev_{ch - 2}_3"))
                            st[f"{router}_lgev{ch}"] = c.S.tok(g.activation(
                                logit[:, par2, :], ps_misc[0:NE, :], AF.Identity,
                                bias=rstate["brt"][:, 0:1]))
                        SS(_)

                        for j in range(4):
                            tt = ch * 4 + j

                            def _(g, ch=ch, par2=par2, j=j, tt=tt):
                                c.T.wait(g, st["idld"], st[f"{router}_lgev{ch}"])
                                c.T.wait(g, st.get(f"{router}_trev_{ch}_{j - 1}"))
                                c.T.wait(g, st.get(f"{router}_trev_{ch - 1}_3"))
                                st[f"{router}_tr_{ch}_{j}"] = c.T.tok(g.transpose(
                                    ps_tr[:, 0:NE], logit[:, par2, j * P:(j + 1) * P],
                                    identf[0:NE, 0:NE]))
                            TS(_)

                            def _(g, ch=ch, j=j, tt=tt):
                                c.V.wait(g, st[f"{router}_tr_{ch}_{j}"])
                                st[f"{router}_trev_{ch}_{j}"] = c.V.tok(g.tensor_copy(
                                    rstate["ltok"][:, tt * NE:(tt + 1) * NE],
                                    ps_tr[:, 0:NE]))
                            VS(_)

                for dt in range(ND):
                    st[f"{tag}_u{dt}"] = st[f"{tag}_u_{NT - 1}_{dt}"]
                st[f"{tag}_u"] = st[f"{tag}_ubc{NT - 1}"]

        def emit_router_post(tag, rstate):
            NTT = TL // P
            ltok, lmax, lden, maskp_t, maskp = (rstate["ltok"], rstate["lmax"],
                                                rstate["lden"], rstate["maskp_t"],
                                                rstate["maskp"])

            def _(g):
                c.V.wait(g, st[f"{tag}_trev_{NT - 1}_3"])
                t = c.V.tok(g.tensor_reduce(
                    lmax.ap().rearrange("p (a b) -> p a b", b=1),
                    ltok.ap().rearrange("p (a b) -> p a b", b=NE), AX.X, AL.max))
                c.V.wait(g, t)
                st[f"{tag}_nmax"] = c.V.tok(
                    g.tensor_scalar(lmax[:, :], lmax[:, :], -1.0, None, AL.mult))
            VS(_)

            def _(g):
                c.S.wait(g, st[f"{tag}_nmax"])
                last = None
                for tt in range(NTT):
                    last = g.activation(
                        maskp_t[:, tt * NE:(tt + 1) * NE],
                        ltok[:, tt * NE:(tt + 1) * NE], AF.Exp,
                        bias=lmax[:, tt:tt + 1], accum_out=lden[:, tt:tt + 1])
                st[f"{tag}_exp"] = c.S.tok(last)
            SS(_)

            def _(g):
                c.V.wait(g, st[f"{tag}_exp"])
                t = c.V.tok(g.reciprocal(lden[:, :], lden[:, :]))
                c.V.wait(g, t)
                for tt in range(NTT):
                    t = c.V.tok(g.tensor_scalar(
                        maskp_t[:, tt * NE:(tt + 1) * NE],
                        ltok[:, tt * NE:(tt + 1) * NE],
                        lmax[:, tt:tt + 1], 0.0, AL.add, AL.is_ge))
                    c.V.wait(g, t)
                    st[f"{tag}_mp{tt}"] = c.V.tok(g.tensor_scalar(
                        maskp_t[:, tt * NE:(tt + 1) * NE],
                        maskp_t[:, tt * NE:(tt + 1) * NE],
                        lden[:, tt:tt + 1], None, AL.mult))
            VS(_)

            for tt in range(NTT):
                def _(g, tt=tt):
                    c.T.wait(g, st[f"{tag}_mp{tt}"])
                    c.T.wait(g, st.get(f"{tag}_mtrev{tt - 1}"))
                    st[f"{tag}_mtr{tt}"] = c.T.tok(g.transpose(
                        ps_tr[0:NE, 0:P], maskp_t[:, tt * NE:(tt + 1) * NE],
                        identf[:, :]))
                TS(_)

                def _(g, tt=tt):
                    c.V.wait(g, st[f"{tag}_mtr{tt}"])
                    st[f"{tag}_mtrev{tt}"] = c.V.tok(g.tensor_copy(
                        maskp[:, tt * P:(tt + 1) * P], ps_tr[0:NE, 0:P]))
                VS(_)
            st[f"{tag}_maskp"] = st[f"{tag}_mtrev{NTT - 1}"]

        def emit_l0_ffn_sublayer(tag, xdep):
            with ExitStack() as ls:
                def sb(n, shape, dtype):
                    return ls.enter_context(nc.sbuf_tensor(f"{tag}_{n}", shape, dtype))

                ubt = sb("ub", [P, ND, TL], BF16)
                wrt = sb("wrt", [P, ND * NE], F32)
                brt = sb("brt", [NE, 1], F32)
                ltok = sb("ltok", [P, TL // P * NE], F32)
                lmax = sb("lmax", [P, TL // P], F32)
                lden = sb("lden", [P, TL // P], F32)
                maskp_t = sb("maskp_t", [P, TL // P * NE], F32)
                maskp = sb("maskp", [NE, TL], BF16)
                mask_bc = sb("mask_bc", [P, TL], BF16)
                b1all = sb("b1all", [P, NE * NH], F32)
                b2sb = sb("b2sb", [NE, D], BF16)
                w1s = [sb(f"w1s{i}", [P, D], BF16) for i in range(3)]
                w2s = [sb(f"w2s{i}", [P, H_FFN], BF16) for i in range(2)]
                hbuf = [sb(f"hbuf{i}", [P, 512], BF16) for i in range(3)]
                hmask = sb("hmask", [P, NH, 512], BF16)
                yebuf = [sb(f"yebuf{i}", [P, 512], F32) for i in range(2)]
                ybk = [sb(f"ybk{i}", [P, 512], F32) for i in range(2)]
                rstate = {"wrt": wrt, "brt": brt, "ltok": ltok, "lmax": lmax,
                          "lden": lden, "maskp_t": maskp_t, "maskp": maskp}
                pw1 = [ps_a[0], ps_a[1], ps_misc]

                def _(g):
                    st[f"{tag}_wrld"] = c.Ds.tok(g.dma_start(
                        out=bass.AP(wrt, 0, [[ND * NE, P], [NE, ND], [1, NE]]),
                        in_=bass.AP(wr, 0, [[NE, P], [P * NE, ND], [1, NE]])))
                    st[f"{tag}_brld"] = c.Ds.tok(g.dma_start(out=brt[:, :],
                                                             in_=brd[:, :]))
                    st[f"{tag}_b1ld"] = c.Ds.tok(g.dma_start(
                        out=bass.AP(b1all, 0, [[NE * NH, P], [NH, NE], [1, NH]]),
                        in_=bass.AP(b1t, 0, [[NH, P], [P * NH, NE], [1, NH]])))
                YS(_)

                def _(g):
                    st[f"{tag}_b2ld"] = c.Dg.tok(g.dma_start(out=b2sb[:, :],
                                                             in_=b2d[:, :]))
                GS(_)

                emit_ln(tag, xdep, ubt, router=tag, rstate=rstate)
                emit_router_post(tag, rstate)

                for e in range(NE):
                    def _(g, e=e):
                        c.G.wait(g, st[f"{tag}_maskp"])
                        c.G.wait(g, st.get(f"{tag}_mfree{e - 1}"))
                        st[f"{tag}_mbc{e}"] = c.Dg.tok(g.dma_start(
                            out=mask_bc[:, :],
                            in_=bass.AP(maskp, e * TL, [[TL, 1], [0, P], [1, TL]])))
                    GS(_)

                    for ch in range(NT):
                        sl = slice(ch * 512, (ch + 1) * 512)
                        for ht in range(NH):
                            ib = (ch * NH + ht) % 3

                            def _(g, e=e, ch=ch, ht=ht, ib=ib):
                                c.Ds.wait(g, st[f"wcast{e}"])
                                c.Ds.wait(g, st.get(f"{tag}_w1fr_{e}_{ch}_{ht - 3}"))
                                st[f"{tag}_w1ld_{e}_{ch}_{ht}"] = c.Ds.tok(g.dma_start(
                                    out=bass.AP(w1s[ib], 0, [[D, P], [P, ND], [1, P]]),
                                    in_=bass.AP(w1bf, (e * NH + ht) * ND * P * P,
                                                [[P, P], [P * P, ND], [1, P]])))
                            YS(_)

                            def _(g, e=e, ch=ch, ht=ht, ib=ib, sl=sl):
                                c.T.wait(g, st[f"{tag}_w1ld_{e}_{ch}_{ht}"])
                                if e == 0 and ch == 0 and ht == 0:
                                    c.T.wait(g, st[f"{tag}_u"],
                                             st[f"{tag}_maskp"])
                                c.T.wait(g, st.get(f"{tag}_hev_{e}_{ch}_{ht - 3}"))
                                for kt in range(ND):
                                    mm = g.matmul(pw1[ib][:, :],
                                                  w1s[ib][:, kt * P:(kt + 1) * P],
                                                  ubt[:, kt, sl], start=(kt == 0),
                                                  stop=(kt == ND - 1))
                                t = c.T.tok(mm)
                                st[f"{tag}_hmm_{e}_{ch}_{ht}"] = t
                                st[f"{tag}_w1fr_{e}_{ch}_{ht}"] = t
                            TS(_)

                            def _(g, e=e, ch=ch, ht=ht, ib=ib, sl=sl):
                                c.V.wait(g, st[f"{tag}_b1ld"],
                                         st[f"{tag}_hmm_{e}_{ch}_{ht}"],
                                         st[f"{tag}_mbc{e}"])
                                t = c.V.tok(g.tensor_scalar(
                                    hbuf[ib][:, :], pw1[ib][:, :],
                                    b1all[:, e * NH + ht:e * NH + ht + 1], 0.0,
                                    AL.add, AL.max))
                                st[f"{tag}_hev_{e}_{ch}_{ht}"] = t
                                c.V.wait(g, t)
                                st[f"{tag}_hmsk_{e}_{ch}_{ht}"] = c.V.tok(g.tensor_mul(
                                    hmask[:, ht, :], hbuf[ib][:, :], mask_bc[:, sl]))
                            VS(_)

                        for dt in range(ND):
                            ib = (ch * ND + dt) % 2

                            def _(g, e=e, ch=ch, dt=dt, ib=ib):
                                c.Ds.wait(g, st[f"wcast{e}"])
                                c.Ds.wait(g, st.get(f"{tag}_w2fr_{e}_{ch}_{dt - 2}"))
                                st[f"{tag}_w2ld_{e}_{ch}_{dt}"] = c.Ds.tok(g.dma_start(
                                    out=bass.AP(w2s[ib], 0,
                                                [[H_FFN, P], [P, NH], [1, P]]),
                                    in_=bass.AP(w2bf, (e * ND + dt) * NH * P * P,
                                                [[P, P], [P * P, NH], [1, P]])))
                            YS(_)

                            def _(g, e=e, ch=ch, dt=dt, ib=ib):
                                c.T.wait(g, st[f"{tag}_w2ld_{e}_{ch}_{dt}"])
                                c.T.wait(g, st[f"{tag}_hmsk_{e}_{ch}_{NH - 1}"])
                                c.T.wait(g, st.get(f"{tag}_yev_{e}_{ch}_{dt - 2}"))
                                for kt in range(NH):
                                    mm = g.matmul(ps_b[ib][:, :],
                                                  w2s[ib][:, kt * P:(kt + 1) * P],
                                                  hmask[:, kt, :], start=(kt == 0),
                                                  stop=(kt == NH - 1))
                                t = c.T.tok(mm)
                                st[f"{tag}_ymm_{e}_{ch}_{dt}"] = t
                                st[f"{tag}_w2fr_{e}_{ch}_{dt}"] = t
                            TS(_)

                            def _(g, e=e, ch=ch, dt=dt, ib=ib):
                                c.V.wait(g, st[f"{tag}_ymm_{e}_{ch}_{dt}"])
                                c.V.wait(g, st.get(f"{tag}_yacc_{e}_{ch}_{dt - 2}"))
                                st[f"{tag}_yev_{e}_{ch}_{dt}"] = c.V.tok(g.tensor_copy(
                                    yebuf[ib][:, :], ps_b[ib][:, :]))
                            VS(_)

                            def _(g, e=e, ch=ch, dt=dt, ib=ib):
                                c.G.wait(g, st[f"{tag}_yev_{e}_{ch}_{dt}"])
                                c.G.wait(g, st.get(f"{tag}_yacc_{e - 1}_{ch}_{dt}"))
                                kw = {} if e == 0 else {"accum_op": AL.add}
                                st[f"{tag}_yacc_{e}_{ch}_{dt}"] = c.Dg.tok(g.dma_start(
                                    out=bass.AP(yd, dt * P * TL + ch * 512,
                                                [[TL, P], [1, 512]]),
                                    in_=yebuf[ib][:, :], **kw))
                            GS(_)

                    st[f"{tag}_mfree{e}"] = st[f"{tag}_hmsk_{e}_{NT - 1}_{NH - 1}"]

                for dt in range(ND):
                    for ch in range(NT):
                        sl = slice(ch * 512, (ch + 1) * 512)
                        kb = (dt * NT + ch) % 2

                        def _(g, dt=dt, ch=ch, kb=kb):
                            c.Ds.wait(g, st[f"{tag}_yacc_{NE - 1}_{ch}_{dt}"])
                            gi = dt * NT + ch - 2
                            c.Ds.wait(g, st.get(f"{tag}_xup_{gi // NT}_{gi % NT}"))
                            st[f"{tag}_ybk_{dt}_{ch}"] = c.Ds.tok(g.dma_start(
                                out=ybk[kb][:, :],
                                in_=bass.AP(yd, dt * P * TL + ch * 512,
                                            [[TL, P], [1, 512]])))
                        YS(_)

                        def _(g, dt=dt, ch=ch):
                            c.T.wait(g, st[f"{tag}_b2ld"], st[f"{tag}_maskp"])
                            c.T.wait(g, st.get(f"{tag}_xup_{dt}_{ch - 1}"))
                            c.T.wait(g, st.get(f"{tag}_xup_{dt - 1}_{NT - 1}"))
                            st[f"{tag}_b2mm_{dt}_{ch}"] = c.T.tok(g.matmul(
                                ps_misc[:, :], b2sb[:, dt * P:(dt + 1) * P],
                                maskp[:, ch * 512:(ch + 1) * 512],
                                start=True, stop=True))
                        TS(_)

                        def _(g, dt=dt, ch=ch, sl=sl, kb=kb):
                            c.V.wait(g, st[f"{tag}_b2mm_{dt}_{ch}"])
                            c.V.wait(g, st[f"{tag}_ybk_{dt}_{ch}"])
                            t = c.V.tok(g.scalar_tensor_tensor(
                                X[:, dt, sl], ps_misc[:, :], 0.5, X[:, dt, sl],
                                AL.mult, AL.add))
                            c.V.wait(g, t)
                            st[f"{tag}_xup_{dt}_{ch}"] = c.V.tok(g.scalar_tensor_tensor(
                                X[:, dt, sl], ybk[kb][:, :], 0.5, X[:, dt, sl],
                                AL.mult, AL.add))
                        VS(_)
                st[f"{tag}_done"] = st[f"{tag}_xup_{ND - 1}_{NT - 1}"]
            fence()
            return st[f"{tag}_done"]

        def emit_mamba(tag, xdep):
            with ExitStack() as ls:
                def sb(n, shape, dtype):
                    return ls.enter_context(nc.sbuf_tensor(f"{tag}_{n}", shape, dtype))


                Brow = sb("Brow", [NS, TL], BF16)
                Crow = sb("Crow", [NS, TL], BF16)
                rstdbF = sb("rstdbF", [P, TL], F32)
                mrsbF = sb("mrsbF", [P, TL], F32)
                Acol = sb("Acol", [P, ND * NS], F32)
                bdtc = sb("bdtc", [P, ND], F32)
                Dcol = sb("Dcol", [P, ND], F32)
                smat = sb("smat", [P, P], F32)
                sini = sb("sini", [P, P], F32)
                wbct = sb("wbct", [P, ND * 2 * NS], BF16)

                def _(g):
                    st[f"{tag}_cl1"] = c.Ds.tok(g.dma_start(
                        out=bass.AP(Acol, 0, [[ND * NS, P], [NS, ND], [1, NS]]),
                        in_=bass.AP(loga_d, 0, [[NS, P], [P * NS, ND], [1, NS]])))
                    st[f"{tag}_cl2"] = c.Ds.tok(g.dma_start(out=bdtc[:, :],
                                                            in_=bdt_d[:, :]))
                    st[f"{tag}_cl3"] = c.Ds.tok(g.dma_start(out=Dcol[:, :],
                                                            in_=dpar_d[:, :]))
                YS(_)

                def _(g):
                    t = c.Dg.tok(g.dma_start(
                        out=bass.AP(wbct, 0, [[ND * 2 * NS, P], [2 * NS, ND], [1, NS]]),
                        in_=bass.AP(wb_t, 0, [[NS, P], [P * NS, ND], [1, NS]])))
                    st[f"{tag}_wbld"] = t
                    t = c.Dg.tok(g.dma_start(
                        out=bass.AP(wbct, NS, [[ND * 2 * NS, P], [2 * NS, ND], [1, NS]]),
                        in_=bass.AP(wc_t, 0, [[NS, P], [P * NS, ND], [1, NS]])))
                    st[f"{tag}_wcld"] = t
                GS(_)

                def _(g):
                    c.S.wait(g, st[f"{tag}_cl1"])
                    st[f"{tag}_Ae"] = c.S.tok(g.activation(Acol[:, :], Acol[:, :],
                                                           AF.Exp))
                SS(_)

                def _(g):
                    c.V.wait(g, st[f"{tag}_Ae"])
                    st[f"{tag}_Aneg"] = c.V.tok(g.tensor_scalar(
                        Acol[:, :], Acol[:, :], -1.0, None, AL.mult))
                VS(_)

                with ExitStack() as ls2:
                    ubt = ls2.enter_context(
                        nc.sbuf_tensor(f"{tag}_ub", [P, ND, TL], BF16))
                    wdts = [ls2.enter_context(
                        nc.sbuf_tensor(f"{tag}_wdts{i}", [P, D], BF16))
                        for i in range(2)]
                    spt = [ls2.enter_context(
                        nc.sbuf_tensor(f"{tag}_spt{i}", [P, 512], F32))
                        for i in range(2)]
                    dt32 = ls2.enter_context(
                        nc.sbuf_tensor(f"{tag}_dt32", [P, 2, 512], F32))

                    emit_ln(tag, xdep, ubt)

                    def _(g):
                        c.G.wait(g, st[f"{tag}_mrs"])
                        g.dma_start(out=rstdbF[:, :],
                                    in_=bass.AP(rows, TL,
                                                [[2 * TL, 1], [0, P], [1, TL]])
                                    ).then_inc(c.Dg.sem, 16)
                        c.Dg.n += 16
                        st[f"{tag}_bcF"] = c.Dg.tok(g.dma_start(
                            out=mrsbF[:, :],
                            in_=bass.AP(rows, 0, [[2 * TL, 1], [0, P], [1, TL]])))
                    GS(_)

                    for et in range(ND):
                        ib = et % 2

                        def _(g, et=et, ib=ib):
                            c.Ds.wait(g, st["wdtcast"])
                            c.Ds.wait(g, st.get(f"{tag}_dtfr{et - 2}"))
                            st[f"{tag}_wdtld{et}"] = c.Ds.tok(g.dma_start(
                                out=bass.AP(wdts[ib], 0, [[D, P], [P, ND], [1, P]]),
                                in_=bass.AP(wdtbf, et * ND * P * P,
                                            [[P, P], [P * P, ND], [1, P]])))
                        YS(_)

                        for ch in range(NT):
                            pp = (et * NT + ch) % 2

                            def _(g, et=et, ch=ch, ib=ib, pp=pp):
                                sl = slice(ch * 512, (ch + 1) * 512)
                                c.T.wait(g, st[f"{tag}_wdtld{et}"], st[f"{tag}_u"])
                                c.T.wait(g, st.get(f"{tag}_spdone{et * NT + ch - 2}"))
                                for kt in range(ND):
                                    mm = g.matmul(ps_a[pp][:, :],
                                                  wdts[ib][:, kt * P:(kt + 1) * P],
                                                  ubt[:, kt, sl], start=(kt == 0),
                                                  stop=(kt == ND - 1))
                                t = c.T.tok(mm)
                                st[f"{tag}_dtmm_{et}_{ch}"] = t
                                if ch == NT - 1:
                                    st[f"{tag}_dtfr{et}"] = t
                            TS(_)

                            def _(g, et=et, ch=ch, pp=pp):
                                bia = bdtc[:, et:et + 1]
                                c.S.wait(g, st[f"{tag}_cl2"],
                                         st[f"{tag}_dtmm_{et}_{ch}"])
                                c.S.wait(g, st.get(f"{tag}_spv{et * NT + ch - 1}"))
                                t = c.S.tok(g.activation(spt[0][:, :], ps_a[pp][:, :],
                                                         AF.Abs, bias=bia))
                                t2 = c.S.tok(g.activation(spt[1][:, :], ps_a[pp][:, :],
                                                          AF.Relu, bias=bia))
                                c.S.wait(g, t, t2)
                                t = c.S.tok(g.activation(spt[0][:, :], spt[0][:, :],
                                                         AF.Exp, scale=-1.0))
                                c.S.wait(g, t)
                                t = c.S.tok(g.activation(spt[0][:, :], spt[0][:, :],
                                                         AF.Ln, bias=consts[:, 0:1]))
                                st[f"{tag}_spln_{et}_{ch}"] = t
                                st[f"{tag}_spdone{et * NT + ch}"] = t
                            SS(_)

                            def _(g, et=et, ch=ch):
                                kq = (et * NT + ch) % 2
                                c.V.wait(g, st[f"{tag}_spln_{et}_{ch}"])
                                c.V.wait(g, st.get(f"{tag}_dtw{et * NT + ch - 2}"))
                                st[f"{tag}_spv{et * NT + ch}"] = c.V.tok(g.tensor_add(
                                    dt32[:, kq, :], spt[0][:, :], spt[1][:, :]))
                            VS(_)

                            def _(g, et=et, ch=ch):
                                kq = (et * NT + ch) % 2
                                c.G.wait(g, st[f"{tag}_spv{et * NT + ch}"])
                                st[f"{tag}_dtw{et * NT + ch}"] = c.Dg.tok(g.dma_start(
                                    out=bass.AP(dtd, et * P * TL + ch * 512,
                                                [[TL, P], [1, 512]]),
                                    in_=dt32[:, kq, :]))
                            GS(_)

                    for ch in range(NT):
                        sl = slice(ch * 512, (ch + 1) * 512)

                        def _(g, ch=ch, sl=sl):
                            c.T.wait(g, st[f"{tag}_wbld"], st[f"{tag}_wcld"],
                                     st[f"{tag}_u"])
                            c.T.wait(g, st.get(f"{tag}_bcev{ch - 1}"))
                            for kt in range(ND):
                                g.matmul(ps_b[0][0:NS, :],
                                         wbct[:, (kt * 2) * NS:(kt * 2 + 1) * NS],
                                         ubt[:, kt, sl], start=(kt == 0),
                                         stop=(kt == ND - 1))
                                mm = g.matmul(
                                    ps_b[1][0:NS, :],
                                    wbct[:, (kt * 2 + 1) * NS:(kt * 2 + 2) * NS],
                                    ubt[:, kt, sl], start=(kt == 0),
                                    stop=(kt == ND - 1))
                            st[f"{tag}_bcmm{ch}"] = c.T.tok(mm)
                        TS(_)

                        def _(g, ch=ch, sl=sl):
                            c.V.wait(g, st[f"{tag}_bcmm{ch}"])
                            g.tensor_copy(Brow[:, sl], ps_b[0][0:NS, :])
                            st[f"{tag}_bcev{ch}"] = c.V.tok(g.tensor_copy(
                                Crow[:, sl], ps_b[1][0:NS, :]))
                        VS(_)


                fence()

                with ExitStack() as ls3:
                    def sb3(n, shape, dtype):
                        return ls3.enter_context(
                            nc.sbuf_tensor(f"{tag}_{n}", shape, dtype))

                    Ball = sb3("Ball", [P, NS, TH], BF16)
                    dtch = [sb3(f"dtch{i}", [P, TH], F32) for i in range(2)]
                    Call = sb3("Call", [P, NS, TH], BF16)
                    u32d = sb3("u32d", [P, TH], F32)
                    dtub = sb3("dtub", [P, TH], F32)
                    dAb = sb3("dAb", [P, 2, TH], F32)
                    dBu = sb3("dBu", [P, 2, TH], BF16)
                    xsc = sb3("xsc", [P, 2, TH], F32)
                    ya = sb3("ya", [P, TH], F32)
                    ym = sb3("ym", [P, 2, TH], BF16)

                    def scan_pass(ph):
                        for th in range(NTH):
                            tsl = slice(th * TH, (th + 1) * TH)

                            def _(g, ph=ph, th=th):
                                c.G.wait(g, st[f"{tag}_bcev{NT - 1}"])
                                c.G.wait(g, st.get(f"{tag}_p{ph}bfree_{th - 1}"))
                                if ph == 3:
                                    c.G.wait(g, st.get(f"{tag}_p1bfree_{NTH - 1}"))
                                for n in range(NS):
                                    g.dma_start(
                                        out=Ball[:, n, :],
                                        in_=bass.AP(Brow, n * TL + th * TH,
                                                    [[TL, 1], [0, P], [1, TH]])
                                    ).then_inc(c.Dg.sem, 16)
                                    c.Dg.n += 16
                                    if ph == 3:
                                        g.dma_start(
                                            out=Call[:, n, :],
                                            in_=bass.AP(Crow, n * TL + th * TH,
                                                        [[TL, 1], [0, P], [1, TH]])
                                        ).then_inc(c.Dg.sem, 16)
                                        c.Dg.n += 16
                                st[f"{tag}_p{ph}bc_{th}"] = (c.Dg.sem, c.Dg.n)
                            GS(_)

                            for et in range(ND):
                                def _(g, ph=ph, th=th, et=et, tsl=tsl):
                                    de = (th * ND + et) % 2
                                    c.Ds.wait(g, st[f"{tag}_dtw{et * NT + NT - 1}"])
                                    c.Ds.wait(g, st.get(f"{tag}_p{ph}ufree_g{(th * ND + et) - 2}"))
                                    st[f"{tag}_p{ph}dtld_{th}_{et}"] = c.Ds.tok(
                                        g.dma_start(out=dtch[de][:, :],
                                                    in_=bass.AP(dtd,
                                                                et * P * TL + th * TH,
                                                                [[TL, P], [1, TH]])))
                                YS(_)

                                def _(g, ph=ph, th=th, et=et, tsl=tsl):
                                    de = (th * ND + et) % 2
                                    c.V.wait(g, st[f"{tag}_bcF"])
                                    c.V.wait(g, st[f"{tag}_p{ph}dtld_{th}_{et}"])
                                    c.V.wait(g, st.get(f"{tag}_p{ph}ufree_g{(th * ND + et) - 1}"))
                                    t = c.V.tok(g.tensor_mul(u32d[:, :],
                                                             X[:, et, tsl],
                                                             rstdbF[:, tsl]))
                                    c.V.wait(g, t)
                                    t = c.V.tok(g.tensor_sub(u32d[:, :], u32d[:, :],
                                                             mrsbF[:, tsl]))
                                    c.V.wait(g, t)
                                    st[f"{tag}_p{ph}dtu_{th}_{et}"] = c.V.tok(
                                        g.tensor_mul(dtub[:, :], dtch[de][:, :],
                                                     u32d[:, :]))
                                VS(_)

                                vstep = 3 if ph == 1 else 4
                                v_base = c.V.n

                                def _(g, ph=ph, th=th, et=et, tsl=tsl,
                                      vstep=vstep, v_base=v_base):
                                    de = (th * ND + et) % 2
                                    c.S.wait(g, st[f"{tag}_Aneg"])
                                    c.S.wait(g, st[f"{tag}_p{ph}dtld_{th}_{et}"])
                                    for n in range(NS):
                                        k = et * NS + n
                                        gk = th * ND * NS + k
                                        pr = k % 2
                                        if n < 2:
                                            c.S.wait(g, st.get(
                                                f"{tag}_p{ph}scn_g{gk - 2}"))
                                        else:
                                            c.S.wait(g, (c.V.sem,
                                                         v_base + vstep * (n - 2)
                                                         + 3))
                                        st[f"{tag}_p{ph}dA_g{gk}"] = c.S.tok(
                                            g.activation(
                                                dAb[:, pr, :], dtch[de][:, :],
                                                AF.Exp,
                                                scale=Acol[:, k:k + 1]))
                                SS(_)

                                g_base = c.G.n

                                def _(g, ph=ph, th=th, et=et, tsl=tsl,
                                      g_base=g_base):
                                    c.V.wait(g, st[f"{tag}_p{ph}dtu_{th}_{et}"])
                                    c.V.wait(g, st[f"{tag}_p{ph}bc_{th}"])
                                    c.V.wait(g, st.get(
                                        f"{tag}_p{ph}scn_g{th * ND * NS + et * NS - 2}"))
                                    if ph == 3 and th == 0:
                                        c.V.wait(g, st[f"{tag}_sini"])
                                    for n in range(NS):
                                        k = et * NS + n
                                        gk = th * ND * NS + k
                                        pr = k % 2
                                        c.V.wait(g, st[f"{tag}_p{ph}dA_g{gk}"])
                                        if ph == 3:
                                            if n < 2:
                                                c.V.wait(g, st.get(
                                                    f"{tag}_p3ya_g{gk - 2}"))
                                            else:
                                                c.V.wait(g, (c.G.sem,
                                                             g_base + n - 1))
                                        t = c.V.tok(g.tensor_mul(
                                            dBu[:, pr, :], dtub[:, :],
                                            Ball[:, n, :]))
                                        c.V.wait(g, t)
                                        if ph == 1:
                                            init = (0.0 if th == 0
                                                    else smat[:, k:k + 1])
                                        else:
                                            init = (sini if th == 0
                                                    else smat)[:, k:k + 1]
                                        t = c.V.tok(g.tensor_tensor_scan(
                                            xsc[:, pr, :], dAb[:, pr, :],
                                            dBu[:, pr, :], init, AL.mult,
                                            AL.add))
                                        c.V.wait(g, t)
                                        t = c.V.tok(g.tensor_copy(
                                            smat[:, k:k + 1],
                                            xsc[:, pr, TH - 1:TH]))
                                        st[f"{tag}_p{ph}scn_g{gk}"] = t
                                        if ph == 3:
                                            st[f"{tag}_p3ym_g{gk}"] = c.V.tok(
                                                g.tensor_mul(ym[:, pr, :],
                                                             xsc[:, pr, :],
                                                             Call[:, n, :]))
                                VS(_)

                                if ph == 3:
                                    def _(g, th=th, et=et):
                                        c.G.wait(g, st.get(
                                            f"{tag}_p3x_g{th * ND + et - 1}"))
                                        for n in range(NS):
                                            k = et * NS + n
                                            gk = th * ND * NS + k
                                            pr = k % 2
                                            c.G.wait(g,
                                                     st[f"{tag}_p3ym_g{gk}"])
                                            if n == 0:
                                                t = c.G.tok(g.tensor_copy(
                                                    ya[:, :], ym[:, pr, :]))
                                            else:
                                                t = c.G.tok(g.tensor_add(
                                                    ya[:, :], ya[:, :],
                                                    ym[:, pr, :]))
                                            c.G.wait(g, t)
                                            st[f"{tag}_p3ya_g{gk}"] = t
                                    GS(_)

                                if ph == 3:
                                    def _(g, th=th, et=et, tsl=tsl):
                                        c.V.wait(g,
                                                 st[f"{tag}_p3ya_g{th * ND * NS + et * NS + NS - 1}"])
                                        t = c.V.tok(g.tensor_add(
                                            X[:, et, tsl], X[:, et, tsl], ya[:, :]))
                                        c.V.wait(g, t, st[f"{tag}_cl3"])
                                        t = c.V.tok(g.scalar_tensor_tensor(
                                            X[:, et, tsl], u32d[:, :],
                                            Dcol[:, et:et + 1], X[:, et, tsl],
                                            AL.mult, AL.add))
                                        st[f"{tag}_p3x_g{th * ND + et}"] = t
                                        st[f"{tag}_p3ufree_g{th * ND + et}"] = t
                                    VS(_)
                                else:
                                    st[f"{tag}_p1ufree_g{th * ND + et}"] = \
                                        st[f"{tag}_p1scn_g{th * ND * NS + et * NS + NS - 1}"]

                            st[f"{tag}_p{ph}bfree_{th}"] = \
                                st[f"{tag}_p{ph}scn_g{th * ND * NS + ND * NS - 1}"]

                    scan_pass(1)

                    def _(g):
                        c.G.wait(g, st[f"{tag}_p1scn_g{NTH * ND * NS - 1}"])
                        g.dma_start(out=ag_s_in[:, :], in_=smat[:, :]).then_inc(
                            c.Dg.sem, 16)
                        c.Dg.n += 16
                        c.G.wait(g, (c.Dg.sem, c.Dg.n))
                        cc = g.collective_compute(
                            "AllGather", AL.bypass,
                            replica_groups=[[0, 1], [2, 3], [4, 5], [6, 7]],
                            ins=[ag_s_in.ap().opt()], outs=[ag_s_out.ap().opt()])
                        st[f"{tag}_ag"] = c.CC.tok(cc)
                        c.G.wait(g, st[f"{tag}_ag"])
                        st[f"{tag}_agld"] = c.Dg.tok(g.dma_start(
                            out=sini[:, :], in_=ag_s_out[0:P, :]))
                    GS(_)

                    def _(g):
                        c.V.wait(g, st[f"{tag}_agld"], st["hmld"])
                        if DBG_ZERO_SINI:
                            st[f"{tag}_sini"] = c.V.tok(g.memset(sini[:, :], 0.0))
                        else:
                            st[f"{tag}_sini"] = c.V.tok(g.tensor_scalar(
                                sini[:, :], sini[:, :], halfm[:, 0:1], None, AL.mult))
                    VS(_)

                    scan_pass(3)

                st[f"{tag}_done"] = st[f"{tag}_p3x_g{NTH * ND - 1}"]
            fence()
            return st[f"{tag}_done"]


        NI = I_FFN // P
        NOC = I_FFN // 512
        NTT = TL // P

        def emit_l1_ffn(tag, xdep, first_call):
            with ExitStack() as ls:
                def sb(n, shape, dtype):
                    return ls.enter_context(nc.sbuf_tensor(f"{tag}_{n}", shape, dtype))

                arsb = sb("arsb", [1, 64], F32)
                arc = sb("arc", [P, 8], F32)
                colt = sb("colt", [P, 4], F32)
                amx = sb("amx", [P, 4], F32)

                def fold_max_col(src, col, dep, key):
                    t = dep
                    for k in [64, 32, 16, 8, 4, 2, 1]:
                        def _(g, k=k, t=t, src=src, col=col, key=key):
                            c.G.wait(g, t)
                            st[f"{tag}_fg_{key}{k}"] = c.Dg.tok(g.dma_start(
                                out=colt[0:k, 0:1], in_=src[k:2 * k, col:col + 1]))
                        GS(_)

                        def _(g, k=k, src=src, col=col, key=key):
                            c.V.wait(g, st[f"{tag}_fg_{key}{k}"])
                            st[f"{tag}_fv_{key}{k}"] = c.V.tok(g.tensor_max(
                                src[0:k, col:col + 1], src[0:k, col:col + 1],
                                colt[0:k, 0:1]))
                        VS(_)
                        t = st[f"{tag}_fv_{key}{k}"]
                    return t

                def allreduce_lanes(key, nlanes, dep):
                    def _(g, key=key, dep=dep, nlanes=nlanes):
                        c.G.wait(g, dep)
                        g.dma_start(out=ar_in.ap(), in_=arsb[0:1, :]).then_inc(
                            c.Dg.sem, 16)
                        c.Dg.n += 16
                        c.G.wait(g, (c.Dg.sem, c.Dg.n))
                        cc = g.collective_compute(
                            "AllReduce", AL.max,
                            replica_groups=[list(range(NCORES))],
                            ins=[ar_in.ap().opt()], outs=[ar_out.ap().opt()])
                        st[f"{tag}_cc_{key}"] = c.CC.tok(cc)
                        c.G.wait(g, st[f"{tag}_cc_{key}"])
                        last = None
                        for i in range(nlanes):
                            last = g.dma_start(
                                out=colt[:, i:i + 1],
                                in_=bass.AP(ar_out, i, [[64, 1], [0, P], [1, 1]]))
                            last.then_inc(c.Dg.sem, 16)
                            c.Dg.n += 16
                        st[f"{tag}_arl_{key}"] = (c.Dg.sem, c.Dg.n)
                    GS(_)
                    return st[f"{tag}_arl_{key}"]

                if first_call:
                    with ExitStack() as dq:
                        wfp = dq.enter_context(
                            nc.sbuf_tensor(f"{tag}_wfp", [P, I_FFN], F32))
                        wrn = dq.enter_context(
                            nc.sbuf_tensor(f"{tag}_wrn", [P, I_FFN], F32))
                        bm = dq.enter_context(
                            nc.sbuf_tensor(f"{tag}_bm", [P, 2, I_FFN // 32], F32))
                        exps = dq.enter_context(
                            nc.sbuf_tensor(f"{tag}_exps", [P, 30 * (I_FFN // 32)], F32))
                        exps2 = dq.enter_context(
                            nc.sbuf_tensor(f"{tag}_exps2", [P, 2, I_FFN], F32))
                        wqb = dq.enter_context(
                            nc.sbuf_tensor(f"{tag}_wqb", [P, I_FFN], BF16))

                        def expand(dsti, srccol, nb, dep, key):
                            EW = 30 * (I_FFN // 32)
                            plan = []
                            cur = ("bm", srccol * (I_FFN // 32), 1, 2 * (I_FFN // 32))
                            offs = [0, 2 * nb, 6 * nb, 14 * nb]
                            t = dep
                            for soff in offs:
                                run = cur[2]
                                for half in range(2):
                                    def _(g, soff=soff, half=half, cur=cur, run=run,
                                          nb=nb, t=t, key=key):
                                        srcn, srcoff, srcrun, srcw = cur
                                        srct = bm if srcn == "bm" else exps
                                        c.V.wait(g, t)
                                        st[f"{tag}_xp_{key}"] = c.V.tok(g.tensor_copy(
                                            bass.AP(exps, soff + half * run,
                                                    [[EW, P], [2 * run, nb], [1, run]]),
                                            bass.AP(srct, srcoff,
                                                    [[srcw, P], [srcrun, nb],
                                                     [1, srcrun]])))
                                    VS(_)
                                    t = st[f"{tag}_xp_{key}"]
                                cur = ("exps", soff, run * 2, EW)
                            run = cur[2]
                            for half in range(2):
                                def _(g, half=half, cur=cur, run=run, nb=nb, t=t,
                                      key=key, dsti=dsti):
                                    c.V.wait(g, t)
                                    st[f"{tag}_xp_{key}"] = c.V.tok(g.tensor_copy(
                                        bass.AP(exps2, dsti * I_FFN + half * run,
                                                [[2 * I_FFN, P], [2 * run, nb],
                                                 [1, run]]),
                                        bass.AP(exps, cur[1],
                                                [[EW, P], [run, nb], [1, run]])))
                                VS(_)
                                t = st[f"{tag}_xp_{key}"]
                            return t

                        mats = [(wg_d, wqg, NI, D), (wu_d, wqu, NI, D),
                                (wd_d, wqd, ND, I_FFN)]
                        for mi, (wsrc, wdst, ntile, wid) in enumerate(mats):
                            nb = wid // 32
                            for ti in range(ntile):
                                key = f"{mi}_{ti}"

                                def _(g, wsrc=wsrc, ti=ti, wid=wid, key=key):
                                    c.Ds.wait(g, st.get(f"{tag}_dq_wfree"))
                                    st[f"{tag}_wfld_{key}"] = c.Ds.tok(g.dma_start(
                                        out=wfp[:, 0:wid],
                                        in_=bass.AP(wsrc, ti * P * wid,
                                                    [[wid, P], [1, wid]])))
                                YS(_)

                                def _(g, wid=wid, nb=nb, key=key):
                                    c.V.wait(g, st[f"{tag}_wfld_{key}"])
                                    c.V.wait(g, st.get(f"{tag}_dq_bmfree"))
                                    t = c.V.tok(g.tensor_reduce(
                                        bm[:, 0, 0:nb].rearrange("p (a b) -> p a b", b=1),
                                        wfp[:, 0:wid].rearrange("p (a b) -> p a b", b=32),
                                        AX.X, AL.max, apply_absolute_value=True))
                                    c.V.wait(g, t)
                                    t = c.V.tok(g.tensor_scalar(
                                        bm[:, 0, 0:nb], bm[:, 0, 0:nb], 1.0 / 7.0,
                                        1e-12, AL.mult, AL.max))
                                    c.V.wait(g, t)
                                    st[f"{tag}_rsb_{key}"] = c.V.tok(g.reciprocal(
                                        bm[:, 1, 0:nb], bm[:, 0, 0:nb]))
                                VS(_)

                                tA = expand(0, 1, nb, st[f"{tag}_rsb_{key}"], key)
                                tB = expand(1, 0, nb, tA, key)

                                def _(g, wid=wid, key=key, tB=tB):
                                    c.V.wait(g, tB)
                                    c.V.wait(g, st.get(f"{tag}_dq_wqfree"))
                                    t = c.V.tok(g.tensor_mul(
                                        wrn[:, 0:wid], wfp[:, 0:wid],
                                        exps2[:, 0, 0:wid]))
                                    st[f"{tag}_dq_wfree"] = t
                                    c.V.wait(g, t)
                                    t = c.V.tok(g.tensor_scalar(
                                        wrn[:, 0:wid], wrn[:, 0:wid], MAGIC, MAGIC,
                                        AL.add, AL.subtract))
                                    c.V.wait(g, t)
                                    t = c.V.tok(g.tensor_mul(
                                        wqb[:, 0:wid], wrn[:, 0:wid],
                                        exps2[:, 1, 0:wid]))
                                    st[f"{tag}_wq_{key}"] = t
                                    st[f"{tag}_dq_bmfree"] = t
                                VS(_)

                                def _(g, wdst=wdst, ti=ti, wid=wid, key=key):
                                    c.G.wait(g, st[f"{tag}_wq_{key}"])
                                    st[f"{tag}_dq_wqfree"] = c.Dg.tok(g.dma_start(
                                        out=bass.AP(wdst, ti * P * wid,
                                                    [[wid, P], [1, wid]]),
                                        in_=wqb[:, 0:wid]))
                                GS(_)
                    fence()

                emit_ln(tag, xdep, None)
                rstdc = sb("rstdc", [P, 512], F32)
                mrsc = sb("mrsc", [P, 512], F32)
                ucc = sb("ucc", [P, 2, 512], F32)
                xqb = sb("xqb", [P, ND, TL], BF16)

                def bc_stats(ch, dep, key):
                    def _(g, ch=ch, dep=dep, key=key):
                        c.G.wait(g, dep)
                        g.dma_start(out=rstdc[:, :],
                                    in_=bass.AP(rows, TL + ch * 512,
                                                [[2 * TL, 1], [0, P], [1, 512]])
                                    ).then_inc(c.Dg.sem, 16)
                        c.Dg.n += 16
                        st[f"{tag}_bcs_{key}"] = c.Dg.tok(g.dma_start(
                            out=mrsc[:, :],
                            in_=bass.AP(rows, ch * 512,
                                        [[2 * TL, 1], [0, P], [1, 512]])))
                    GS(_)
                    return st[f"{tag}_bcs_{key}"]

                def _(g):
                    st[f"{tag}_am0"] = c.V.tok(g.memset(amx[:, 0:1], 0.0))
                VS(_)

                for ch in range(NT):
                    dep = st[f"{tag}_u"] if ch == 0 else st[f"{tag}_amx{ch - 1}"]
                    bdep = bc_stats(ch, dep, f"a{ch}")
                    for dt in range(ND):
                        def _(g, ch=ch, dt=dt, bdep=bdep):
                            sl = slice(ch * 512, (ch + 1) * 512)
                            pq = dt % 2
                            c.V.wait(g, bdep, st[f"{tag}_am0"])
                            t = c.V.tok(g.tensor_mul(ucc[:, pq, :], X[:, dt, sl],
                                                     rstdc[:, :]))
                            c.V.wait(g, t)
                            t = c.V.tok(g.tensor_sub(ucc[:, pq, :], ucc[:, pq, :],
                                                     mrsc[:, :]))
                            c.V.wait(g, t)
                            t = c.V.tok(g.tensor_reduce(
                                amx[:, 1:2], ucc[:, pq, :], AX.X, AL.max,
                                apply_absolute_value=True))
                            c.V.wait(g, t)
                            t = c.V.tok(g.tensor_max(amx[:, 0:1], amx[:, 0:1],
                                                     amx[:, 1:2]))
                            if dt == ND - 1:
                                st[f"{tag}_amx{ch}"] = t
                        VS(_)

                tf = fold_max_col(amx, 0, st[f"{tag}_amx{NT - 1}"], "x")

                def _(g, tf=tf):
                    c.V.wait(g, tf)
                    st[f"{tag}_arw0"] = c.V.tok(g.tensor_copy(arsb[0:1, 0:1],
                                                              amx[0:1, 0:1]))
                VS(_)

                ta = allreduce_lanes("x", 1, st[f"{tag}_arw0"])

                def _(g, ta=ta):
                    c.V.wait(g, ta)
                    t = c.V.tok(g.tensor_scalar(arc[:, 1:2], colt[:, 0:1],
                                                1.0 / 32767.0, 1e-12, AL.mult, AL.max))
                    c.V.wait(g, t)
                    st[f"{tag}_sx"] = c.V.tok(g.reciprocal(arc[:, 0:1], arc[:, 1:2]))
                VS(_)

                for ch in range(NT):
                    bdep = bc_stats(ch, st.get(f"{tag}_xq{ch - 1}", st[f"{tag}_sx"]),
                                    f"b{ch}")
                    for dt in range(ND):
                        def _(g, ch=ch, dt=dt, bdep=bdep):
                            sl = slice(ch * 512, (ch + 1) * 512)
                            pq = dt % 2
                            c.V.wait(g, bdep, st[f"{tag}_sx"])
                            t = c.V.tok(g.tensor_mul(ucc[:, pq, :], X[:, dt, sl],
                                                     rstdc[:, :]))
                            c.V.wait(g, t)
                            t = c.V.tok(g.tensor_sub(ucc[:, pq, :], ucc[:, pq, :],
                                                     mrsc[:, :]))
                            c.V.wait(g, t)
                            t = c.V.tok(g.tensor_scalar(
                                ucc[:, pq, :], ucc[:, pq, :], arc[:, 0:1], MAGIC,
                                AL.mult, AL.add))
                            c.V.wait(g, t)
                            t = c.V.tok(g.tensor_scalar(
                                xqb[:, dt, sl], ucc[:, pq, :], MAGIC, arc[:, 1:2],
                                AL.subtract, AL.mult))
                            if dt == ND - 1:
                                st[f"{tag}_xq{ch}"] = t
                        VS(_)
                st[f"{tag}_xqd"] = st[f"{tag}_xq{NT - 1}"]
                fence()

                with ExitStack() as mm:
                    slabs = mm.enter_context(
                        nc.sbuf_tensor(f"{tag}_slabs", [P, ND, I_FFN], BF16))
                    evb = [mm.enter_context(
                        nc.sbuf_tensor(f"{tag}_evb{i}", [P, 512], BF16))
                        for i in range(2)]

                    for mi, (wqsrc, dstd, lane) in enumerate(
                            [(wqu, upd, 2), (wqg, gated, 3)]):
                        def _(g, mi=mi, wqsrc=wqsrc):
                            for kt in range(ND):
                                c.Ds.wait(g, st.get(f"{tag}_mmdone{mi - 1}"))
                                g.dma_start(
                                    out=slabs[:, kt, :],
                                    in_=bass.AP(wqsrc, kt * P,
                                                [[D, I_FFN], [1, P]]),
                                    transpose=True).then_inc(c.Ds.sem, 16)
                                c.Ds.n += 16
                            st[f"{tag}_slab{mi}"] = (c.Ds.sem, c.Ds.n)
                        YS(_)

                        def _(g, mi=mi, lane=lane):
                            st[f"{tag}_amz{mi}"] = c.V.tok(g.memset(
                                amx[:, 2:3], 0.0))
                        VS(_)

                        for tt in range(NTT):
                            for oc in range(NOC):
                                gi = tt * NOC + oc
                                pq = gi % 2

                                def _(g, mi=mi, tt=tt, oc=oc, gi=gi, pq=pq):
                                    c.T.wait(g, st[f"{tag}_slab{mi}"],
                                             st[f"{tag}_xqd"])
                                    c.T.wait(g, st.get(f"{tag}_ev{mi}_{gi - 2}"))
                                    for kt in range(ND):
                                        mmi = g.matmul(
                                            ps_a[pq][:, :],
                                            xqb[:, kt, tt * P:(tt + 1) * P],
                                            slabs[:, kt, oc * 512:(oc + 1) * 512],
                                            start=(kt == 0), stop=(kt == ND - 1))
                                    st[f"{tag}_mm{mi}_{gi}"] = c.T.tok(mmi)
                                TS(_)

                                def _(g, mi=mi, gi=gi, pq=pq):
                                    c.V.wait(g, st[f"{tag}_mm{mi}_{gi}"])
                                    c.V.wait(g, st.get(f"{tag}_st{mi}_{gi - 2}"))
                                    st[f"{tag}_ev{mi}_{gi}"] = c.V.tok(g.tensor_copy(
                                        evb[pq][:, :], ps_a[pq][:, :]))
                                VS(_)

                                def _(g, mi=mi, gi=gi, pq=pq):
                                    c.V.wait(g, st[f"{tag}_ev{mi}_{gi}"],
                                             st[f"{tag}_amz{mi}"])
                                    c.V.wait(g, st.get(f"{tag}_amv{mi}_{gi - 1}"))
                                    t = c.V.tok(g.tensor_reduce(
                                        amx[:, 3:4], evb[pq][:, :], AX.X, AL.max,
                                        apply_absolute_value=True))
                                    c.V.wait(g, t)
                                    st[f"{tag}_amv{mi}_{gi}"] = c.V.tok(g.tensor_max(
                                        amx[:, 2:3], amx[:, 2:3], amx[:, 3:4]))
                                VS(_)

                                def _(g, mi=mi, tt=tt, oc=oc, gi=gi, pq=pq,
                                      dstd=dstd):
                                    c.G.wait(g, st[f"{tag}_ev{mi}_{gi}"])
                                    st[f"{tag}_st{mi}_{gi}"] = c.Dg.tok(g.dma_start(
                                        out=bass.AP(dstd,
                                                    tt * P * I_FFN + oc * 512,
                                                    [[I_FFN, P], [1, 512]]),
                                        in_=evb[pq][:, :]))
                                GS(_)

                        tfm = fold_max_col(amx, 2,
                                           st[f"{tag}_amv{mi}_{NTT * NOC - 1}"],
                                           f"m{mi}")

                        def _(g, mi=mi, tfm=tfm, lane=lane):
                            c.V.wait(g, tfm)
                            st[f"{tag}_mmdone{mi}"] = c.V.tok(g.tensor_copy(
                                arsb[0:1, mi:mi + 1], amx[0:1, 2:3]))
                        VS(_)

                tb = allreduce_lanes("ug", 2, st[f"{tag}_mmdone1"])

                def _(g, tb=tb):
                    c.V.wait(g, tb)
                    t = c.V.tok(g.tensor_scalar(arc[:, 5:6], colt[:, 0:1],
                                                1.0 / 32767.0, 1e-12, AL.mult, AL.max))
                    c.V.wait(g, t)
                    t = c.V.tok(g.reciprocal(arc[:, 2:3], arc[:, 5:6]))
                    c.V.wait(g, t)
                    t = c.V.tok(g.tensor_scalar(arc[:, 4:5], colt[:, 1:2],
                                                1.0 / 32767.0, 1e-12, AL.mult, AL.max))
                    c.V.wait(g, t)
                    st[f"{tag}_scug"] = c.V.tok(g.reciprocal(arc[:, 3:4],
                                                             arc[:, 4:5]))
                VS(_)
                fence()

                with ExitStack() as pb:
                    grow = [pb.enter_context(
                        nc.sbuf_tensor(f"{tag}_grow{i}", [P, I_FFN], BF16))
                        for i in range(2)]
                    t1r = pb.enter_context(
                        nc.sbuf_tensor(f"{tag}_t1r", [P, I_FFN], F32))
                    urow = pb.enter_context(
                        nc.sbuf_tensor(f"{tag}_urow", [P, I_FFN], BF16))
                    t2r = pb.enter_context(
                        nc.sbuf_tensor(f"{tag}_t2r", [P, I_FFN], F32))

                    def _(g):
                        st[f"{tag}_amz2"] = c.V.tok(g.memset(amx[:, 2:3], 0.0))
                    VS(_)

                    for tt in range(NTT):
                        pq = tt % 2

                        def _(g, tt=tt, pq=pq):
                            c.Ds.wait(g, st[f"{tag}_st1_{tt * NOC + NOC - 1}"])
                            c.Ds.wait(g, st.get(f"{tag}_g2w{tt - 2}"))
                            st[f"{tag}_gld{tt}"] = c.Ds.tok(g.dma_start(
                                out=grow[pq][:, :],
                                in_=bass.AP(gated, tt * P * I_FFN,
                                            [[I_FFN, P], [1, I_FFN]])))
                        YS(_)

                        def _(g, tt=tt, pq=pq):
                            c.V.wait(g, st[f"{tag}_gld{tt}"], st[f"{tag}_scug"],
                                     st[f"{tag}_amz2"])
                            c.V.wait(g, st.get(f"{tag}_g2v{tt - 1}"))
                            t = c.V.tok(g.tensor_scalar(
                                t1r[:, :], grow[pq][:, :], arc[:, 3:4], MAGIC,
                                AL.mult, AL.add))
                            c.V.wait(g, t)
                            st[f"{tag}_t1_{tt}"] = c.V.tok(g.tensor_scalar(
                                t1r[:, :], t1r[:, :], MAGIC, None, AL.subtract))
                        VS(_)

                        def _(g, tt=tt):
                            c.S.wait(g, st[f"{tag}_t1_{tt}"])
                            st[f"{tag}_sg_{tt}"] = c.S.tok(g.activation(
                                t2r[:, :], t1r[:, :], AF.Sigmoid,
                                scale=arc[:, 4:5]))
                        SS(_)

                        def _(g, tt=tt, pq=pq):
                            c.V.wait(g, st[f"{tag}_sg_{tt}"])
                            t = c.V.tok(g.tensor_scalar(
                                t2r[:, :], t2r[:, :], 65536.0, MAGIC,
                                AL.mult, AL.add))
                            c.V.wait(g, t)
                            t = c.V.tok(g.tensor_scalar(
                                t2r[:, :], t2r[:, :], MAGIC, 65535.0,
                                AL.subtract, AL.min))
                            c.V.wait(g, t)
                            t = c.V.tok(g.tensor_mul(grow[pq][:, :], t1r[:, :],
                                                     t2r[:, :]))
                            c.V.wait(g, t)
                            t = c.V.tok(g.tensor_reduce(
                                amx[:, 3:4], grow[pq][:, :], AX.X, AL.max,
                                apply_absolute_value=True))
                            c.V.wait(g, t)
                            st[f"{tag}_g2v{tt}"] = c.V.tok(g.tensor_max(
                                amx[:, 2:3], amx[:, 2:3], amx[:, 3:4]))
                        VS(_)

                        def _(g, tt=tt, pq=pq):
                            c.G.wait(g, st[f"{tag}_g2v{tt}"])
                            st[f"{tag}_g2w{tt}"] = c.Dg.tok(g.dma_start(
                                out=bass.AP(g2d, tt * P * I_FFN,
                                            [[I_FFN, P], [1, I_FFN]]),
                                in_=grow[pq][:, :]))
                        GS(_)

                    tf2 = fold_max_col(amx, 2, st[f"{tag}_g2v{NTT - 1}"], "g2")

                    def _(g, tf2=tf2):
                        c.V.wait(g, tf2)
                        st[f"{tag}_arw2"] = c.V.tok(g.tensor_copy(
                            arsb[0:1, 0:1], amx[0:1, 2:3]))
                    VS(_)

                    tc = allreduce_lanes("g2", 1, st[f"{tag}_arw2"])

                    def _(g, tc=tc):
                        c.V.wait(g, tc)
                        t = c.V.tok(g.tensor_scalar(colt[:, 1:2], arc[:, 4:5],
                                                    1.0 / 65536.0, None, AL.mult))
                        c.V.wait(g, t)
                        t = c.V.tok(g.tensor_mul(colt[:, 2:3], colt[:, 0:1],
                                                 colt[:, 1:2]))
                        c.V.wait(g, t)
                        t = c.V.tok(g.tensor_scalar(colt[:, 2:3], colt[:, 2:3],
                                                    1.0 / 32767.0, 1e-12,
                                                    AL.mult, AL.max))
                        c.V.wait(g, t)
                        t = c.V.tok(g.reciprocal(arc[:, 5:6], colt[:, 2:3]))
                        c.V.wait(g, t)
                        st[f"{tag}_sg2"] = c.V.tok(g.tensor_mul(
                            arc[:, 5:6], arc[:, 5:6], colt[:, 1:2]))
                    VS(_)

                    def _(g):
                        st[f"{tag}_amz3"] = c.V.tok(g.memset(amx[:, 2:3], 0.0))
                    VS(_)

                    for tt in range(NTT):
                        pq = tt % 2

                        def _(g, tt=tt, pq=pq):
                            c.Ds.wait(g, st[f"{tag}_sg2"], st[f"{tag}_g2w{NTT - 1}"])
                            c.Ds.wait(g, st.get(f"{tag}_orw{tt - 2}"))
                            g.dma_start(
                                out=grow[pq][:, :],
                                in_=bass.AP(g2d, tt * P * I_FFN,
                                            [[I_FFN, P], [1, I_FFN]])
                            ).then_inc(c.Ds.sem, 16)
                            c.Ds.n += 16
                            st[f"{tag}_c1ld{tt}"] = c.Ds.tok(g.dma_start(
                                out=urow[:, :],
                                in_=bass.AP(upd, tt * P * I_FFN,
                                            [[I_FFN, P], [1, I_FFN]])))
                        YS(_)

                        def _(g, tt=tt, pq=pq):
                            c.V.wait(g, st[f"{tag}_c1ld{tt}"], st[f"{tag}_amz3"])
                            c.V.wait(g, st.get(f"{tag}_orv{tt - 1}"))
                            t = c.V.tok(g.tensor_scalar(
                                t1r[:, :], urow[:, :], arc[:, 2:3], MAGIC,
                                AL.mult, AL.add))
                            c.V.wait(g, t)
                            t = c.V.tok(g.tensor_scalar(
                                t1r[:, :], t1r[:, :], MAGIC, None, AL.subtract))
                            c.V.wait(g, t)
                            t = c.V.tok(g.tensor_scalar(
                                t2r[:, :], grow[pq][:, :], arc[:, 5:6], MAGIC,
                                AL.mult, AL.add))
                            c.V.wait(g, t)
                            t = c.V.tok(g.tensor_scalar(
                                t2r[:, :], t2r[:, :], MAGIC, None, AL.subtract))
                            c.V.wait(g, t)
                            t = c.V.tok(g.tensor_mul(grow[pq][:, :], t1r[:, :],
                                                     t2r[:, :]))
                            c.V.wait(g, t)
                            t = c.V.tok(g.tensor_reduce(
                                amx[:, 3:4], grow[pq][:, :], AX.X, AL.max,
                                apply_absolute_value=True))
                            c.V.wait(g, t)
                            st[f"{tag}_orv{tt}"] = c.V.tok(g.tensor_max(
                                amx[:, 2:3], amx[:, 2:3], amx[:, 3:4]))
                        VS(_)

                        def _(g, tt=tt, pq=pq):
                            c.G.wait(g, st[f"{tag}_orv{tt}"])
                            st[f"{tag}_orw{tt}"] = c.Dg.tok(g.dma_start(
                                out=bass.AP(ord_, tt * P * I_FFN,
                                            [[I_FFN, P], [1, I_FFN]]),
                                in_=grow[pq][:, :]))
                        GS(_)

                    tf3 = fold_max_col(amx, 2, st[f"{tag}_orv{NTT - 1}"], "or")

                    def _(g, tf3=tf3):
                        c.V.wait(g, tf3)
                        st[f"{tag}_arw3"] = c.V.tok(g.tensor_copy(
                            arsb[0:1, 0:1], amx[0:1, 2:3]))
                    VS(_)

                    td = allreduce_lanes("or", 1, st[f"{tag}_arw3"])

                    def _(g, td=td):
                        c.V.wait(g, td)
                        t = c.V.tok(g.reciprocal(colt[:, 3:4], arc[:, 2:3]))
                        c.V.wait(g, t)
                        t = c.V.tok(g.tensor_mul(colt[:, 1:2], colt[:, 2:3],
                                                 colt[:, 3:4]))
                        c.V.wait(g, t)
                        t = c.V.tok(g.tensor_mul(colt[:, 3:4], colt[:, 0:1],
                                                 colt[:, 1:2]))
                        c.V.wait(g, t)
                        t = c.V.tok(g.tensor_scalar(colt[:, 3:4], colt[:, 3:4],
                                                    1.0 / 32767.0, 1e-12,
                                                    AL.mult, AL.max))
                        c.V.wait(g, t)
                        t = c.V.tok(g.reciprocal(arc[:, 6:7], colt[:, 3:4]))
                        c.V.wait(g, t)
                        t = c.V.tok(g.tensor_mul(arc[:, 6:7], arc[:, 6:7],
                                                 colt[:, 1:2]))
                        c.V.wait(g, t)
                        st[f"{tag}_so"] = c.V.tok(g.tensor_scalar(
                            arc[:, 7:8], colt[:, 3:4], 0.5, None, AL.mult))
                    VS(_)
                fence()

                with ExitStack() as dn:
                    wds = dn.enter_context(
                        nc.sbuf_tensor(f"{tag}_wds", [P, NI, D], BF16))
                    ofq = dn.enter_context(
                        nc.sbuf_tensor(f"{tag}_ofq", [P, NI, 512], BF16))
                    ofb = dn.enter_context(
                        nc.sbuf_tensor(f"{tag}_ofb", [P, 2, 512], BF16))
                    oft = dn.enter_context(
                        nc.sbuf_tensor(f"{tag}_oft", [P, 2, 512], F32))

                    def _(g):
                        for kt in range(NI):
                            g.dma_start(
                                out=wds[:, kt, :],
                                in_=bass.AP(wqd, kt * P, [[I_FFN, D], [1, P]]),
                                transpose=True).then_inc(c.Ds.sem, 16)
                            c.Ds.n += 16
                        st[f"{tag}_wdsld"] = (c.Ds.sem, c.Ds.n)
                    YS(_)

                    for tch in range(NT):
                        for kt in range(NI):
                            def _(g, tch=tch, kt=kt):
                                c.Ds.wait(g, st[f"{tag}_orw{NTT - 1}"])
                                if kt == 0:
                                    c.Ds.wait(g, st.get(f"{tag}_xup{tch - 1}"))
                                st[f"{tag}_ofld_{tch}_{kt}"] = c.Ds.tok(g.dma_start(
                                    out=ofb[:, kt % 2, :],
                                    in_=bass.AP(ord_, tch * 512 * I_FFN + kt * P,
                                                [[I_FFN, 512], [1, P]]),
                                    transpose=True))
                            YS(_)

                            def _(g, tch=tch, kt=kt):
                                c.V.wait(g, st[f"{tag}_ofld_{tch}_{kt}"],
                                         st[f"{tag}_so"])
                                c.V.wait(g, st.get(f"{tag}_ofq_{tch}_{kt - 2}"))
                                t = c.V.tok(g.tensor_scalar(
                                    oft[:, kt % 2, :], ofb[:, kt % 2, :],
                                    arc[:, 6:7], MAGIC, AL.mult, AL.add))
                                c.V.wait(g, t)
                                st[f"{tag}_ofqv_{tch}_{kt}"] = c.V.tok(g.tensor_scalar(
                                    ofq[:, kt, :], oft[:, kt % 2, :], MAGIC, None,
                                    AL.subtract))
                                st[f"{tag}_ofq_{tch}_{kt}"] = st[f"{tag}_ofqv_{tch}_{kt}"]
                            VS(_)

                        for dt in range(ND):
                            pq = dt % 2

                            def _(g, tch=tch, dt=dt, pq=pq):
                                c.T.wait(g, st[f"{tag}_wdsld"])
                                c.T.wait(g, st[f"{tag}_ofq_{tch}_{NI - 1}"])
                                c.T.wait(g, st.get(f"{tag}_dnx_{tch}_{dt - 2}"))
                                for kt in range(NI):
                                    mmi = g.matmul(
                                        ps_a[pq][:, :],
                                        wds[:, kt, dt * P:(dt + 1) * P],
                                        ofq[:, kt, :],
                                        start=(kt == 0), stop=(kt == NI - 1))
                                st[f"{tag}_dnmm_{tch}_{dt}"] = c.T.tok(mmi)
                            TS(_)

                            def _(g, tch=tch, dt=dt, pq=pq):
                                sl = slice(tch * 512, (tch + 1) * 512)
                                c.V.wait(g, st[f"{tag}_dnmm_{tch}_{dt}"])
                                t = c.V.tok(g.scalar_tensor_tensor(
                                    X[:, dt, sl], ps_a[pq][:, :], arc[:, 7:8],
                                    X[:, dt, sl], AL.mult, AL.add))
                                st[f"{tag}_dnx_{tch}_{dt}"] = t
                                if dt == ND - 1:
                                    st[f"{tag}_xup{tch}"] = t
                            VS(_)

                st[f"{tag}_done"] = st[f"{tag}_xup{NT - 1}"]
            fence()
            return st[f"{tag}_done"]

        def emit_l1_attn(tag, xdep):
            with ExitStack() as ls:
                def sb(n, shape, dtype):
                    return ls.enter_context(
                        nc.sbuf_tensor(f"{tag}_{n}", shape, dtype))

                colt = sb("colt", [P, 4], F32)
                acol = sb("acol", [P, 4], F32)
                aco2 = sb("aco2", [P, 4], F32)
                gq8 = sb("gq8", [P, NPT], F32)
                bo2 = sb("bo2", [P, 2], F32)
                amf = sb("amf", [P, 512], F32)
                gb = sb("gb", [P, 512], F32)
                ucc = sb("ucc", [P, 2, 512], F32)
                rstdc = sb("rstdc", [P, 512], F32)
                mrsc = sb("mrsc", [P, 512], F32)
                glocal = sb("glocal", [1, TL], F32)
                grow = sb("grow", [1, SEQ], F32)

                def bc_stats(ch, dep, key):
                    def _(g, ch=ch, dep=dep, key=key):
                        c.G.wait(g, dep)
                        g.dma_start(out=rstdc[:, :],
                                    in_=bass.AP(rows, TL + ch * 512,
                                                [[2 * TL, 1], [0, P], [1, 512]])
                                    ).then_inc(c.Dg.sem, 16)
                        c.Dg.n += 16
                        st[f"{tag}_bcs_{key}"] = c.Dg.tok(g.dma_start(
                            out=mrsc[:, :],
                            in_=bass.AP(rows, ch * 512,
                                        [[2 * TL, 1], [0, P], [1, 512]])))
                    GS(_)
                    return st[f"{tag}_bcs_{key}"]

                def fold_rowmax(dep, key, dstrow, dstw, dstoff):
                    t = dep
                    for k in [64, 32, 16, 8, 4, 2, 1]:
                        def _(g, k=k, t=t, key=key):
                            c.G.wait(g, t)
                            st[f"{tag}_pf_{key}{k}"] = c.Dg.tok(g.dma_start(
                                out=ucc[0:k, 0, :], in_=amf[k:2 * k, :]))
                        GS(_)

                        def _(g, k=k, key=key):
                            c.V.wait(g, st[f"{tag}_pf_{key}{k}"])
                            st[f"{tag}_pv_{key}{k}"] = c.V.tok(g.tensor_max(
                                amf[0:k, :], amf[0:k, :], ucc[0:k, 0, :]))
                        VS(_)
                        t = st[f"{tag}_pv_{key}{k}"]

                    def _(g, t=t, key=key, dstrow=dstrow, dstw=dstw,
                          dstoff=dstoff):
                        c.V.wait(g, t)
                        st[f"{tag}_pr_{key}"] = c.V.tok(g.tensor_copy(
                            bass.AP(dstrow, dstoff, [[dstw, 1], [1, 512]]),
                            amf[0:1, :]))
                    VS(_)
                    return st[f"{tag}_pr_{key}"]

                def walpha(wsrc, wid, nelem, rscol, acolap, key, wfp, wab,
                           acc1, dep):
                    def _(g, key=key):
                        c.V.wait(g, st.get(f"{tag}_alast"))
                        st[f"{tag}_az_{key}"] = c.V.tok(g.memset(acc1[:, 0:1],
                                                                 0.0))
                    VS(_)
                    for ti in range(ND):
                        def _(g, ti=ti, key=key, wsrc=wsrc, wid=wid, dep=dep):
                            c.Ds.wait(g, st.get(f"{tag}_wsa_{key}{ti - 1}"))
                            if ti == 0:
                                c.Ds.wait(g, dep)
                            st[f"{tag}_wsl_{key}{ti}"] = c.Ds.tok(g.dma_start(
                                out=wfp[:, 0:wid],
                                in_=bass.AP(wsrc, ti * P * wid,
                                            [[wid, P], [1, wid]])))
                        YS(_)

                        def _(g, ti=ti, key=key, wid=wid):
                            c.S.wait(g, st[f"{tag}_wsl_{key}{ti}"])
                            c.S.wait(g, st.get(f"{tag}_wsa_{key}{ti - 1}"))
                            st[f"{tag}_wsb_{key}{ti}"] = c.S.tok(g.activation(
                                wab[:, 0:wid], wfp[:, 0:wid], AF.Abs))
                        SS(_)

                        def _(g, ti=ti, key=key, wid=wid):
                            c.V.wait(g, st[f"{tag}_wsb_{key}{ti}"],
                                     st[f"{tag}_az_{key}"])
                            t = c.V.tok(g.tensor_reduce(
                                acc1[:, 1:2], wab[:, 0:wid], AX.X, AL.add))
                            c.V.wait(g, t)
                            st[f"{tag}_wsa_{key}{ti}"] = c.V.tok(g.tensor_add(
                                acc1[:, 0:1], acc1[:, 0:1], acc1[:, 1:2]))
                        VS(_)
                    t = st[f"{tag}_wsa_{key}{ND - 1}"]
                    for k in [64, 32, 16, 8, 4, 2, 1]:
                        def _(g, k=k, t=t, key=key, acc1=acc1):
                            c.G.wait(g, t)
                            st[f"{tag}_wfg_{key}{k}"] = c.Dg.tok(g.dma_start(
                                out=colt[0:k, 3:4], in_=acc1[k:2 * k, 0:1]))
                        GS(_)

                        def _(g, k=k, key=key, acc1=acc1):
                            c.V.wait(g, st[f"{tag}_wfg_{key}{k}"])
                            st[f"{tag}_wfv_{key}{k}"] = c.V.tok(g.tensor_add(
                                acc1[0:k, 0:1], acc1[0:k, 0:1], colt[0:k, 3:4]))
                        VS(_)
                        t = st[f"{tag}_wfv_{key}{k}"]

                    def _(g, t=t, key=key, acc1=acc1):
                        c.G.wait(g, t)
                        st[f"{tag}_abc_{key}"] = c.Dg.tok(g.dma_start(
                            out=colt[:, 3:4],
                            in_=bass.AP(acc1, 0, [[2, 1], [0, P], [1, 1]])))
                    GS(_)

                    def _(g, key=key, nelem=nelem, rscol=rscol, acolap=acolap):
                        c.V.wait(g, st[f"{tag}_abc_{key}"])
                        t = c.V.tok(g.tensor_scalar(
                            acolap, colt[:, 3:4], 1.0 / nelem, 1e-8,
                            AL.mult, AL.max))
                        c.V.wait(g, t)
                        st[f"{tag}_al_{key}"] = c.V.tok(
                            g.reciprocal(rscol, acolap))
                        st[f"{tag}_alast"] = st[f"{tag}_al_{key}"]
                    VS(_)
                    return st[f"{tag}_al_{key}"]

                def wtern(wsrc, wid, ntile, dstw, dst, rscol, adep, key, wfp,
                          wab):
                    for ti in range(ntile):
                        def _(g, ti=ti, key=key, wsrc=wsrc, wid=wid,
                              dstw=dstw):
                            c.Ds.wait(g, st.get(f"{tag}_tn_{key}{ti - 1}"))
                            if ti == 0:
                                c.Ds.wait(g, st[f"{tag}_wsb_{key}{ND - 1}"])
                            st[f"{tag}_tl_{key}{ti}"] = c.Ds.tok(g.dma_start(
                                out=wfp[:, 0:dstw],
                                in_=bass.AP(wsrc, ti * P * wid,
                                            [[wid, P], [1, dstw]])))
                        YS(_)

                        def _(g, ti=ti, key=key, dstw=dstw, dst=dst,
                              rscol=rscol, adep=adep):
                            c.V.wait(g, st[f"{tag}_tl_{key}{ti}"], adep)
                            t = c.V.tok(g.tensor_scalar(
                                wab[:, 0:dstw], wfp[:, 0:dstw], rscol, 0.5,
                                AL.mult, AL.is_ge))
                            c.V.wait(g, t)
                            t = c.V.tok(g.tensor_scalar(
                                wfp[:, 0:dstw], wfp[:, 0:dstw], rscol, -0.5,
                                AL.mult, AL.is_le))
                            c.V.wait(g, t)
                            st[f"{tag}_tn_{key}{ti}"] = c.V.tok(g.tensor_sub(
                                dst[:, ti, 0:dstw], wab[:, 0:dstw],
                                wfp[:, 0:dstw]))
                        VS(_)
                    return st[f"{tag}_tn_{key}{ntile - 1}"]

                emit_ln(tag, xdep, None)

                with ExitStack() as xs_:
                    xqb = xs_.enter_context(
                        nc.sbuf_tensor(f"{tag}_xqb", [P, ND, TL], BF16))

                    for ch in range(NT):
                        dep = (st[f"{tag}_u"] if ch == 0
                               else st[f"{tag}_gx{ch - 1}"])
                        bdep = bc_stats(ch, dep, f"a{ch}")
                        for dt in range(ND):
                            def _(g, ch=ch, dt=dt, bdep=bdep):
                                sl = slice(ch * 512, (ch + 1) * 512)
                                c.V.wait(g, bdep)
                                t = c.V.tok(g.tensor_mul(
                                    ucc[:, 0, :], X[:, dt, sl], rstdc[:, :]))
                                c.V.wait(g, t)
                                st[f"{tag}_uc_{ch}_{dt}"] = c.V.tok(
                                    g.tensor_sub(ucc[:, 0, :], ucc[:, 0, :],
                                                 mrsc[:, :]))
                            VS(_)

                            def _(g, ch=ch, dt=dt):
                                c.S.wait(g, st[f"{tag}_uc_{ch}_{dt}"])
                                c.S.wait(g, st.get(f"{tag}_ab_{ch}_{dt - 1}"))
                                st[f"{tag}_abs_{ch}_{dt}"] = c.S.tok(
                                    g.activation(ucc[:, 1, :], ucc[:, 0, :],
                                                 AF.Abs))
                            SS(_)

                            def _(g, ch=ch, dt=dt):
                                c.V.wait(g, st[f"{tag}_abs_{ch}_{dt}"])
                                if dt == 0:
                                    t = c.V.tok(g.tensor_copy(amf[:, :],
                                                              ucc[:, 1, :]))
                                else:
                                    t = c.V.tok(g.tensor_max(
                                        amf[:, :], amf[:, :], ucc[:, 1, :]))
                                st[f"{tag}_ab_{ch}_{dt}"] = t
                            VS(_)

                        st[f"{tag}_gx{ch}"] = fold_rowmax(
                            st[f"{tag}_ab_{ch}_{ND - 1}"], f"gx{ch}",
                            glocal, TL, ch * 512)

                    def _(g):
                        c.V.wait(g, st[f"{tag}_gx{NT - 1}"])
                        st[f"{tag}_gam"] = c.V.tok(g.tensor_scalar(
                            glocal[0:1, :], glocal[0:1, :], 1e-8, 1.0 / 127.0,
                            AL.max, AL.mult))
                    VS(_)

                    def _(g):
                        c.G.wait(g, st[f"{tag}_gam"])
                        st[f"{tag}_gsnd"] = c.Dg.tok(g.dma_start(
                            out=agg_in.ap(), in_=glocal[0:1, :]))
                        c.G.wait(g, st[f"{tag}_gsnd"])
                        cc = g.collective_compute(
                            "AllGather", AL.bypass,
                            replica_groups=[[0, 1], [2, 3], [4, 5], [6, 7]],
                            ins=[agg_in.ap().opt()], outs=[agg_out.ap().opt()])
                        st[f"{tag}_agg"] = c.CC.tok(cc)
                        c.G.wait(g, st[f"{tag}_agg"])
                        g.dma_start(out=bass.AP(grow, 0, [[SEQ, 1], [1, TL]]),
                                    in_=agg_out[0:1, :]).then_inc(c.Dg.sem, 16)
                        c.Dg.n += 16
                        st[f"{tag}_gfull"] = c.Dg.tok(g.dma_start(
                            out=bass.AP(grow, TL, [[SEQ, 1], [1, TL]]),
                            in_=agg_out[1:2, :]))
                    GS(_)

                    def _(g):
                        c.V.wait(g, st[f"{tag}_gsnd"])
                        st[f"{tag}_rg"] = c.V.tok(g.reciprocal(
                            glocal[0:1, :], glocal[0:1, :]))
                    VS(_)

                    for ch in range(NT):
                        def _(g, ch=ch):
                            c.G.wait(g, st[f"{tag}_rg"])
                            c.G.wait(g, st.get(f"{tag}_xi{ch - 1}"))
                            st[f"{tag}_rgb{ch}"] = c.Dg.tok(g.dma_start(
                                out=gb[:, :],
                                in_=bass.AP(glocal, ch * 512,
                                            [[TL, 1], [0, P], [1, 512]])))
                        GS(_)
                        bdep = bc_stats(ch, st[f"{tag}_rg"] if ch == 0
                                        else st[f"{tag}_xi{ch - 1}"], f"b{ch}")
                        for dt in range(ND):
                            def _(g, ch=ch, dt=dt, bdep=bdep):
                                sl = slice(ch * 512, (ch + 1) * 512)
                                pq = dt % 2
                                c.V.wait(g, bdep, st[f"{tag}_rgb{ch}"])
                                t = c.V.tok(g.tensor_mul(
                                    ucc[:, pq, :], X[:, dt, sl], rstdc[:, :]))
                                c.V.wait(g, t)
                                t = c.V.tok(g.tensor_sub(
                                    ucc[:, pq, :], ucc[:, pq, :], mrsc[:, :]))
                                c.V.wait(g, t)
                                t = c.V.tok(g.tensor_mul(
                                    ucc[:, pq, :], ucc[:, pq, :], gb[:, :]))
                                c.V.wait(g, t)
                                t = c.V.tok(g.tensor_scalar(
                                    xqb[:, dt, sl], ucc[:, pq, :], MAGIC,
                                    MAGIC, AL.add, AL.subtract))
                                if dt == ND - 1:
                                    st[f"{tag}_xi{ch}"] = t
                            VS(_)

                    def _(g):
                        c.G.wait(g, st[f"{tag}_xi{NT - 1}"])
                        st[f"{tag}_xsnd"] = c.Dg.tok(g.dma_start(
                            out=agx_in.ap(), in_=xqb[:, :, :]))
                        c.G.wait(g, st[f"{tag}_xsnd"])
                        cc = g.collective_compute(
                            "AllGather", AL.bypass,
                            replica_groups=[[0, 1], [2, 3], [4, 5], [6, 7]],
                            ins=[agx_in.ap().opt()], outs=[agx_out.ap().opt()])
                        st[f"{tag}_agx"] = c.CC.tok(cc)
                    GS(_)
                fence()

                def _(g):
                    g.memset(bo2[:, :], 0.0)
                    g.memset(bo2[0:DH, 0:1], 1.0)
                    st["bo2m"] = c.V.tok(g.memset(bo2[DH:P, 1:2], 1.0))
                VS(_)

                def _(g):
                    st[f"{tag}_qgld"] = c.Ds.tok(g.dma_start(
                        out=amf[0:1, 0:NHEAD], in_=qg_in[0:1, :]))
                YS(_)

                def _(g):
                    c.S.wait(g, st[f"{tag}_qgld"])
                    st[f"{tag}_qgab"] = c.S.tok(g.activation(
                        amf[0:1, 16:32], amf[0:1, 0:16], AF.Abs))
                SS(_)

                def _(g):
                    c.V.wait(g, st[f"{tag}_qgab"])
                    t = c.V.tok(g.tensor_reduce(
                        amf[0:1, 32:33], amf[0:1, 16:32], AX.X, AL.max))
                    c.V.wait(g, t)
                    st[f"{tag}_ngc"] = c.V.tok(g.tensor_scalar(
                        amf[0:1, 32:33], amf[0:1, 32:33], -8.0, None, AL.mult))
                VS(_)

                def _(g):
                    c.G.wait(g, st[f"{tag}_ngc"])
                    g.dma_start(out=acol[:, 3:4],
                                in_=bass.AP(amf, 32, [[512, 1], [0, P], [1, 1]])
                                ).then_inc(c.Dg.sem, 16)
                    c.Dg.n += 16
                    for pt in range(NPT):
                        g.dma_start(out=gq8[0:64, pt:pt + 1],
                                    in_=bass.AP(amf, 2 * pt,
                                                [[512, 1], [0, 64], [1, 1]])
                                    ).then_inc(c.Dg.sem, 16)
                        c.Dg.n += 16
                        g.dma_start(out=gq8[64:P, pt:pt + 1],
                                    in_=bass.AP(amf, 2 * pt + 1,
                                                [[512, 1], [0, 64], [1, 1]])
                                    ).then_inc(c.Dg.sem, 16)
                        c.Dg.n += 16
                    st[f"{tag}_gq8r"] = (c.Dg.sem, c.Dg.n)
                GS(_)

                def _(g):
                    c.V.wait(g, st[f"{tag}_gq8r"])
                    st[f"{tag}_gq8"] = c.V.tok(g.tensor_scalar(
                        gq8[:, :], gq8[:, :], 0.125, None, AL.mult))
                VS(_)

                cds = ExitStack()
                qs = cds.enter_context(
                    nc.sbuf_tensor(f"{tag}_qs", [P, NPT, SEQ], BF16))
                ks = cds.enter_context(
                    nc.sbuf_tensor(f"{tag}_ks", [P, SEQ], BF16))
                vT = cds.enter_context(
                    nc.sbuf_tensor(f"{tag}_vT", [P, 2, SEQ // P, DH + 1],
                                   BF16))

                bcw = ExitStack()
                wqT = bcw.enter_context(
                    nc.sbuf_tensor(f"{tag}_wqT", [P, ND, 512], BF16))
                wkT = bcw.enter_context(
                    nc.sbuf_tensor(f"{tag}_wkT", [P, ND, P], BF16))
                wvT = bcw.enter_context(
                    nc.sbuf_tensor(f"{tag}_wvT", [P, ND, P], BF16))

                with ExitStack() as ws:
                    wfp = ws.enter_context(
                        nc.sbuf_tensor(f"{tag}_wfp", [P, D], F32))
                    wab = ws.enter_context(
                        nc.sbuf_tensor(f"{tag}_wab", [P, D], F32))
                    acc1 = ws.enter_context(
                        nc.sbuf_tensor(f"{tag}_acc1", [P, 2], F32))

                    tq = walpha(wq_td, D, D * D, acol[:, 0:1], aco2[:, 0:1],
                                "q", wfp, wab, acc1, None)
                    tq2 = wtern(wq_td, D, ND, 512, wqT, acol[:, 0:1], tq, "q",
                                wfp, wab)
                    tk_ = walpha(wk_td, 2 * P, D * 2 * P, acol[:, 1:2],
                                 aco2[:, 1:2], "k", wfp, wab, acc1, tq2)
                    tk2 = wtern(wk_td, 2 * P, ND, P, wkT, acol[:, 1:2], tk_,
                                "k", wfp, wab)
                    tv_ = walpha(wv_td, 2 * P, D * 2 * P, acol[:, 2:3],
                                 aco2[:, 2:3], "v", wfp, wab, acc1, tk2)
                    st[f"{tag}_terns"] = wtern(wv_td, 2 * P, ND, P, wvT,
                                               acol[:, 2:3], tv_, "v", wfp,
                                               wab)
                fence()

                with ExitStack() as qp:
                    xsl = qp.enter_context(
                        nc.sbuf_tensor(f"{tag}_xsl", [P, ND, 512], BF16))
                    ccc = qp.enter_context(
                        nc.sbuf_tensor(f"{tag}_ccc", [P, 512], F32))
                    ssc = qp.enter_context(
                        nc.sbuf_tensor(f"{tag}_ssc", [P, 512], F32))
                    qc32 = qp.enter_context(
                        nc.sbuf_tensor(f"{tag}_qc32", [P, 512], F32))
                    qrt = qp.enter_context(
                        nc.sbuf_tensor(f"{tag}_qrt", [P, 512], F32))
                    mrm = qp.enter_context(
                        nc.sbuf_tensor(f"{tag}_mrm", [2, 512], F32))
                    rb = qp.enter_context(
                        nc.sbuf_tensor(f"{tag}_rb", [P, 512], F32))
                    vlow = qp.enter_context(
                        nc.sbuf_tensor(f"{tag}_vlow", [64, 512], F32))

                    def _(g):
                        st[f"{tag}_vones"] = c.V.tok(g.memset(vT[:, :, :, :],
                                                              1.0))
                    VS(_)

                    for tch in range(SEQT):
                        blk, lch = tch // NT, tch % NT

                        def _(g, tch=tch, blk=blk, lch=lch):
                            c.Ds.wait(g, st[f"{tag}_agx"])
                            c.Ds.wait(g, st.get(f"{tag}_vtr_{tch - 1}_7"))
                            g.dma_start(
                                out=xsl[:, :, :],
                                in_=bass.AP(agx_out,
                                            blk * P * ND * TL + lch * 512,
                                            [[ND * TL, P], [TL, ND], [1, 512]])
                            ).then_inc(c.Ds.sem, 16)
                            c.Ds.n += 16
                            g.dma_start(out=ccc[:, :],
                                        in_=bass.AP(cc_in, tch * 512,
                                                    [[SEQ, P], [1, 512]])
                                        ).then_inc(c.Ds.sem, 16)
                            c.Ds.n += 16
                            st[f"{tag}_cld{tch}"] = c.Ds.tok(g.dma_start(
                                out=ssc[:, :],
                                in_=bass.AP(ss_in, tch * 512,
                                            [[SEQ, P], [1, 512]])))
                        YS(_)

                        def _(g, tch=tch):
                            c.G.wait(g, st[f"{tag}_gfull"])
                            c.G.wait(g, st.get(f"{tag}_evv_{tch - 1}_5"))
                            st[f"{tag}_gbb{tch}"] = c.Dg.tok(g.dma_start(
                                out=gb[:, :],
                                in_=bass.AP(grow, tch * 512,
                                            [[SEQ, 1], [0, P], [1, 512]])))
                        GS(_)

                        for pi in range(6):
                            pq = (tch * 6 + pi) % 2
                            aci = [0, 0, 0, 0, 1, 2][pi]

                            def _(g, tch=tch, pi=pi, pq=pq):
                                c.T.wait(g, st[f"{tag}_terns"],
                                         st[f"{tag}_cld{tch}"])
                                c.T.wait(g, st.get(
                                    f"{tag}_evv_{tch}_{pi - 2}" if pi >= 2
                                    else f"{tag}_evv_{tch - 1}_{pi + 4}"))
                                for kt in range(ND):
                                    if pi < 4:
                                        lw = wqT[:, kt, pi * P:(pi + 1) * P]
                                    elif pi == 4:
                                        lw = wkT[:, kt, :]
                                    else:
                                        lw = wvT[:, kt, :]
                                    mmi = g.matmul(ps_a[pq][:, :], lw,
                                                   xsl[:, kt, :],
                                                   start=(kt == 0),
                                                   stop=(kt == ND - 1))
                                st[f"{tag}_mm_{tch}_{pi}"] = c.T.tok(mmi)
                            TS(_)

                            def _(g, tch=tch, pi=pi, pq=pq, aci=aci):
                                c.V.wait(g, st[f"{tag}_mm_{tch}_{pi}"],
                                         st[f"{tag}_gbb{tch}"])
                                if pi == 0:
                                    c.V.wait(g,
                                             st.get(f"{tag}_vtrm_{tch - 1}_6"),
                                             st.get(f"{tag}_vlw{tch - 1}"))
                                st[f"{tag}_evv_{tch}_{pi}"] = c.V.tok(
                                    g.scalar_tensor_tensor(
                                        qc32[:, :], ps_a[pq][:, :],
                                        aco2[:, aci:aci + 1], gb[:, :],
                                        AL.mult, AL.mult))
                            VS(_)

                            if pi < 5:
                                def _(g, tch=tch, pi=pi):
                                    c.S.wait(g, st[f"{tag}_evv_{tch}_{pi}"])
                                    c.S.wait(g, st.get(
                                        f"{tag}_rmm_{tch}_{pi - 1}" if pi
                                        else f"{tag}_rmm_{tch - 1}_4"))
                                    st[f"{tag}_sq_{tch}_{pi}"] = c.S.tok(
                                        g.activation(ucc[:, 0, :], qc32[:, :],
                                                     AF.Square))
                                SS(_)

                                def _(g, tch=tch, pi=pi):
                                    c.T.wait(g, st[f"{tag}_sq_{tch}_{pi}"],
                                             st["bo2m"])
                                    c.T.wait(g, st.get(
                                        f"{tag}_mre_{tch}_{pi - 1}" if pi
                                        else f"{tag}_mre_{tch - 1}_4"))
                                    st[f"{tag}_rmm_{tch}_{pi}"] = c.T.tok(
                                        g.matmul(ps_misc[0:2, :], bo2[:, :],
                                                 ucc[:, 0, :], start=True,
                                                 stop=True))
                                TS(_)

                                def _(g, tch=tch, pi=pi):
                                    c.V.wait(g, st[f"{tag}_rmm_{tch}_{pi}"])
                                    c.V.wait(g, st.get(
                                        f"{tag}_rbb_{tch}_{pi - 1}" if pi
                                        else f"{tag}_rbb_{tch - 1}_4"))
                                    st[f"{tag}_mre_{tch}_{pi}"] = c.V.tok(
                                        g.tensor_scalar(
                                            mrm[0:2, :], ps_misc[0:2, :],
                                            1.0 / DH, RMS_EPS, AL.mult,
                                            AL.add))
                                VS(_)

                                def _(g, tch=tch, pi=pi):
                                    c.S.wait(g, st[f"{tag}_mre_{tch}_{pi}"])
                                    t = c.S.tok(g.activation(
                                        mrm[0:2, :], mrm[0:2, :], AF.Ln))
                                    c.S.wait(g, t)
                                    st[f"{tag}_rst_{tch}_{pi}"] = c.S.tok(
                                        g.activation(mrm[0:2, :], mrm[0:2, :],
                                                     AF.Exp, scale=-0.5))
                                SS(_)

                                def _(g, tch=tch, pi=pi):
                                    c.G.wait(g, st[f"{tag}_rst_{tch}_{pi}"])
                                    c.G.wait(g, st.get(
                                        f"{tag}_rt1_{tch}_{pi - 1}" if pi
                                        else f"{tag}_rt1_{tch - 1}_4"))
                                    g.dma_start(
                                        out=rb[0:64, :],
                                        in_=bass.AP(mrm, 0,
                                                    [[512, 1], [0, 64],
                                                     [1, 512]])
                                    ).then_inc(c.Dg.sem, 16)
                                    c.Dg.n += 16
                                    st[f"{tag}_rbb_{tch}_{pi}"] = c.Dg.tok(
                                        g.dma_start(
                                            out=rb[64:P, :],
                                            in_=bass.AP(mrm, 512,
                                                        [[512, 1], [0, 64],
                                                         [1, 512]])))
                                GS(_)

                                def _(g, tch=tch, pi=pi):
                                    c.V.wait(g, st[f"{tag}_rbb_{tch}_{pi}"])
                                    t = c.V.tok(g.tensor_mul(
                                        qc32[:, :], qc32[:, :], rb[:, :]))
                                    c.V.wait(g, t)
                                    c.V.wait(g, st.get(
                                        f"{tag}_rpa_{tch}_{pi - 1}" if pi
                                        else f"{tag}_rpa_{tch - 1}_4"))
                                    t = c.V.tok(g.tensor_mul(
                                        qrt[:, :], qc32[:, :], ssc[:, :]))
                                    c.V.wait(g, t)
                                    st[f"{tag}_rt1_{tch}_{pi}"] = c.V.tok(
                                        g.tensor_mul(qc32[:, :], qc32[:, :],
                                                     ccc[:, :]))
                                VS(_)

                                def _(g, tch=tch, pi=pi):
                                    c.G.wait(g, st[f"{tag}_rt1_{tch}_{pi}"])
                                    for b2 in range(2):
                                        ba = b2 * DH
                                        g.dma_start(
                                            out=qc32[ba:ba + 32, :],
                                            in_=qrt[ba + 32:ba + DH, :],
                                            accum_op=AL.add).then_inc(
                                            c.Dg.sem, 16)
                                        c.Dg.n += 16
                                        g.dma_start(
                                            out=qc32[ba + 32:ba + DH, :],
                                            in_=qrt[ba:ba + 32, :],
                                            accum_op=AL.add).then_inc(
                                            c.Dg.sem, 16)
                                        c.Dg.n += 16
                                    st[f"{tag}_rpa_{tch}_{pi}"] = (c.Dg.sem,
                                                                   c.Dg.n)
                                GS(_)

                                def _(g, tch=tch, pi=pi):
                                    sl = slice(tch * 512, (tch + 1) * 512)
                                    c.V.wait(g, st[f"{tag}_rpa_{tch}_{pi}"],
                                             st[f"{tag}_gq8"])
                                    if pi < 4:
                                        t = c.V.tok(g.tensor_scalar(
                                            qs[:, pi, sl], qc32[:, :],
                                            gq8[:, pi:pi + 1], None, AL.mult))
                                    else:
                                        t = c.V.tok(g.tensor_copy(
                                            ks[:, sl], qc32[:, :]))
                                    st[f"{tag}_fin_{tch}_{pi}"] = t
                                VS(_)
                            else:
                                def _(g, tch=tch):
                                    c.G.wait(g, st[f"{tag}_evv_{tch}_5"])
                                    c.G.wait(g,
                                             st.get(f"{tag}_vtrm_{tch - 1}_7"))
                                    st[f"{tag}_vlw{tch}"] = c.Dg.tok(
                                        g.dma_start(out=vlow[:, :],
                                                    in_=qc32[64:P, :]))
                                GS(_)

                                for i in range(4):
                                    tk = tch * 4 + i

                                    def _(g, tch=tch, i=i):
                                        c.T.wait(g, st["idld"],
                                                 st[f"{tag}_evv_{tch}_5"])
                                        if i == 0:
                                            c.T.wait(g, st.get(
                                                f"{tag}_vtr_{tch - 1}_7"))
                                        else:
                                            c.T.wait(g, st.get(
                                                f"{tag}_vtr_{tch}_{2 * i - 1}"))
                                        st[f"{tag}_vtrm_{tch}_{2 * i}"] = (
                                            c.T.tok(g.transpose(
                                                ps_tr[:, 0:DH],
                                                qc32[0:DH, i * P:(i + 1) * P],
                                                identf[0:DH, 0:DH])))
                                    TS(_)

                                    def _(g, tch=tch, i=i, tk=tk):
                                        c.V.wait(g,
                                                 st[f"{tag}_vtrm_{tch}_{2*i}"],
                                                 st[f"{tag}_vones"])
                                        st[f"{tag}_vtr_{tch}_{2 * i}"] = (
                                            c.V.tok(g.tensor_copy(
                                                vT[:, 0, tk, 0:DH],
                                                ps_tr[:, 0:DH])))
                                    VS(_)

                                    def _(g, tch=tch, i=i):
                                        c.T.wait(g, st[f"{tag}_vlw{tch}"])
                                        c.T.wait(g,
                                                 st[f"{tag}_vtr_{tch}_{2*i}"])
                                        st[f"{tag}_vtrm_{tch}_{2 * i + 1}"] = (
                                            c.T.tok(g.transpose(
                                                ps_tr[:, 0:DH],
                                                vlow[0:DH, i * P:(i + 1) * P],
                                                identf[0:DH, 0:DH])))
                                    TS(_)

                                    def _(g, tch=tch, i=i, tk=tk):
                                        c.V.wait(
                                            g,
                                            st[f"{tag}_vtrm_{tch}_{2*i+1}"])
                                        st[f"{tag}_vtr_{tch}_{2 * i + 1}"] = (
                                            c.V.tok(g.tensor_copy(
                                                vT[:, 1, tk, 0:DH],
                                                ps_tr[:, 0:DH])))
                                    VS(_)
                fence()

                with ExitStack() as ap_:
                    mskt = ap_.enter_context(
                        nc.sbuf_tensor(f"{tag}_mskt", [P, NPT, 512], BF16))
                    esb = ap_.enter_context(
                        nc.sbuf_tensor(f"{tag}_esb", [P, 4, 512], BF16))
                    pses = [ps_a[0], ps_a[1], ps_b[1], ps_misc]
                    den = ap_.enter_context(
                        nc.sbuf_tensor(f"{tag}_den", [1, 512], F32))
                    dnb = ap_.enter_context(
                        nc.sbuf_tensor(f"{tag}_dnb", [DH, 512], F32))
                    ob = ap_.enter_context(
                        nc.sbuf_tensor(f"{tag}_ob", [DH, 512], BF16))

                    def _(g):
                        st[f"{tag}_mskld"] = c.Ds.tok(g.dma_start(
                            out=mskt[:, :, :], in_=msk_in[:, :]))
                    YS(_)

                    for h in range(8):
                        pt = h % 4
                        ro = (h // 4) * DH
                        kv = h // 4
                        for qc in range(SEQT):
                            hq = h * SEQT + qc
                            ntk = (qc + 1) * NT
                            s_base = c.S.n
                            v_base = c.V.n

                            def _(g, h=h, pt=pt, ro=ro, kv=kv, qc=qc, hq=hq,
                                  ntk=ntk, s_base=s_base, v_base=v_base):
                                c.T.wait(g, st.get(f"{tag}_onrm{hq - 1}"))
                                LA = 4
                                for step in range(ntk + LA):
                                    tk = step
                                    if tk < ntk:
                                        pq = tk % 4
                                        if tk >= 4:
                                            c.T.wait(g, (c.S.sem,
                                                         s_base + tk - 3))
                                        mmi = g.matmul(
                                            pses[pq][:, :],
                                            ks[ro:ro + DH,
                                               tk * P:(tk + 1) * P],
                                            qs[ro:ro + DH, pt,
                                               qc * 512:(qc + 1) * 512],
                                            start=True, stop=True)
                                        st[f"{tag}_smm_{h}_{qc}_{tk}"] = (
                                            c.T.tok(mmi))
                                    j = step - LA
                                    if j < 0:
                                        continue
                                    pqj = j % 4
                                    if j >= ntk - NT:
                                        c.T.wait(g, (c.V.sem,
                                                     v_base + j - (ntk - NT)
                                                     + 1))
                                    else:
                                        c.T.wait(g, (c.S.sem,
                                                     s_base + j + 1))
                                    mmi = g.matmul(ps_b[0][0:DH + 1, :],
                                                   vT[:, kv, j, :],
                                                   esb[:, pqj, :],
                                                   start=(j == 0),
                                                   stop=(j == ntk - 1))
                                    t = c.T.tok(mmi)
                                    st[f"{tag}_pv_{h}_{qc}_{j}"] = t
                                    if j == ntk - 1:
                                        st[f"{tag}_pvl{hq}"] = t
                            TS(_)

                            def _(g, h=h, qc=qc, hq=hq, ntk=ntk):
                                c.S.wait(g, st.get(f"{tag}_pvl{hq - 1}"))
                                for tk in range(ntk):
                                    pq = tk % 4
                                    c.S.wait(g, st[f"{tag}_smm_{h}_{qc}_{tk}"])
                                    c.S.wait(g,
                                             st.get(f"{tag}_pv_{h}_{qc}_{tk-4}"))
                                    st[f"{tag}_es_{h}_{qc}_{tk}"] = c.S.tok(
                                        g.activation(esb[:, pq, :],
                                                     pses[pq][:, :], AF.Exp,
                                                     bias=acol[:, 3:4]))
                            SS(_)

                            def _(g, h=h, qc=qc, ntk=ntk):
                                c.V.wait(g, st[f"{tag}_mskld"])
                                for tk in range(ntk - NT, ntk):
                                    pq = tk % 4
                                    mi = tk - (ntk - NT)
                                    c.V.wait(g, st[f"{tag}_es_{h}_{qc}_{tk}"])
                                    st[f"{tag}_mk_{h}_{qc}_{tk}"] = (
                                        c.V.tok(g.tensor_mul(
                                            esb[:, pq, :], esb[:, pq, :],
                                            mskt[:, mi, :])))
                            VS(_)

                            def _(g, hq=hq):
                                c.V.wait(g, st[f"{tag}_pvl{hq}"])
                                st[f"{tag}_den{hq}"] = c.V.tok(g.reciprocal(
                                    den[0:1, :], ps_b[0][DH:DH + 1, :]))
                            VS(_)

                            def _(g, hq=hq):
                                c.G.wait(g, st[f"{tag}_den{hq}"])
                                st[f"{tag}_dnb{hq}"] = c.Dg.tok(g.dma_start(
                                    out=dnb[:, :],
                                    in_=bass.AP(den, 0, [[512, 1], [0, DH],
                                                         [1, 512]])))
                            GS(_)

                            def _(g, hq=hq):
                                c.V.wait(g, st[f"{tag}_dnb{hq}"])
                                st[f"{tag}_onrm{hq}"] = c.V.tok(g.tensor_mul(
                                    ob[:, :], ps_b[0][0:DH, :], dnb[:, :]))
                            VS(_)

                            def _(g, pt=pt, ro=ro, qc=qc, hq=hq):
                                c.G.wait(g, st[f"{tag}_onrm{hq}"])
                                st[f"{tag}_oev{hq}"] = c.Dg.tok(g.dma_start(
                                    out=bass.AP(y_hbm,
                                                (pt * P + ro) * SEQ + qc * 512,
                                                [[SEQ, DH], [1, 512]]),
                                    in_=ob[:, :]))
                            GS(_)
                    fence()
                bcw.close()
                cds.close()

                wpT = sb("wpT", [P, NPT, D], BF16)
                with ExitStack() as ws2:
                    wfp2 = ws2.enter_context(
                        nc.sbuf_tensor(f"{tag}_wfp2", [P, D], F32))
                    wab2 = ws2.enter_context(
                        nc.sbuf_tensor(f"{tag}_wab2", [P, D], F32))
                    acc2 = ws2.enter_context(
                        nc.sbuf_tensor(f"{tag}_acc2", [P, 2], F32))
                    tp = walpha(wp_td, D, D * D, colt[:, 2:3], aco2[:, 3:4],
                                "p", wfp2, wab2, acc2, None)
                    st[f"{tag}_wpt"] = wtern(wp_td, D, NPT, D, wpT,
                                             colt[:, 2:3], tp, "p", wfp2,
                                             wab2)
                fence()

                with ExitStack() as yp:
                    ysc = yp.enter_context(
                        nc.sbuf_tensor(f"{tag}_ysc", [P, NPT, 512], BF16))
                    yq = yp.enter_context(
                        nc.sbuf_tensor(f"{tag}_yq", [P, NPT, 512], BF16))
                    rgy = yp.enter_context(
                        nc.sbuf_tensor(f"{tag}_rgy", [1, SEQ], F32))
                    gya = yp.enter_context(
                        nc.sbuf_tensor(f"{tag}_gya", [1, SEQ], F32))
                    gyb = yp.enter_context(
                        nc.sbuf_tensor(f"{tag}_gyb", [P, 512], F32))
                    evf = yp.enter_context(
                        nc.sbuf_tensor(f"{tag}_evf", [P, 2, 512], F32))

                    for tch in range(SEQT):
                        def _(g, tch=tch):
                            c.Ds.wait(g, st.get(f"{tag}_ym{tch - 1}"))
                            st[f"{tag}_yld{tch}"] = c.Ds.tok(g.dma_start(
                                out=ysc[:, :, :],
                                in_=bass.AP(y_hbm, tch * 512,
                                            [[SEQ, P], [P * SEQ, NPT],
                                             [1, 512]])))
                        YS(_)

                        for pt in range(NPT):
                            def _(g, tch=tch, pt=pt):
                                c.S.wait(g, st[f"{tag}_yld{tch}"])
                                c.S.wait(g, st.get(f"{tag}_yab_{tch}_{pt - 2}"))
                                st[f"{tag}_yas_{tch}_{pt}"] = c.S.tok(
                                    g.activation(ucc[:, pt % 2, :],
                                                 ysc[:, pt, :], AF.Abs))
                            SS(_)

                            def _(g, tch=tch, pt=pt):
                                c.V.wait(g, st[f"{tag}_yas_{tch}_{pt}"])
                                if pt == 0:
                                    t = c.V.tok(g.tensor_copy(amf[:, :],
                                                              ucc[:, 0, :]))
                                else:
                                    t = c.V.tok(g.tensor_max(
                                        amf[:, :], amf[:, :],
                                        ucc[:, pt % 2, :]))
                                st[f"{tag}_yab_{tch}_{pt}"] = t
                            VS(_)

                        st[f"{tag}_ym{tch}"] = fold_rowmax(
                            st[f"{tag}_yab_{tch}_{NPT - 1}"], f"ym{tch}",
                            rgy, SEQ, tch * 512)

                    def _(g):
                        c.G.wait(g, st[f"{tag}_ym{SEQT - 1}"])
                        st[f"{tag}_ysnd"] = c.Dg.tok(g.dma_start(
                            out=ary_in.ap(), in_=rgy[0:1, :]))
                        c.G.wait(g, st[f"{tag}_ysnd"])
                        cc = g.collective_compute(
                            "AllReduce", AL.max,
                            replica_groups=[[0, 1], [2, 3], [4, 5], [6, 7]],
                            ins=[ary_in.ap().opt()], outs=[ary_out.ap().opt()])
                        st[f"{tag}_ary"] = c.CC.tok(cc)
                        c.G.wait(g, st[f"{tag}_ary"])
                        st[f"{tag}_yrld"] = c.Dg.tok(g.dma_start(
                            out=rgy[0:1, :], in_=ary_out.ap()))
                    GS(_)

                    def _(g):
                        c.V.wait(g, st[f"{tag}_yrld"], st[f"{tag}_wpt"])
                        t = c.V.tok(g.tensor_scalar(
                            rgy[0:1, :], rgy[0:1, :], 1e-8, 1.0 / 127.0,
                            AL.max, AL.mult))
                        c.V.wait(g, t)
                        t = c.V.tok(g.tensor_scalar(
                            gya[0:1, :], rgy[0:1, :], aco2[0:1, 3:4], None,
                            AL.mult))
                        c.V.wait(g, t)
                        st[f"{tag}_gyr"] = c.V.tok(g.reciprocal(
                            rgy[0:1, :], rgy[0:1, :]))
                    VS(_)

                    for tch in range(SEQT):
                        blk, lch = tch // NT, tch % NT

                        def _(g, tch=tch):
                            c.Ds.wait(g, st[f"{tag}_gyr"])
                            c.Ds.wait(g,
                                      st.get(f"{tag}_yqv_{tch - 1}_{NPT - 1}"))
                            st[f"{tag}_yl2{tch}"] = c.Ds.tok(g.dma_start(
                                out=ysc[:, :, :],
                                in_=bass.AP(y_hbm, tch * 512,
                                            [[SEQ, P], [P * SEQ, NPT],
                                             [1, 512]])))
                        YS(_)

                        def _(g, tch=tch):
                            c.G.wait(g, st[f"{tag}_gyr"])
                            c.G.wait(g,
                                     st.get(f"{tag}_pex_{tch - 1}_{ND - 1}"))
                            g.dma_start(out=gb[:, :],
                                        in_=bass.AP(rgy, tch * 512,
                                                    [[SEQ, 1], [0, P],
                                                     [1, 512]])
                                        ).then_inc(c.Dg.sem, 16)
                            c.Dg.n += 16
                            st[f"{tag}_gyb{tch}"] = c.Dg.tok(g.dma_start(
                                out=gyb[:, :],
                                in_=bass.AP(gya, tch * 512,
                                            [[SEQ, 1], [0, P], [1, 512]])))
                        GS(_)

                        for pt in range(NPT):
                            def _(g, tch=tch, pt=pt):
                                c.V.wait(g, st[f"{tag}_yl2{tch}"],
                                         st[f"{tag}_gyb{tch}"])
                                if pt == 0:
                                    c.V.wait(g, st.get(
                                        f"{tag}_pmm_{tch - 1}_{ND - 1}"))
                                t = c.V.tok(g.tensor_mul(
                                    ucc[:, pt % 2, :], ysc[:, pt, :],
                                    gb[:, :]))
                                c.V.wait(g, t)
                                st[f"{tag}_yqv_{tch}_{pt}"] = c.V.tok(
                                    g.tensor_scalar(
                                        yq[:, pt, :], ucc[:, pt % 2, :],
                                        MAGIC, MAGIC, AL.add, AL.subtract))
                            VS(_)

                        for dt in range(ND):
                            pq = dt % 2

                            def _(g, tch=tch, dt=dt, pq=pq):
                                c.T.wait(g, st[f"{tag}_wpt"],
                                         st[f"{tag}_yqv_{tch}_{NPT - 1}"])
                                c.T.wait(g, st.get(
                                    f"{tag}_pev_{tch}_{dt - 2}" if dt >= 2
                                    else f"{tag}_pev_{tch - 1}_{dt + 6}"))
                                for kt in range(NPT):
                                    mmi = g.matmul(
                                        ps_a[pq][:, :],
                                        wpT[:, kt, dt * P:(dt + 1) * P],
                                        yq[:, kt, :],
                                        start=(kt == 0), stop=(kt == NPT - 1))
                                st[f"{tag}_pmm_{tch}_{dt}"] = c.T.tok(mmi)
                            TS(_)

                            def _(g, tch=tch, dt=dt, pq=pq):
                                c.V.wait(g, st[f"{tag}_pmm_{tch}_{dt}"])
                                c.V.wait(g, st.get(
                                    f"{tag}_pex_{tch}_{dt - 2}" if dt >= 2
                                    else f"{tag}_pex_{tch - 1}_{dt + 6}"))
                                st[f"{tag}_pev_{tch}_{dt}"] = c.V.tok(
                                    g.tensor_mul(evf[:, pq, :],
                                                 ps_a[pq][:, :], gyb[:, :]))
                            VS(_)

                            def _(g, tch=tch, dt=dt, pq=pq, blk=blk,
                                  lch=lch):
                                c.G.wait(g, st[f"{tag}_pev_{tch}_{dt}"])
                                st[f"{tag}_pex_{tch}_{dt}"] = c.Dg.tok(
                                    g.dma_start(
                                        out=bass.AP(
                                            pps_hbm,
                                            blk * D * TL + dt * P * TL
                                            + lch * 512,
                                            [[TL, P], [1, 512]]),
                                        in_=evf[:, pq, :]))
                            GS(_)

                    def _(g):
                        c.G.wait(g, st[f"{tag}_pex_{SEQT - 1}_{ND - 1}"])
                        cc = g.collective_compute(
                            "ReduceScatter", AL.add,
                            replica_groups=[[0, 1], [2, 3], [4, 5], [6, 7]],
                            ins=[pps_hbm.ap().opt()],
                            outs=[rsy_out.ap().opt()])
                        st[f"{tag}_rs"] = c.CC.tok(cc)
                    GS(_)

                    for ch in range(NT):
                        for dt in range(ND):
                            idx = ch * ND + dt
                            pq = idx % 2

                            def _(g, dt=dt, ch=ch, idx=idx, pq=pq):
                                c.Ds.wait(g, st[f"{tag}_rs"])
                                c.Ds.wait(g, st.get(f"{tag}_xf_{idx - 2}"))
                                st[f"{tag}_rl_{idx}"] = c.Ds.tok(g.dma_start(
                                    out=evf[:, pq, :],
                                    in_=bass.AP(rsy_out,
                                                dt * P * TL + ch * 512,
                                                [[TL, P], [1, 512]])))
                            YS(_)

                            def _(g, dt=dt, ch=ch, idx=idx, pq=pq):
                                sl = slice(ch * 512, (ch + 1) * 512)
                                c.V.wait(g, st[f"{tag}_rl_{idx}"])
                                st[f"{tag}_xf_{idx}"] = c.V.tok(g.tensor_add(
                                    X[:, dt, sl], X[:, dt, sl],
                                    evf[:, pq, :]))
                            VS(_)

                    st[f"{tag}_done"] = st[f"{tag}_xf_{NT * ND - 1}"]
            fence()
            return st[f"{tag}_done"]

        dep = st["xld"]
        if stage >= 1:
            dep = emit_l0_ffn_sublayer("f0", dep)
        if stage >= 2:
            dep = emit_mamba("m0", dep)
        if stage >= 3:
            dep = emit_l0_ffn_sublayer("f1", dep)
        if stage >= 4:
            dep = emit_l1_ffn("q0", dep, True)
        if stage >= 5:
            dep = emit_l1_attn("a1", dep)
        if stage >= 6:
            dep = emit_l1_ffn("q1", dep, False)

        def _(g):
            c.Ds.wait(g, dep)
            t = c.Ds.tok(g.dma_start(
                out=bass.AP(xout, 0, [[TL, P], [P * TL, ND], [1, TL]]),
                in_=X[:, :, :]))
            c.Ds.wait(g, t)
        YS(_)

    return nc


def _prep_inputs(inputs):
    f32 = np.float32
    x = np.asarray(inputs["L0_attn_u"], f32)
    ident = np.eye(P, dtype=f32)
    w1 = np.ascontiguousarray(np.asarray(inputs["L0_ffn_W1"], f32))
    w2 = np.ascontiguousarray(np.asarray(inputs["L0_ffn_W2"], f32))
    wrv = np.ascontiguousarray(np.asarray(inputs["L0_ffn_Wr"], f32))
    br = np.asarray(inputs["L0_ffn_br"], f32).reshape(NE, 1)
    b1 = np.asarray(inputs["L0_ffn_b1"], f32)
    b1tv = np.ascontiguousarray(b1.reshape(NE, NH, P).transpose(0, 2, 1))
    b2 = np.ascontiguousarray(np.asarray(inputs["L0_ffn_b2"], f32))
    wdt = np.ascontiguousarray(np.asarray(inputs["L0_attn_W_dt"], f32).T)
    wb = np.ascontiguousarray(np.asarray(inputs["L0_attn_W_B"], f32).T)
    wc = np.ascontiguousarray(np.asarray(inputs["L0_attn_W_C"], f32).T)
    bdt = np.ascontiguousarray(
        np.asarray(inputs["L0_attn_b_dt"], f32).reshape(ND, P).T)
    loga = np.ascontiguousarray(
        np.asarray(inputs["L0_attn_log_A"], f32).reshape(ND, P, NS))
    dpar = np.ascontiguousarray(
        np.asarray(inputs["L0_attn_D_param"], f32).reshape(ND, P).T)
    wg = np.ascontiguousarray(np.asarray(inputs["L1_ffn_w_gate"], f32))
    wu = np.ascontiguousarray(np.asarray(inputs["L1_ffn_w_up"], f32))
    wd = np.ascontiguousarray(np.asarray(inputs["L1_ffn_w_down"], f32))
    wq = np.asarray(inputs["L1_attn_Wq"], f32)
    wk = np.asarray(inputs["L1_attn_Wk"], f32)
    wv = np.asarray(inputs["L1_attn_Wv"], f32)
    wp = np.asarray(inputs["L1_attn_Wp"], f32)
    qg = np.asarray(inputs["L1_attn_q_gain"], f32).reshape(-1)
    invf = 1.0 / 10000.0 ** (np.arange(0, DH, 2, dtype=np.float64) / DH)
    fr = np.outer(invf, np.arange(SEQ, dtype=np.float64))
    cosr, sinr = np.cos(fr), np.sin(fr)
    cct = np.ascontiguousarray(np.tile(cosr, (4, 1)).astype(f32))
    sst = np.ascontiguousarray(
        np.concatenate([-sinr, sinr, -sinr, sinr], axis=0).astype(f32))
    jj = np.arange(512)
    pp = np.arange(P)
    mskb = np.ascontiguousarray(np.stack(
        [(jj[None, :] >= (mi * P + pp)[:, None]) for mi in range(4)],
        axis=1).reshape(P, 4 * 512).astype(ml_dtypes.bfloat16))
    per_core = []
    attn_pc = []
    for half in range(2):
        myh = [8 * half + pt + 4 * ab for pt in range(4) for ab in range(2)]
        oth = [8 * (1 - half) + pt + 4 * ab for pt in range(4)
               for ab in range(2)]
        csel = np.concatenate(
            [np.arange(h * DH, (h + 1) * DH) for h in myh + oth])
        mykv = [2 * half, 2 * half + 1]
        okv = [2 * (1 - half), 2 * (1 - half) + 1]
        ksel = np.concatenate(
            [np.arange(k * DH, (k + 1) * DH) for k in mykv + okv])
        attn_pc.append({
            "wq_td": np.ascontiguousarray(wq.T[:, csel]),
            "wk_td": np.ascontiguousarray(wk.T[:, ksel]),
            "wv_td": np.ascontiguousarray(wv.T[:, ksel]),
            "wp_td": np.ascontiguousarray(wp.T[csel, :]),
            "qg_in": np.ascontiguousarray(
                qg[myh + oth].reshape(1, NHEAD)),
        })
    for core in range(NCORES):
        b, half = core // 2, core % 2
        xs = np.ascontiguousarray(x[b, half * TL:(half + 1) * TL, :].T)
        hm = np.full((P, 1), float(half), f32)
        per_core.append({
            "xin": xs, "ident_in": ident, "halfm_in": hm,
            "l0w1": w1, "l0w2": w2, "l0wr": wrv, "l0br": br,
            "l0b1t": b1tv, "l0b2": b2,
            "wdt_t": wdt, "wb_t": wb, "wc_t": wc,
            "bdt_d": bdt, "loga_d": loga, "dpar_d": dpar,
            "wg_d": wg, "wu_d": wu, "wd_d": wd,
            "cc_in": cct, "ss_in": sst, "msk_in": mskb,
            **attn_pc[half],
        })
    return per_core


def kernel(**inputs):
    per_core = _prep_inputs(inputs)
    nc = build(BUILD_STAGE)
    res = run_bass_kernel_spmd(nc, per_core, core_ids=list(range(NCORES)))
    out = np.empty((4, SEQ, D), np.float32)
    for core in range(NCORES):
        b, half = core // 2, core % 2
        out[b, half * TL:(half + 1) * TL, :] = res.results[core]["xout"].T
    return out


# version 06 (final)
# speedup vs baseline: 1.0268x; 1.0268x over previous
import numpy as np
import ml_dtypes
import concourse.bass as bass
import concourse.mybir as mybir
from concourse.bass_utils import run_bass_kernel_spmd
from contextlib import ExitStack

F32 = mybir.dt.float32
BF16 = mybir.dt.bfloat16
AF = mybir.ActivationFunctionType
AL = mybir.AluOpType
AX = mybir.AxisListType

MAGIC = 12582912.0
P = 128
TL = 2048
NT = TL // 512
TH = 512
NTH = TL // TH
D = 1024
ND = D // P
SEQ = 4096
NCORES = 8
LN_EPS = 1e-5
RMS_EPS = 1.1920929e-07
H_FFN = 2048
NH = H_FFN // P
NE = 8
NS = 16
I_FFN = 3072
NHEAD = 16
NKV = 4
DH = 64
SEQT = SEQ // 512
NPT = 4

BUILD_STAGE = 6
DBG_ZERO_SINI = False


class Eng:
    def __init__(self, sem, dma_step=1):
        self.sem = sem
        self.n = 0
        self.step = dma_step
        self.waited = {}

    def wait(self, g, *toks):
        for t in toks:
            if t is None:
                continue
            sem, v = t
            if self.waited.get(sem, -1) >= v:
                continue
            g.wait_ge(sem, v)
            self.waited[sem] = v

    def tok(self, instr):
        instr.then_inc(self.sem, self.step)
        self.n += self.step
        return (self.sem, self.n)


def build(stage=BUILD_STAGE):
    nc = bass.Bass("TRN2", target_bir_lowering=False)

    xin = nc.dram_tensor("xin", [D, TL], F32, kind="ExternalInput")
    xout = nc.dram_tensor("xout", [D, TL], F32, kind="ExternalOutput")
    ident_in = nc.dram_tensor("ident_in", [P, P], F32, kind="ExternalInput")
    halfm_in = nc.dram_tensor("halfm_in", [P, 1], F32, kind="ExternalInput")

    w1 = nc.dram_tensor("l0w1", [NE, D, H_FFN], F32, kind="ExternalInput")
    w2 = nc.dram_tensor("l0w2", [NE, H_FFN, D], F32, kind="ExternalInput")
    wr = nc.dram_tensor("l0wr", [D, NE], F32, kind="ExternalInput")
    brd = nc.dram_tensor("l0br", [NE, 1], F32, kind="ExternalInput")
    b1t = nc.dram_tensor("l0b1t", [NE, P, NH], F32, kind="ExternalInput")
    b2d = nc.dram_tensor("l0b2", [NE, D], F32, kind="ExternalInput")

    wdt_t = nc.dram_tensor("wdt_t", [D, D], F32, kind="ExternalInput")
    wb_t = nc.dram_tensor("wb_t", [D, NS], F32, kind="ExternalInput")
    wc_t = nc.dram_tensor("wc_t", [D, NS], F32, kind="ExternalInput")
    bdt_d = nc.dram_tensor("bdt_d", [P, ND], F32, kind="ExternalInput")
    loga_d = nc.dram_tensor("loga_d", [ND, P, NS], F32, kind="ExternalInput")
    dpar_d = nc.dram_tensor("dpar_d", [P, ND], F32, kind="ExternalInput")

    w1bf = nc.dram_tensor("w1bf", [NE, NH, ND, P, P], BF16, kind="Internal")
    w2bf = nc.dram_tensor("w2bf", [NE, ND, NH, P, P], BF16, kind="Internal")
    wdtbf = nc.dram_tensor("wdtbf", [ND, ND, P, P], BF16, kind="Internal")
    wg_d = nc.dram_tensor("wg_d", [I_FFN, D], F32, kind="ExternalInput")
    wu_d = nc.dram_tensor("wu_d", [I_FFN, D], F32, kind="ExternalInput")
    wd_d = nc.dram_tensor("wd_d", [D, I_FFN], F32, kind="ExternalInput")
    wqg = nc.dram_tensor("wqg", [I_FFN, D], BF16, kind="Internal")
    wqu = nc.dram_tensor("wqu", [I_FFN, D], BF16, kind="Internal")
    wqd = nc.dram_tensor("wqd", [D, I_FFN], BF16, kind="Internal")
    upd = nc.dram_tensor("upd", [TL, I_FFN], BF16, kind="Internal")
    gated = nc.dram_tensor("gated", [TL, I_FFN], BF16, kind="Internal")
    g2d = nc.dram_tensor("g2d", [TL, I_FFN], BF16, kind="Internal")
    ord_ = nc.dram_tensor("ord_", [TL, I_FFN], BF16, kind="Internal")
    ar_in = nc.dram_tensor("ar_in", [1, 64], F32, kind="Internal")
    ar_out = nc.dram_tensor("ar_out", [1, 64], F32, kind="Internal")
    yd = nc.dram_tensor("yd", [ND, P, TL], F32, kind="Internal")
    dtd = nc.dram_tensor("dtd", [ND, P, TL], F32, kind="Internal")
    ag_s_in = nc.dram_tensor("ag_s_in", [P, P], F32, kind="Internal")
    ag_s_out = nc.dram_tensor("ag_s_out", [2 * P, P], F32, kind="Internal")

    wq_td = nc.dram_tensor("wq_td", [D, D], F32, kind="ExternalInput")
    wk_td = nc.dram_tensor("wk_td", [D, 2 * P], F32, kind="ExternalInput")
    wv_td = nc.dram_tensor("wv_td", [D, 2 * P], F32, kind="ExternalInput")
    wp_td = nc.dram_tensor("wp_td", [D, D], F32, kind="ExternalInput")
    qg_in = nc.dram_tensor("qg_in", [1, NHEAD], F32, kind="ExternalInput")
    cc_in = nc.dram_tensor("cc_in", [P, SEQ], F32, kind="ExternalInput")
    ss_in = nc.dram_tensor("ss_in", [P, SEQ], F32, kind="ExternalInput")
    msk_in = nc.dram_tensor("msk_in", [P, 4 * 512], BF16, kind="ExternalInput")
    agg_in = nc.dram_tensor("agg_in", [1, TL], F32, kind="Internal")
    agg_out = nc.dram_tensor("agg_out", [2, TL], F32, kind="Internal")
    agx_in = nc.dram_tensor("agx_in", [P, ND, TL], BF16, kind="Internal")
    agx_out = nc.dram_tensor("agx_out", [2, P, ND, TL], BF16, kind="Internal")
    y_hbm = nc.dram_tensor("y_hbm", [4 * P, SEQ], BF16, kind="Internal")
    ary_in = nc.dram_tensor("ary_in", [1, SEQ], F32, kind="Internal")
    ary_out = nc.dram_tensor("ary_out", [1, SEQ], F32, kind="Internal")
    pps_hbm = nc.dram_tensor("pps_hbm", [2, D, TL], F32, kind="Internal")
    rsy_out = nc.dram_tensor("rsy_out", [D, TL], F32, kind="Internal")

    es = ExitStack()
    with es:
        block = es.enter_context(nc.Block())

        class C:
            pass
        c = C()
        c.V = Eng(es.enter_context(nc.semaphore("sV")))
        c.S = Eng(es.enter_context(nc.semaphore("sS")))
        c.T = Eng(es.enter_context(nc.semaphore("sT")))
        c.G = Eng(es.enter_context(nc.semaphore("sG")))
        c.Dg = Eng(es.enter_context(nc.semaphore("sDg")), 16)
        c.Ds = Eng(es.enter_context(nc.semaphore("sDs")), 16)
        c.CC = Eng(es.enter_context(nc.semaphore("sCC")))

        def sbg(name, shape, dtype):
            return es.enter_context(nc.sbuf_tensor(name, shape, dtype))

        def psg(name, shape, dtype=F32):
            return es.enter_context(nc.psum_tensor(name, shape, dtype))

        VS, SS, TS, GS, YS = (block.vector, block.scalar, block.tensor,
                              block.gpsimd, block.sync)

        X = sbg("X", [P, ND, TL], F32)
        rows = sbg("rows", [1, 2 * TL], F32)
        consts = sbg("consts", [P, 4], F32)
        onesf = sbg("onesf", [P, 1], F32)
        identf = sbg("identf", [P, P], F32)
        halfm = sbg("halfm", [P, 1], F32)

        ps_su = psg("ps_su", [1, 512])
        ps_sq = psg("ps_sq", [1, 512])
        ps_misc = psg("ps_misc", [P, 512])
        ps_tr = psg("ps_tr", [P, P])
        ps_a = [psg(f"ps_a{i}", [P, 512]) for i in range(2)]
        ps_b = [psg(f"ps_b{i}", [P, 512]) for i in range(2)]

        st = {}

        def fence():
            toks = [(e.sem, e.n) for e in (c.V, c.S, c.T, c.G, c.Dg, c.Ds)]

            def _(g):
                c.V.wait(g, *toks)
            VS(_)

            def _(g):
                c.S.wait(g, *toks)
            SS(_)

            def _(g):
                c.T.wait(g, *toks)
            TS(_)

            def _(g):
                c.G.wait(g, *toks)
            GS(_)

            def _(g):
                c.Ds.wait(g, *toks)
            YS(_)

        def _(g):
            st["xld"] = c.Ds.tok(g.dma_start(
                out=X[:, :, :],
                in_=bass.AP(xin, 0, [[TL, P], [P * TL, ND], [1, TL]])))
            st["idld"] = c.Ds.tok(g.dma_start(out=identf[:, :], in_=ident_in[:, :]))
            st["hmld"] = c.Ds.tok(g.dma_start(out=halfm[:, :], in_=halfm_in[:, :]))
        YS(_)

        def _(g):
            g.memset(consts[:, 0:1], 1.0)
            g.memset(consts[:, 1:2], LN_EPS)
            st["consts"] = c.V.tok(g.memset(onesf[:, 0:1], 1.0))
        VS(_)

        def _(g):
            for e in range(NE):
                for ht in range(NH):
                    c.Dg.tok(g.dma_start(
                        out=bass.AP(w1bf, (e * NH + ht) * ND * P * P,
                                    [[1, 1], [1, ND * P * P]]),
                        in_=bass.AP(w1, e * D * H_FFN + ht * P,
                                    [[1, 1], [P * H_FFN, ND], [H_FFN, P], [1, P]])))
                for dt in range(ND):
                    t2 = c.Dg.tok(g.dma_start(
                        out=bass.AP(w2bf, (e * ND + dt) * NH * P * P,
                                    [[1, 1], [1, NH * P * P]]),
                        in_=bass.AP(w2, e * H_FFN * D + dt * P,
                                    [[1, 1], [P * D, NH], [D, P], [1, P]])))
                st[f"wcast{e}"] = t2
            if stage >= 2:
                for et in range(ND):
                    t = c.Dg.tok(g.dma_start(
                        out=bass.AP(wdtbf, et * ND * P * P,
                                    [[1, 1], [1, ND * P * P]]),
                        in_=bass.AP(wdt_t, et * P,
                                    [[1, 1], [P * D, ND], [D, P], [1, P]])))
                st["wdtcast"] = t
        GS(_)

        def emit_ln(tag, xdep, ubt, router=None, rstate=None):
            with ExitStack() as ls:
                sq = ls.enter_context(nc.sbuf_tensor(f"{tag}_sq", [P, 2, 512], F32))
                u32c = ls.enter_context(nc.sbuf_tensor(f"{tag}_u32c", [P, 2, 512], F32))
                rstdb = ls.enter_context(nc.sbuf_tensor(f"{tag}_rstdb", [P, 512], F32))
                mrsb = ls.enter_context(nc.sbuf_tensor(f"{tag}_mrsb", [P, 512], F32))
                logit = (ls.enter_context(nc.sbuf_tensor(f"{tag}_lgt", [NE, 2, 512], F32))
                         if router else None)

                def row(i):
                    return rows[0:1, i * TL:(i + 1) * TL]

                for ch in range(NT):
                    sl = slice(ch * 512, (ch + 1) * 512)
                    for dt in range(ND):
                        par = (ch * ND + dt) % 2

                        def _(g, ch=ch, dt=dt, par=par, sl=sl):
                            c.V.wait(g, xdep)
                            c.V.wait(g, st.get(f"{tag}_smm{ch * ND + dt - 2}"))
                            st[f"{tag}_sq_{ch}_{dt}"] = c.V.tok(g.tensor_mul(
                                sq[:, par, :], X[:, dt, sl], X[:, dt, sl]))
                        VS(_)

                        def _(g, ch=ch, dt=dt, par=par, sl=sl):
                            c.T.wait(g, st["consts"], xdep)
                            c.T.wait(g, st.get(f"{tag}_evac{ch - 1}"))
                            c.T.wait(g, st[f"{tag}_sq_{ch}_{dt}"])
                            g.matmul(ps_su[:, :], onesf[:, 0:1], X[:, dt, sl],
                                     start=(dt == 0), stop=(dt == ND - 1))
                            mm2 = g.matmul(ps_sq[:, :], onesf[:, 0:1], sq[:, par, :],
                                           start=(dt == 0), stop=(dt == ND - 1))
                            st[f"{tag}_smm{ch * ND + dt}"] = c.T.tok(mm2)
                        TS(_)

                    def _(g, ch=ch, sl=sl):
                        c.V.wait(g, st[f"{tag}_smm{ch * ND + ND - 1}"])
                        g.tensor_scalar(row(0)[:, sl], ps_su[:, :], 1.0 / D,
                                        None, AL.mult)
                        st[f"{tag}_evac{ch}"] = c.V.tok(g.tensor_scalar(
                            row(1)[:, sl], ps_sq[:, :], 1.0 / D, None, AL.mult))
                    VS(_)

                def _(g):
                    c.V.wait(g, st[f"{tag}_evac{NT - 1}"])
                    for ch2 in range(NT):
                        sl2 = slice(ch2 * 512, (ch2 + 1) * 512)
                        t1 = c.V.tok(g.tensor_mul(ps_misc[0:1, :], row(0)[:, sl2],
                                                  row(0)[:, sl2]))
                        c.V.wait(g, t1)
                        t2 = c.V.tok(g.tensor_sub(row(1)[:, sl2], row(1)[:, sl2],
                                                  ps_misc[0:1, :]))
                    st[f"{tag}_var"] = t2
                VS(_)

                def _(g):
                    c.S.wait(g, st[f"{tag}_var"])
                    t = c.S.tok(g.activation(row(1)[:, :], row(1)[:, :], AF.Ln,
                                             bias=consts[0:1, 1:2]))
                    c.S.wait(g, t)
                    st[f"{tag}_lnv"] = c.S.tok(
                        g.activation(row(1)[:, :], row(1)[:, :], AF.Exp, scale=-0.5))
                SS(_)

                def _(g):
                    c.V.wait(g, st[f"{tag}_lnv"])
                    st[f"{tag}_mrs"] = c.V.tok(
                        g.tensor_mul(row(0)[:, :], row(0)[:, :], row(1)[:, :]))
                VS(_)

                if ubt is None:
                    st[f"{tag}_u"] = st[f"{tag}_mrs"]
                    return

                for ch in range(NT):
                    def _(g, ch=ch):
                        c.G.wait(g, st[f"{tag}_mrs"])
                        c.G.wait(g, st.get(f"{tag}_ubc{ch - 1}"))
                        g.dma_start(out=rstdb[:, :],
                                    in_=bass.AP(rows, TL + ch * 512,
                                                [[2 * TL, 1], [0, P], [1, 512]])
                                    ).then_inc(c.Dg.sem, 16)
                        c.Dg.n += 16
                        st[f"{tag}_bc{ch}"] = c.Dg.tok(g.dma_start(
                            out=mrsb[:, :],
                            in_=bass.AP(rows, ch * 512,
                                        [[2 * TL, 1], [0, P], [1, 512]])))
                    GS(_)

                    for dt in range(ND):
                        def _(g, ch=ch, dt=dt):
                            sl = slice(ch * 512, (ch + 1) * 512)
                            par = (ch * ND + dt) % 2
                            upar = dt % 2
                            c.V.wait(g, st[f"{tag}_bc{ch}"])
                            c.V.wait(g, st.get(f"{tag}_rmm{ch * ND + dt - 2}"))
                            t = c.V.tok(g.tensor_mul(sq[:, par, :], X[:, dt, sl],
                                                     rstdb[:, :]))
                            c.V.wait(g, t)
                            tu = c.V.tok(g.tensor_sub(ubt[:, dt, sl], sq[:, par, :],
                                                      mrsb[:, :]))
                            st[f"{tag}_u_{ch}_{dt}"] = tu
                            if router:
                                st[f"{tag}_u32_{ch}_{dt}"] = c.V.tok(g.tensor_sub(
                                    u32c[:, upar, :], sq[:, par, :], mrsb[:, :]))
                            if dt == ND - 1:
                                st[f"{tag}_ubc{ch}"] = st.get(
                                    f"{tag}_u32_{ch}_{dt}", tu)
                        VS(_)

                        if router:
                            def _(g, ch=ch, dt=dt):
                                upar = dt % 2
                                c.T.wait(g, st[f"{tag}_u32_{ch}_{dt}"],
                                         st[f"{router}_wrld"])
                                if dt == 0:
                                    c.T.wait(g, st.get(f"{router}_lgev{ch - 1}"))
                                mm = g.matmul(ps_misc[0:NE, :],
                                              rstate["wrt"][:, dt * NE:(dt + 1) * NE],
                                              u32c[:, upar, :],
                                              start=(dt == 0), stop=(dt == ND - 1))
                                t = c.T.tok(mm)
                                st[f"{tag}_rmm{ch * ND + dt}"] = t
                                if dt == ND - 1:
                                    st[f"{router}_lg{ch}"] = t
                            TS(_)

                    if router:
                        par2 = ch % 2

                        def _(g, ch=ch, par2=par2):
                            c.S.wait(g, st[f"{router}_brld"], st[f"{router}_lg{ch}"])
                            c.S.wait(g, st.get(f"{router}_trev_{ch - 2}_3"))
                            st[f"{router}_lgev{ch}"] = c.S.tok(g.activation(
                                logit[:, par2, :], ps_misc[0:NE, :], AF.Identity,
                                bias=rstate["brt"][:, 0:1]))
                        SS(_)

                        for j in range(4):
                            tt = ch * 4 + j

                            def _(g, ch=ch, par2=par2, j=j, tt=tt):
                                c.T.wait(g, st["idld"], st[f"{router}_lgev{ch}"])
                                c.T.wait(g, st.get(f"{router}_trev_{ch}_{j - 1}"))
                                c.T.wait(g, st.get(f"{router}_trev_{ch - 1}_3"))
                                st[f"{router}_tr_{ch}_{j}"] = c.T.tok(g.transpose(
                                    ps_tr[:, 0:NE], logit[:, par2, j * P:(j + 1) * P],
                                    identf[0:NE, 0:NE]))
                            TS(_)

                            def _(g, ch=ch, j=j, tt=tt):
                                c.V.wait(g, st[f"{router}_tr_{ch}_{j}"])
                                st[f"{router}_trev_{ch}_{j}"] = c.V.tok(g.tensor_copy(
                                    rstate["ltok"][:, tt * NE:(tt + 1) * NE],
                                    ps_tr[:, 0:NE]))
                            VS(_)

                for dt in range(ND):
                    st[f"{tag}_u{dt}"] = st[f"{tag}_u_{NT - 1}_{dt}"]
                st[f"{tag}_u"] = st[f"{tag}_ubc{NT - 1}"]

        def emit_router_post(tag, rstate):
            NTT = TL // P
            ltok, lmax, lden, maskp_t, maskp = (rstate["ltok"], rstate["lmax"],
                                                rstate["lden"], rstate["maskp_t"],
                                                rstate["maskp"])

            def _(g):
                c.V.wait(g, st[f"{tag}_trev_{NT - 1}_3"])
                t = c.V.tok(g.tensor_reduce(
                    lmax.ap().rearrange("p (a b) -> p a b", b=1),
                    ltok.ap().rearrange("p (a b) -> p a b", b=NE), AX.X, AL.max))
                c.V.wait(g, t)
                st[f"{tag}_nmax"] = c.V.tok(
                    g.tensor_scalar(lmax[:, :], lmax[:, :], -1.0, None, AL.mult))
            VS(_)

            def _(g):
                c.S.wait(g, st[f"{tag}_nmax"])
                last = None
                for tt in range(NTT):
                    last = g.activation(
                        maskp_t[:, tt * NE:(tt + 1) * NE],
                        ltok[:, tt * NE:(tt + 1) * NE], AF.Exp,
                        bias=lmax[:, tt:tt + 1], accum_out=lden[:, tt:tt + 1])
                st[f"{tag}_exp"] = c.S.tok(last)
            SS(_)

            def _(g):
                c.V.wait(g, st[f"{tag}_exp"])
                t = c.V.tok(g.reciprocal(lden[:, :], lden[:, :]))
                c.V.wait(g, t)
                for tt in range(NTT):
                    t = c.V.tok(g.tensor_scalar(
                        maskp_t[:, tt * NE:(tt + 1) * NE],
                        ltok[:, tt * NE:(tt + 1) * NE],
                        lmax[:, tt:tt + 1], 0.0, AL.add, AL.is_ge))
                    c.V.wait(g, t)
                    st[f"{tag}_mp{tt}"] = c.V.tok(g.tensor_scalar(
                        maskp_t[:, tt * NE:(tt + 1) * NE],
                        maskp_t[:, tt * NE:(tt + 1) * NE],
                        lden[:, tt:tt + 1], None, AL.mult))
            VS(_)

            for tt in range(NTT):
                def _(g, tt=tt):
                    c.T.wait(g, st[f"{tag}_mp{tt}"])
                    c.T.wait(g, st.get(f"{tag}_mtrev{tt - 1}"))
                    st[f"{tag}_mtr{tt}"] = c.T.tok(g.transpose(
                        ps_tr[0:NE, 0:P], maskp_t[:, tt * NE:(tt + 1) * NE],
                        identf[:, :]))
                TS(_)

                def _(g, tt=tt):
                    c.V.wait(g, st[f"{tag}_mtr{tt}"])
                    st[f"{tag}_mtrev{tt}"] = c.V.tok(g.tensor_copy(
                        maskp[:, tt * P:(tt + 1) * P], ps_tr[0:NE, 0:P]))
                VS(_)
            st[f"{tag}_maskp"] = st[f"{tag}_mtrev{NTT - 1}"]

        def emit_l0_ffn_sublayer(tag, xdep):
            with ExitStack() as ls:
                def sb(n, shape, dtype):
                    return ls.enter_context(nc.sbuf_tensor(f"{tag}_{n}", shape, dtype))

                ubt = sb("ub", [P, ND, TL], BF16)
                wrt = sb("wrt", [P, ND * NE], F32)
                brt = sb("brt", [NE, 1], F32)
                ltok = sb("ltok", [P, TL // P * NE], F32)
                lmax = sb("lmax", [P, TL // P], F32)
                lden = sb("lden", [P, TL // P], F32)
                maskp_t = sb("maskp_t", [P, TL // P * NE], F32)
                maskp = sb("maskp", [NE, TL], BF16)
                mask_bc = sb("mask_bc", [P, TL], BF16)
                b1all = sb("b1all", [P, NE * NH], F32)
                b2sb = sb("b2sb", [NE, D], BF16)
                w1s = [sb(f"w1s{i}", [P, D], BF16) for i in range(3)]
                w2s = [sb(f"w2s{i}", [P, H_FFN], BF16) for i in range(2)]
                hbuf = [sb(f"hbuf{i}", [P, 512], BF16) for i in range(3)]
                hmask = sb("hmask", [P, NH, 512], BF16)
                yebuf = [sb(f"yebuf{i}", [P, 512], F32) for i in range(2)]
                ybk = [sb(f"ybk{i}", [P, 512], F32) for i in range(2)]
                rstate = {"wrt": wrt, "brt": brt, "ltok": ltok, "lmax": lmax,
                          "lden": lden, "maskp_t": maskp_t, "maskp": maskp}
                pw1 = [ps_a[0], ps_a[1], ps_misc]

                def _(g):
                    st[f"{tag}_wrld"] = c.Ds.tok(g.dma_start(
                        out=bass.AP(wrt, 0, [[ND * NE, P], [NE, ND], [1, NE]]),
                        in_=bass.AP(wr, 0, [[NE, P], [P * NE, ND], [1, NE]])))
                    st[f"{tag}_brld"] = c.Ds.tok(g.dma_start(out=brt[:, :],
                                                             in_=brd[:, :]))
                    st[f"{tag}_b1ld"] = c.Ds.tok(g.dma_start(
                        out=bass.AP(b1all, 0, [[NE * NH, P], [NH, NE], [1, NH]]),
                        in_=bass.AP(b1t, 0, [[NH, P], [P * NH, NE], [1, NH]])))
                YS(_)

                def _(g):
                    st[f"{tag}_b2ld"] = c.Dg.tok(g.dma_start(out=b2sb[:, :],
                                                             in_=b2d[:, :]))
                GS(_)

                emit_ln(tag, xdep, ubt, router=tag, rstate=rstate)
                emit_router_post(tag, rstate)

                for e in range(NE):
                    def _(g, e=e):
                        c.G.wait(g, st[f"{tag}_maskp"])
                        c.G.wait(g, st.get(f"{tag}_mfree{e - 1}"))
                        st[f"{tag}_mbc{e}"] = c.Dg.tok(g.dma_start(
                            out=mask_bc[:, :],
                            in_=bass.AP(maskp, e * TL, [[TL, 1], [0, P], [1, TL]])))
                    GS(_)

                    for ch in range(NT):
                        sl = slice(ch * 512, (ch + 1) * 512)
                        for ht in range(NH):
                            ib = (ch * NH + ht) % 3

                            def _(g, e=e, ch=ch, ht=ht, ib=ib):
                                c.Ds.wait(g, st[f"wcast{e}"])
                                c.Ds.wait(g, st.get(f"{tag}_w1fr_{e}_{ch}_{ht - 3}"))
                                st[f"{tag}_w1ld_{e}_{ch}_{ht}"] = c.Ds.tok(g.dma_start(
                                    out=bass.AP(w1s[ib], 0, [[D, P], [P, ND], [1, P]]),
                                    in_=bass.AP(w1bf, (e * NH + ht) * ND * P * P,
                                                [[P, P], [P * P, ND], [1, P]])))
                            YS(_)

                            def _(g, e=e, ch=ch, ht=ht, ib=ib, sl=sl):
                                c.T.wait(g, st[f"{tag}_w1ld_{e}_{ch}_{ht}"])
                                if e == 0 and ch == 0 and ht == 0:
                                    c.T.wait(g, st[f"{tag}_u"],
                                             st[f"{tag}_maskp"])
                                c.T.wait(g, st.get(f"{tag}_hev_{e}_{ch}_{ht - 3}"))
                                for kt in range(ND):
                                    mm = g.matmul(pw1[ib][:, :],
                                                  w1s[ib][:, kt * P:(kt + 1) * P],
                                                  ubt[:, kt, sl], start=(kt == 0),
                                                  stop=(kt == ND - 1))
                                t = c.T.tok(mm)
                                st[f"{tag}_hmm_{e}_{ch}_{ht}"] = t
                                st[f"{tag}_w1fr_{e}_{ch}_{ht}"] = t
                            TS(_)

                            def _(g, e=e, ch=ch, ht=ht, ib=ib, sl=sl):
                                c.V.wait(g, st[f"{tag}_b1ld"],
                                         st[f"{tag}_hmm_{e}_{ch}_{ht}"],
                                         st[f"{tag}_mbc{e}"])
                                t = c.V.tok(g.tensor_scalar(
                                    hbuf[ib][:, :], pw1[ib][:, :],
                                    b1all[:, e * NH + ht:e * NH + ht + 1], 0.0,
                                    AL.add, AL.max))
                                st[f"{tag}_hev_{e}_{ch}_{ht}"] = t
                                c.V.wait(g, t)
                                st[f"{tag}_hmsk_{e}_{ch}_{ht}"] = c.V.tok(g.tensor_mul(
                                    hmask[:, ht, :], hbuf[ib][:, :], mask_bc[:, sl]))
                            VS(_)

                        for dt in range(ND):
                            ib = (ch * ND + dt) % 2

                            def _(g, e=e, ch=ch, dt=dt, ib=ib):
                                c.Ds.wait(g, st[f"wcast{e}"])
                                c.Ds.wait(g, st.get(f"{tag}_w2fr_{e}_{ch}_{dt - 2}"))
                                st[f"{tag}_w2ld_{e}_{ch}_{dt}"] = c.Ds.tok(g.dma_start(
                                    out=bass.AP(w2s[ib], 0,
                                                [[H_FFN, P], [P, NH], [1, P]]),
                                    in_=bass.AP(w2bf, (e * ND + dt) * NH * P * P,
                                                [[P, P], [P * P, NH], [1, P]])))
                            YS(_)

                            def _(g, e=e, ch=ch, dt=dt, ib=ib):
                                c.T.wait(g, st[f"{tag}_w2ld_{e}_{ch}_{dt}"])
                                c.T.wait(g, st[f"{tag}_hmsk_{e}_{ch}_{NH - 1}"])
                                c.T.wait(g, st.get(f"{tag}_yev_{e}_{ch}_{dt - 2}"))
                                for kt in range(NH):
                                    mm = g.matmul(ps_b[ib][:, :],
                                                  w2s[ib][:, kt * P:(kt + 1) * P],
                                                  hmask[:, kt, :], start=(kt == 0),
                                                  stop=(kt == NH - 1))
                                t = c.T.tok(mm)
                                st[f"{tag}_ymm_{e}_{ch}_{dt}"] = t
                                st[f"{tag}_w2fr_{e}_{ch}_{dt}"] = t
                            TS(_)

                            def _(g, e=e, ch=ch, dt=dt, ib=ib):
                                c.V.wait(g, st[f"{tag}_ymm_{e}_{ch}_{dt}"])
                                c.V.wait(g, st.get(f"{tag}_yacc_{e}_{ch}_{dt - 2}"))
                                st[f"{tag}_yev_{e}_{ch}_{dt}"] = c.V.tok(g.tensor_copy(
                                    yebuf[ib][:, :], ps_b[ib][:, :]))
                            VS(_)

                            def _(g, e=e, ch=ch, dt=dt, ib=ib):
                                c.G.wait(g, st[f"{tag}_yev_{e}_{ch}_{dt}"])
                                c.G.wait(g, st.get(f"{tag}_yacc_{e - 1}_{ch}_{dt}"))
                                kw = {} if e == 0 else {"accum_op": AL.add}
                                st[f"{tag}_yacc_{e}_{ch}_{dt}"] = c.Dg.tok(g.dma_start(
                                    out=bass.AP(yd, dt * P * TL + ch * 512,
                                                [[TL, P], [1, 512]]),
                                    in_=yebuf[ib][:, :], **kw))
                            GS(_)

                    st[f"{tag}_mfree{e}"] = st[f"{tag}_hmsk_{e}_{NT - 1}_{NH - 1}"]

                for dt in range(ND):
                    for ch in range(NT):
                        sl = slice(ch * 512, (ch + 1) * 512)
                        kb = (dt * NT + ch) % 2

                        def _(g, dt=dt, ch=ch, kb=kb):
                            c.Ds.wait(g, st[f"{tag}_yacc_{NE - 1}_{ch}_{dt}"])
                            gi = dt * NT + ch - 2
                            c.Ds.wait(g, st.get(f"{tag}_xup_{gi // NT}_{gi % NT}"))
                            st[f"{tag}_ybk_{dt}_{ch}"] = c.Ds.tok(g.dma_start(
                                out=ybk[kb][:, :],
                                in_=bass.AP(yd, dt * P * TL + ch * 512,
                                            [[TL, P], [1, 512]])))
                        YS(_)

                        def _(g, dt=dt, ch=ch):
                            c.T.wait(g, st[f"{tag}_b2ld"], st[f"{tag}_maskp"])
                            c.T.wait(g, st.get(f"{tag}_xup_{dt}_{ch - 1}"))
                            c.T.wait(g, st.get(f"{tag}_xup_{dt - 1}_{NT - 1}"))
                            st[f"{tag}_b2mm_{dt}_{ch}"] = c.T.tok(g.matmul(
                                ps_misc[:, :], b2sb[:, dt * P:(dt + 1) * P],
                                maskp[:, ch * 512:(ch + 1) * 512],
                                start=True, stop=True))
                        TS(_)

                        def _(g, dt=dt, ch=ch, sl=sl, kb=kb):
                            c.V.wait(g, st[f"{tag}_b2mm_{dt}_{ch}"])
                            c.V.wait(g, st[f"{tag}_ybk_{dt}_{ch}"])
                            t = c.V.tok(g.scalar_tensor_tensor(
                                X[:, dt, sl], ps_misc[:, :], 0.5, X[:, dt, sl],
                                AL.mult, AL.add))
                            c.V.wait(g, t)
                            st[f"{tag}_xup_{dt}_{ch}"] = c.V.tok(g.scalar_tensor_tensor(
                                X[:, dt, sl], ybk[kb][:, :], 0.5, X[:, dt, sl],
                                AL.mult, AL.add))
                        VS(_)
                st[f"{tag}_done"] = st[f"{tag}_xup_{ND - 1}_{NT - 1}"]
            fence()
            return st[f"{tag}_done"]

        def emit_mamba(tag, xdep):
            with ExitStack() as ls:
                def sb(n, shape, dtype):
                    return ls.enter_context(nc.sbuf_tensor(f"{tag}_{n}", shape, dtype))


                Brow = sb("Brow", [NS, TL], BF16)
                Crow = sb("Crow", [NS, TL], BF16)
                rstdbF = sb("rstdbF", [P, TL], F32)
                mrsbF = sb("mrsbF", [P, TL], F32)
                Acol = sb("Acol", [P, ND * NS], F32)
                bdtc = sb("bdtc", [P, ND], F32)
                Dcol = sb("Dcol", [P, ND], F32)
                smat = sb("smat", [P, P], F32)
                sini = sb("sini", [P, P], F32)
                wbct = sb("wbct", [P, ND * 2 * NS], BF16)

                def _(g):
                    st[f"{tag}_cl1"] = c.Ds.tok(g.dma_start(
                        out=bass.AP(Acol, 0, [[ND * NS, P], [NS, ND], [1, NS]]),
                        in_=bass.AP(loga_d, 0, [[NS, P], [P * NS, ND], [1, NS]])))
                    st[f"{tag}_cl2"] = c.Ds.tok(g.dma_start(out=bdtc[:, :],
                                                            in_=bdt_d[:, :]))
                    st[f"{tag}_cl3"] = c.Ds.tok(g.dma_start(out=Dcol[:, :],
                                                            in_=dpar_d[:, :]))
                YS(_)

                def _(g):
                    t = c.Dg.tok(g.dma_start(
                        out=bass.AP(wbct, 0, [[ND * 2 * NS, P], [2 * NS, ND], [1, NS]]),
                        in_=bass.AP(wb_t, 0, [[NS, P], [P * NS, ND], [1, NS]])))
                    st[f"{tag}_wbld"] = t
                    t = c.Dg.tok(g.dma_start(
                        out=bass.AP(wbct, NS, [[ND * 2 * NS, P], [2 * NS, ND], [1, NS]]),
                        in_=bass.AP(wc_t, 0, [[NS, P], [P * NS, ND], [1, NS]])))
                    st[f"{tag}_wcld"] = t
                GS(_)

                def _(g):
                    c.S.wait(g, st[f"{tag}_cl1"])
                    st[f"{tag}_Ae"] = c.S.tok(g.activation(Acol[:, :], Acol[:, :],
                                                           AF.Exp))
                SS(_)

                def _(g):
                    c.V.wait(g, st[f"{tag}_Ae"])
                    st[f"{tag}_Aneg"] = c.V.tok(g.tensor_scalar(
                        Acol[:, :], Acol[:, :], -1.0, None, AL.mult))
                VS(_)

                with ExitStack() as ls2:
                    ubt = ls2.enter_context(
                        nc.sbuf_tensor(f"{tag}_ub", [P, ND, TL], BF16))
                    wdts = [ls2.enter_context(
                        nc.sbuf_tensor(f"{tag}_wdts{i}", [P, D], BF16))
                        for i in range(2)]
                    spt = [ls2.enter_context(
                        nc.sbuf_tensor(f"{tag}_spt{i}", [P, 512], F32))
                        for i in range(2)]
                    dt32 = ls2.enter_context(
                        nc.sbuf_tensor(f"{tag}_dt32", [P, 2, 512], F32))

                    emit_ln(tag, xdep, ubt)

                    def _(g):
                        c.G.wait(g, st[f"{tag}_mrs"])
                        g.dma_start(out=rstdbF[:, :],
                                    in_=bass.AP(rows, TL,
                                                [[2 * TL, 1], [0, P], [1, TL]])
                                    ).then_inc(c.Dg.sem, 16)
                        c.Dg.n += 16
                        st[f"{tag}_bcF"] = c.Dg.tok(g.dma_start(
                            out=mrsbF[:, :],
                            in_=bass.AP(rows, 0, [[2 * TL, 1], [0, P], [1, TL]])))
                    GS(_)

                    for et in range(ND):
                        ib = et % 2

                        def _(g, et=et, ib=ib):
                            c.Ds.wait(g, st["wdtcast"])
                            c.Ds.wait(g, st.get(f"{tag}_dtfr{et - 2}"))
                            st[f"{tag}_wdtld{et}"] = c.Ds.tok(g.dma_start(
                                out=bass.AP(wdts[ib], 0, [[D, P], [P, ND], [1, P]]),
                                in_=bass.AP(wdtbf, et * ND * P * P,
                                            [[P, P], [P * P, ND], [1, P]])))
                        YS(_)

                        for ch in range(NT):
                            pp = (et * NT + ch) % 2

                            def _(g, et=et, ch=ch, ib=ib, pp=pp):
                                sl = slice(ch * 512, (ch + 1) * 512)
                                c.T.wait(g, st[f"{tag}_wdtld{et}"], st[f"{tag}_u"])
                                c.T.wait(g, st.get(f"{tag}_spdone{et * NT + ch - 2}"))
                                for kt in range(ND):
                                    mm = g.matmul(ps_a[pp][:, :],
                                                  wdts[ib][:, kt * P:(kt + 1) * P],
                                                  ubt[:, kt, sl], start=(kt == 0),
                                                  stop=(kt == ND - 1))
                                t = c.T.tok(mm)
                                st[f"{tag}_dtmm_{et}_{ch}"] = t
                                if ch == NT - 1:
                                    st[f"{tag}_dtfr{et}"] = t
                            TS(_)

                            def _(g, et=et, ch=ch, pp=pp):
                                bia = bdtc[:, et:et + 1]
                                c.S.wait(g, st[f"{tag}_cl2"],
                                         st[f"{tag}_dtmm_{et}_{ch}"])
                                c.S.wait(g, st.get(f"{tag}_spv{et * NT + ch - 1}"))
                                t = c.S.tok(g.activation(spt[0][:, :], ps_a[pp][:, :],
                                                         AF.Abs, bias=bia))
                                t2 = c.S.tok(g.activation(spt[1][:, :], ps_a[pp][:, :],
                                                          AF.Relu, bias=bia))
                                c.S.wait(g, t, t2)
                                t = c.S.tok(g.activation(spt[0][:, :], spt[0][:, :],
                                                         AF.Exp, scale=-1.0))
                                c.S.wait(g, t)
                                t = c.S.tok(g.activation(spt[0][:, :], spt[0][:, :],
                                                         AF.Ln, bias=consts[:, 0:1]))
                                st[f"{tag}_spln_{et}_{ch}"] = t
                                st[f"{tag}_spdone{et * NT + ch}"] = t
                            SS(_)

                            def _(g, et=et, ch=ch):
                                kq = (et * NT + ch) % 2
                                c.V.wait(g, st[f"{tag}_spln_{et}_{ch}"])
                                c.V.wait(g, st.get(f"{tag}_dtw{et * NT + ch - 2}"))
                                st[f"{tag}_spv{et * NT + ch}"] = c.V.tok(g.tensor_add(
                                    dt32[:, kq, :], spt[0][:, :], spt[1][:, :]))
                            VS(_)

                            def _(g, et=et, ch=ch):
                                kq = (et * NT + ch) % 2
                                c.G.wait(g, st[f"{tag}_spv{et * NT + ch}"])
                                st[f"{tag}_dtw{et * NT + ch}"] = c.Dg.tok(g.dma_start(
                                    out=bass.AP(dtd, et * P * TL + ch * 512,
                                                [[TL, P], [1, 512]]),
                                    in_=dt32[:, kq, :]))
                            GS(_)

                    for ch in range(NT):
                        sl = slice(ch * 512, (ch + 1) * 512)

                        def _(g, ch=ch, sl=sl):
                            c.T.wait(g, st[f"{tag}_wbld"], st[f"{tag}_wcld"],
                                     st[f"{tag}_u"])
                            c.T.wait(g, st.get(f"{tag}_bcev{ch - 1}"))
                            for kt in range(ND):
                                g.matmul(ps_b[0][0:NS, :],
                                         wbct[:, (kt * 2) * NS:(kt * 2 + 1) * NS],
                                         ubt[:, kt, sl], start=(kt == 0),
                                         stop=(kt == ND - 1))
                                mm = g.matmul(
                                    ps_b[1][0:NS, :],
                                    wbct[:, (kt * 2 + 1) * NS:(kt * 2 + 2) * NS],
                                    ubt[:, kt, sl], start=(kt == 0),
                                    stop=(kt == ND - 1))
                            st[f"{tag}_bcmm{ch}"] = c.T.tok(mm)
                        TS(_)

                        def _(g, ch=ch, sl=sl):
                            c.V.wait(g, st[f"{tag}_bcmm{ch}"])
                            g.tensor_copy(Brow[:, sl], ps_b[0][0:NS, :])
                            st[f"{tag}_bcev{ch}"] = c.V.tok(g.tensor_copy(
                                Crow[:, sl], ps_b[1][0:NS, :]))
                        VS(_)


                fence()

                with ExitStack() as ls3:
                    def sb3(n, shape, dtype):
                        return ls3.enter_context(
                            nc.sbuf_tensor(f"{tag}_{n}", shape, dtype))

                    Ball = sb3("Ball", [P, NS, TH], BF16)
                    dtch = [sb3(f"dtch{i}", [P, TH], F32) for i in range(2)]
                    Call = sb3("Call", [P, NS, TH], BF16)
                    u32d = sb3("u32d", [P, TH], F32)
                    dtub = sb3("dtub", [P, TH], F32)
                    dAb = sb3("dAb", [P, 2, TH], F32)
                    dBu = sb3("dBu", [P, 2, TH], BF16)
                    xsc = sb3("xsc", [P, 2, TH], F32)
                    ya = sb3("ya", [P, TH], F32)
                    ym = sb3("ym", [P, 2, TH], BF16)

                    def scan_pass(ph):
                        for th in range(NTH):
                            tsl = slice(th * TH, (th + 1) * TH)

                            def _(g, ph=ph, th=th):
                                c.G.wait(g, st[f"{tag}_bcev{NT - 1}"])
                                c.G.wait(g, st.get(f"{tag}_p{ph}bfree_{th - 1}"))
                                if ph == 3:
                                    c.G.wait(g, st.get(f"{tag}_p1bfree_{NTH - 1}"))
                                for n in range(NS):
                                    g.dma_start(
                                        out=Ball[:, n, :],
                                        in_=bass.AP(Brow, n * TL + th * TH,
                                                    [[TL, 1], [0, P], [1, TH]])
                                    ).then_inc(c.Dg.sem, 16)
                                    c.Dg.n += 16
                                    if ph == 3:
                                        g.dma_start(
                                            out=Call[:, n, :],
                                            in_=bass.AP(Crow, n * TL + th * TH,
                                                        [[TL, 1], [0, P], [1, TH]])
                                        ).then_inc(c.Dg.sem, 16)
                                        c.Dg.n += 16
                                st[f"{tag}_p{ph}bc_{th}"] = (c.Dg.sem, c.Dg.n)
                            GS(_)

                            for et in range(ND):
                                def _(g, ph=ph, th=th, et=et, tsl=tsl):
                                    de = (th * ND + et) % 2
                                    c.Ds.wait(g, st[f"{tag}_dtw{et * NT + NT - 1}"])
                                    c.Ds.wait(g, st.get(f"{tag}_p{ph}ufree_g{(th * ND + et) - 2}"))
                                    st[f"{tag}_p{ph}dtld_{th}_{et}"] = c.Ds.tok(
                                        g.dma_start(out=dtch[de][:, :],
                                                    in_=bass.AP(dtd,
                                                                et * P * TL + th * TH,
                                                                [[TL, P], [1, TH]])))
                                YS(_)

                                def _(g, ph=ph, th=th, et=et, tsl=tsl):
                                    de = (th * ND + et) % 2
                                    c.V.wait(g, st[f"{tag}_bcF"])
                                    c.V.wait(g, st[f"{tag}_p{ph}dtld_{th}_{et}"])
                                    c.V.wait(g, st.get(f"{tag}_p{ph}ufree_g{(th * ND + et) - 1}"))
                                    t = c.V.tok(g.tensor_mul(u32d[:, :],
                                                             X[:, et, tsl],
                                                             rstdbF[:, tsl]))
                                    c.V.wait(g, t)
                                    t = c.V.tok(g.tensor_sub(u32d[:, :], u32d[:, :],
                                                             mrsbF[:, tsl]))
                                    c.V.wait(g, t)
                                    st[f"{tag}_p{ph}dtu_{th}_{et}"] = c.V.tok(
                                        g.tensor_mul(dtub[:, :], dtch[de][:, :],
                                                     u32d[:, :]))
                                VS(_)

                                vstep = 2 if ph == 1 else 4
                                voff = 2 if ph == 1 else 3
                                v_base = c.V.n

                                def _(g, ph=ph, th=th, et=et, tsl=tsl,
                                      vstep=vstep, voff=voff, v_base=v_base):
                                    de = (th * ND + et) % 2
                                    c.S.wait(g, st[f"{tag}_Aneg"])
                                    c.S.wait(g, st[f"{tag}_p{ph}dtld_{th}_{et}"])
                                    for n in range(NS):
                                        k = et * NS + n
                                        gk = th * ND * NS + k
                                        pr = k % 2
                                        if n < 2:
                                            c.S.wait(g, st.get(
                                                f"{tag}_p{ph}scn_g{gk - 2}"))
                                        else:
                                            c.S.wait(g, (c.V.sem,
                                                         v_base + vstep * (n - 2)
                                                         + voff))
                                        st[f"{tag}_p{ph}dA_g{gk}"] = c.S.tok(
                                            g.activation(
                                                dAb[:, pr, :], dtch[de][:, :],
                                                AF.Exp,
                                                scale=Acol[:, k:k + 1]))
                                SS(_)

                                if ph == 1:
                                    def _(g, th=th, et=et, v_base=v_base):
                                        c.G.wait(g,
                                                 st[f"{tag}_p1dtu_{th}_{et}"])
                                        c.G.wait(g, st[f"{tag}_p1bc_{th}"])
                                        for n in range(NS):
                                            k = et * NS + n
                                            gk = th * ND * NS + k
                                            pr = k % 2
                                            if n >= 2:
                                                c.G.wait(g, (c.V.sem,
                                                             v_base
                                                             + 2 * (n - 2)
                                                             + 1))
                                            else:
                                                c.G.wait(g, st.get(
                                                    f"{tag}_p1scn_g{gk - 2}"))
                                            t = c.G.tok(g.tensor_mul(
                                                dBu[:, pr, :], dtub[:, :],
                                                Ball[:, n, :]))
                                            st[f"{tag}_p1dbu_g{gk}"] = t
                                    GS(_)

                                g_base = c.G.n

                                def _(g, ph=ph, th=th, et=et, tsl=tsl,
                                      g_base=g_base):
                                    c.V.wait(g, st[f"{tag}_p{ph}dtu_{th}_{et}"])
                                    c.V.wait(g, st[f"{tag}_p{ph}bc_{th}"])
                                    c.V.wait(g, st.get(
                                        f"{tag}_p{ph}scn_g{th * ND * NS + et * NS - 2}"))
                                    if ph == 3 and th == 0:
                                        c.V.wait(g, st[f"{tag}_sini"])
                                    for n in range(NS):
                                        k = et * NS + n
                                        gk = th * ND * NS + k
                                        pr = k % 2
                                        c.V.wait(g, st[f"{tag}_p{ph}dA_g{gk}"])
                                        if ph == 3:
                                            if n < 2:
                                                c.V.wait(g, st.get(
                                                    f"{tag}_p3ya_g{gk - 2}"))
                                            else:
                                                c.V.wait(g, (c.G.sem,
                                                             g_base + n - 1))
                                            t = c.V.tok(g.tensor_mul(
                                                dBu[:, pr, :], dtub[:, :],
                                                Ball[:, n, :]))
                                            c.V.wait(g, t)
                                        else:
                                            c.V.wait(g,
                                                     st[f"{tag}_p1dbu_g{gk}"])
                                        if ph == 1:
                                            init = (0.0 if th == 0
                                                    else smat[:, k:k + 1])
                                        else:
                                            init = (sini if th == 0
                                                    else smat)[:, k:k + 1]
                                        t = c.V.tok(g.tensor_tensor_scan(
                                            xsc[:, pr, :], dAb[:, pr, :],
                                            dBu[:, pr, :], init, AL.mult,
                                            AL.add))
                                        c.V.wait(g, t)
                                        t = c.V.tok(g.tensor_copy(
                                            smat[:, k:k + 1],
                                            xsc[:, pr, TH - 1:TH]))
                                        st[f"{tag}_p{ph}scn_g{gk}"] = t
                                        if ph == 3:
                                            st[f"{tag}_p3ym_g{gk}"] = c.V.tok(
                                                g.tensor_mul(ym[:, pr, :],
                                                             xsc[:, pr, :],
                                                             Call[:, n, :]))
                                VS(_)

                                if ph == 3:
                                    def _(g, th=th, et=et):
                                        c.G.wait(g, st.get(
                                            f"{tag}_p3x_g{th * ND + et - 1}"))
                                        for n in range(NS):
                                            k = et * NS + n
                                            gk = th * ND * NS + k
                                            pr = k % 2
                                            c.G.wait(g,
                                                     st[f"{tag}_p3ym_g{gk}"])
                                            if n == 0:
                                                t = c.G.tok(g.tensor_copy(
                                                    ya[:, :], ym[:, pr, :]))
                                            else:
                                                t = c.G.tok(g.tensor_add(
                                                    ya[:, :], ya[:, :],
                                                    ym[:, pr, :]))
                                            c.G.wait(g, t)
                                            st[f"{tag}_p3ya_g{gk}"] = t
                                    GS(_)

                                if ph == 3:
                                    def _(g, th=th, et=et, tsl=tsl):
                                        c.V.wait(g,
                                                 st[f"{tag}_p3ya_g{th * ND * NS + et * NS + NS - 1}"])
                                        t = c.V.tok(g.tensor_add(
                                            X[:, et, tsl], X[:, et, tsl], ya[:, :]))
                                        c.V.wait(g, t, st[f"{tag}_cl3"])
                                        t = c.V.tok(g.scalar_tensor_tensor(
                                            X[:, et, tsl], u32d[:, :],
                                            Dcol[:, et:et + 1], X[:, et, tsl],
                                            AL.mult, AL.add))
                                        st[f"{tag}_p3x_g{th * ND + et}"] = t
                                        st[f"{tag}_p3ufree_g{th * ND + et}"] = t
                                    VS(_)
                                else:
                                    st[f"{tag}_p1ufree_g{th * ND + et}"] = \
                                        st[f"{tag}_p1scn_g{th * ND * NS + et * NS + NS - 1}"]

                            st[f"{tag}_p{ph}bfree_{th}"] = \
                                st[f"{tag}_p{ph}scn_g{th * ND * NS + ND * NS - 1}"]

                    scan_pass(1)

                    def _(g):
                        c.G.wait(g, st[f"{tag}_p1scn_g{NTH * ND * NS - 1}"])
                        g.dma_start(out=ag_s_in[:, :], in_=smat[:, :]).then_inc(
                            c.Dg.sem, 16)
                        c.Dg.n += 16
                        c.G.wait(g, (c.Dg.sem, c.Dg.n))
                        cc = g.collective_compute(
                            "AllGather", AL.bypass,
                            replica_groups=[[0, 1], [2, 3], [4, 5], [6, 7]],
                            ins=[ag_s_in.ap().opt()], outs=[ag_s_out.ap().opt()])
                        st[f"{tag}_ag"] = c.CC.tok(cc)
                        c.G.wait(g, st[f"{tag}_ag"])
                        st[f"{tag}_agld"] = c.Dg.tok(g.dma_start(
                            out=sini[:, :], in_=ag_s_out[0:P, :]))
                    GS(_)

                    def _(g):
                        c.V.wait(g, st[f"{tag}_agld"], st["hmld"])
                        if DBG_ZERO_SINI:
                            st[f"{tag}_sini"] = c.V.tok(g.memset(sini[:, :], 0.0))
                        else:
                            st[f"{tag}_sini"] = c.V.tok(g.tensor_scalar(
                                sini[:, :], sini[:, :], halfm[:, 0:1], None, AL.mult))
                    VS(_)

                    scan_pass(3)

                st[f"{tag}_done"] = st[f"{tag}_p3x_g{NTH * ND - 1}"]
            fence()
            return st[f"{tag}_done"]


        NI = I_FFN // P
        NOC = I_FFN // 512
        NTT = TL // P

        def emit_l1_ffn(tag, xdep, first_call):
            with ExitStack() as ls:
                def sb(n, shape, dtype):
                    return ls.enter_context(nc.sbuf_tensor(f"{tag}_{n}", shape, dtype))

                arsb = sb("arsb", [1, 64], F32)
                arc = sb("arc", [P, 8], F32)
                colt = sb("colt", [P, 4], F32)
                amx = sb("amx", [P, 4], F32)

                def fold_max_col(src, col, dep, key):
                    t = dep
                    for k in [64, 32, 16, 8, 4, 2, 1]:
                        def _(g, k=k, t=t, src=src, col=col, key=key):
                            c.G.wait(g, t)
                            st[f"{tag}_fg_{key}{k}"] = c.Dg.tok(g.dma_start(
                                out=colt[0:k, 0:1], in_=src[k:2 * k, col:col + 1]))
                        GS(_)

                        def _(g, k=k, src=src, col=col, key=key):
                            c.V.wait(g, st[f"{tag}_fg_{key}{k}"])
                            st[f"{tag}_fv_{key}{k}"] = c.V.tok(g.tensor_max(
                                src[0:k, col:col + 1], src[0:k, col:col + 1],
                                colt[0:k, 0:1]))
                        VS(_)
                        t = st[f"{tag}_fv_{key}{k}"]
                    return t

                def allreduce_lanes(key, nlanes, dep):
                    def _(g, key=key, dep=dep, nlanes=nlanes):
                        c.G.wait(g, dep)
                        g.dma_start(out=ar_in.ap(), in_=arsb[0:1, :]).then_inc(
                            c.Dg.sem, 16)
                        c.Dg.n += 16
                        c.G.wait(g, (c.Dg.sem, c.Dg.n))
                        cc = g.collective_compute(
                            "AllReduce", AL.max,
                            replica_groups=[list(range(NCORES))],
                            ins=[ar_in.ap().opt()], outs=[ar_out.ap().opt()])
                        st[f"{tag}_cc_{key}"] = c.CC.tok(cc)
                        c.G.wait(g, st[f"{tag}_cc_{key}"])
                        last = None
                        for i in range(nlanes):
                            last = g.dma_start(
                                out=colt[:, i:i + 1],
                                in_=bass.AP(ar_out, i, [[64, 1], [0, P], [1, 1]]))
                            last.then_inc(c.Dg.sem, 16)
                            c.Dg.n += 16
                        st[f"{tag}_arl_{key}"] = (c.Dg.sem, c.Dg.n)
                    GS(_)
                    return st[f"{tag}_arl_{key}"]

                if first_call:
                    with ExitStack() as dq:
                        wfp = dq.enter_context(
                            nc.sbuf_tensor(f"{tag}_wfp", [P, I_FFN], F32))
                        wrn = dq.enter_context(
                            nc.sbuf_tensor(f"{tag}_wrn", [P, I_FFN], F32))
                        bm = dq.enter_context(
                            nc.sbuf_tensor(f"{tag}_bm", [P, 2, I_FFN // 32], F32))
                        exps = dq.enter_context(
                            nc.sbuf_tensor(f"{tag}_exps", [P, 30 * (I_FFN // 32)], F32))
                        exps2 = dq.enter_context(
                            nc.sbuf_tensor(f"{tag}_exps2", [P, 2, I_FFN], F32))
                        wqb = dq.enter_context(
                            nc.sbuf_tensor(f"{tag}_wqb", [P, I_FFN], BF16))

                        def expand(dsti, srccol, nb, dep, key):
                            EW = 30 * (I_FFN // 32)
                            plan = []
                            cur = ("bm", srccol * (I_FFN // 32), 1, 2 * (I_FFN // 32))
                            offs = [0, 2 * nb, 6 * nb, 14 * nb]
                            t = dep
                            for soff in offs:
                                run = cur[2]
                                for half in range(2):
                                    def _(g, soff=soff, half=half, cur=cur, run=run,
                                          nb=nb, t=t, key=key):
                                        srcn, srcoff, srcrun, srcw = cur
                                        srct = bm if srcn == "bm" else exps
                                        c.V.wait(g, t)
                                        st[f"{tag}_xp_{key}"] = c.V.tok(g.tensor_copy(
                                            bass.AP(exps, soff + half * run,
                                                    [[EW, P], [2 * run, nb], [1, run]]),
                                            bass.AP(srct, srcoff,
                                                    [[srcw, P], [srcrun, nb],
                                                     [1, srcrun]])))
                                    VS(_)
                                    t = st[f"{tag}_xp_{key}"]
                                cur = ("exps", soff, run * 2, EW)
                            run = cur[2]
                            for half in range(2):
                                def _(g, half=half, cur=cur, run=run, nb=nb, t=t,
                                      key=key, dsti=dsti):
                                    c.V.wait(g, t)
                                    st[f"{tag}_xp_{key}"] = c.V.tok(g.tensor_copy(
                                        bass.AP(exps2, dsti * I_FFN + half * run,
                                                [[2 * I_FFN, P], [2 * run, nb],
                                                 [1, run]]),
                                        bass.AP(exps, cur[1],
                                                [[EW, P], [run, nb], [1, run]])))
                                VS(_)
                                t = st[f"{tag}_xp_{key}"]
                            return t

                        mats = [(wg_d, wqg, NI, D), (wu_d, wqu, NI, D),
                                (wd_d, wqd, ND, I_FFN)]
                        for mi, (wsrc, wdst, ntile, wid) in enumerate(mats):
                            nb = wid // 32
                            for ti in range(ntile):
                                key = f"{mi}_{ti}"

                                def _(g, wsrc=wsrc, ti=ti, wid=wid, key=key):
                                    c.Ds.wait(g, st.get(f"{tag}_dq_wfree"))
                                    st[f"{tag}_wfld_{key}"] = c.Ds.tok(g.dma_start(
                                        out=wfp[:, 0:wid],
                                        in_=bass.AP(wsrc, ti * P * wid,
                                                    [[wid, P], [1, wid]])))
                                YS(_)

                                def _(g, wid=wid, nb=nb, key=key):
                                    c.V.wait(g, st[f"{tag}_wfld_{key}"])
                                    c.V.wait(g, st.get(f"{tag}_dq_bmfree"))
                                    t = c.V.tok(g.tensor_reduce(
                                        bm[:, 0, 0:nb].rearrange("p (a b) -> p a b", b=1),
                                        wfp[:, 0:wid].rearrange("p (a b) -> p a b", b=32),
                                        AX.X, AL.max, apply_absolute_value=True))
                                    c.V.wait(g, t)
                                    t = c.V.tok(g.tensor_scalar(
                                        bm[:, 0, 0:nb], bm[:, 0, 0:nb], 1.0 / 7.0,
                                        1e-12, AL.mult, AL.max))
                                    c.V.wait(g, t)
                                    st[f"{tag}_rsb_{key}"] = c.V.tok(g.reciprocal(
                                        bm[:, 1, 0:nb], bm[:, 0, 0:nb]))
                                VS(_)

                                tA = expand(0, 1, nb, st[f"{tag}_rsb_{key}"], key)
                                tB = expand(1, 0, nb, tA, key)

                                def _(g, wid=wid, key=key, tB=tB):
                                    c.V.wait(g, tB)
                                    c.V.wait(g, st.get(f"{tag}_dq_wqfree"))
                                    t = c.V.tok(g.tensor_mul(
                                        wrn[:, 0:wid], wfp[:, 0:wid],
                                        exps2[:, 0, 0:wid]))
                                    st[f"{tag}_dq_wfree"] = t
                                    c.V.wait(g, t)
                                    t = c.V.tok(g.tensor_scalar(
                                        wrn[:, 0:wid], wrn[:, 0:wid], MAGIC, MAGIC,
                                        AL.add, AL.subtract))
                                    c.V.wait(g, t)
                                    t = c.V.tok(g.tensor_mul(
                                        wqb[:, 0:wid], wrn[:, 0:wid],
                                        exps2[:, 1, 0:wid]))
                                    st[f"{tag}_wq_{key}"] = t
                                    st[f"{tag}_dq_bmfree"] = t
                                VS(_)

                                def _(g, wdst=wdst, ti=ti, wid=wid, key=key):
                                    c.G.wait(g, st[f"{tag}_wq_{key}"])
                                    st[f"{tag}_dq_wqfree"] = c.Dg.tok(g.dma_start(
                                        out=bass.AP(wdst, ti * P * wid,
                                                    [[wid, P], [1, wid]]),
                                        in_=wqb[:, 0:wid]))
                                GS(_)
                    fence()

                emit_ln(tag, xdep, None)
                rstdc = sb("rstdc", [P, 512], F32)
                mrsc = sb("mrsc", [P, 512], F32)
                ucc = sb("ucc", [P, 2, 512], F32)
                xqb = sb("xqb", [P, ND, TL], BF16)

                def bc_stats(ch, dep, key):
                    def _(g, ch=ch, dep=dep, key=key):
                        c.G.wait(g, dep)
                        g.dma_start(out=rstdc[:, :],
                                    in_=bass.AP(rows, TL + ch * 512,
                                                [[2 * TL, 1], [0, P], [1, 512]])
                                    ).then_inc(c.Dg.sem, 16)
                        c.Dg.n += 16
                        st[f"{tag}_bcs_{key}"] = c.Dg.tok(g.dma_start(
                            out=mrsc[:, :],
                            in_=bass.AP(rows, ch * 512,
                                        [[2 * TL, 1], [0, P], [1, 512]])))
                    GS(_)
                    return st[f"{tag}_bcs_{key}"]

                def _(g):
                    st[f"{tag}_am0"] = c.V.tok(g.memset(amx[:, 0:1], 0.0))
                VS(_)

                for ch in range(NT):
                    dep = st[f"{tag}_u"] if ch == 0 else st[f"{tag}_amx{ch - 1}"]
                    bdep = bc_stats(ch, dep, f"a{ch}")
                    for dt in range(ND):
                        def _(g, ch=ch, dt=dt, bdep=bdep):
                            sl = slice(ch * 512, (ch + 1) * 512)
                            pq = dt % 2
                            c.V.wait(g, bdep, st[f"{tag}_am0"])
                            t = c.V.tok(g.tensor_mul(ucc[:, pq, :], X[:, dt, sl],
                                                     rstdc[:, :]))
                            c.V.wait(g, t)
                            t = c.V.tok(g.tensor_sub(ucc[:, pq, :], ucc[:, pq, :],
                                                     mrsc[:, :]))
                            c.V.wait(g, t)
                            t = c.V.tok(g.tensor_reduce(
                                amx[:, 1:2], ucc[:, pq, :], AX.X, AL.max,
                                apply_absolute_value=True))
                            c.V.wait(g, t)
                            t = c.V.tok(g.tensor_max(amx[:, 0:1], amx[:, 0:1],
                                                     amx[:, 1:2]))
                            if dt == ND - 1:
                                st[f"{tag}_amx{ch}"] = t
                        VS(_)

                tf = fold_max_col(amx, 0, st[f"{tag}_amx{NT - 1}"], "x")

                def _(g, tf=tf):
                    c.V.wait(g, tf)
                    st[f"{tag}_arw0"] = c.V.tok(g.tensor_copy(arsb[0:1, 0:1],
                                                              amx[0:1, 0:1]))
                VS(_)

                ta = allreduce_lanes("x", 1, st[f"{tag}_arw0"])

                def _(g, ta=ta):
                    c.V.wait(g, ta)
                    t = c.V.tok(g.tensor_scalar(arc[:, 1:2], colt[:, 0:1],
                                                1.0 / 32767.0, 1e-12, AL.mult, AL.max))
                    c.V.wait(g, t)
                    st[f"{tag}_sx"] = c.V.tok(g.reciprocal(arc[:, 0:1], arc[:, 1:2]))
                VS(_)

                for ch in range(NT):
                    bdep = bc_stats(ch, st.get(f"{tag}_xq{ch - 1}", st[f"{tag}_sx"]),
                                    f"b{ch}")
                    for dt in range(ND):
                        def _(g, ch=ch, dt=dt, bdep=bdep):
                            sl = slice(ch * 512, (ch + 1) * 512)
                            pq = dt % 2
                            c.V.wait(g, bdep, st[f"{tag}_sx"])
                            t = c.V.tok(g.tensor_mul(ucc[:, pq, :], X[:, dt, sl],
                                                     rstdc[:, :]))
                            c.V.wait(g, t)
                            t = c.V.tok(g.tensor_sub(ucc[:, pq, :], ucc[:, pq, :],
                                                     mrsc[:, :]))
                            c.V.wait(g, t)
                            t = c.V.tok(g.tensor_scalar(
                                ucc[:, pq, :], ucc[:, pq, :], arc[:, 0:1], MAGIC,
                                AL.mult, AL.add))
                            c.V.wait(g, t)
                            t = c.V.tok(g.tensor_scalar(
                                xqb[:, dt, sl], ucc[:, pq, :], MAGIC, arc[:, 1:2],
                                AL.subtract, AL.mult))
                            if dt == ND - 1:
                                st[f"{tag}_xq{ch}"] = t
                        VS(_)
                st[f"{tag}_xqd"] = st[f"{tag}_xq{NT - 1}"]
                fence()

                with ExitStack() as mm:
                    slabs = mm.enter_context(
                        nc.sbuf_tensor(f"{tag}_slabs", [P, ND, I_FFN], BF16))
                    evb = [mm.enter_context(
                        nc.sbuf_tensor(f"{tag}_evb{i}", [P, 512], BF16))
                        for i in range(2)]

                    for mi, (wqsrc, dstd, lane) in enumerate(
                            [(wqu, upd, 2), (wqg, gated, 3)]):
                        def _(g, mi=mi, wqsrc=wqsrc):
                            for kt in range(ND):
                                c.Ds.wait(g, st.get(f"{tag}_mmdone{mi - 1}"))
                                g.dma_start(
                                    out=slabs[:, kt, :],
                                    in_=bass.AP(wqsrc, kt * P,
                                                [[D, I_FFN], [1, P]]),
                                    transpose=True).then_inc(c.Ds.sem, 16)
                                c.Ds.n += 16
                            st[f"{tag}_slab{mi}"] = (c.Ds.sem, c.Ds.n)
                        YS(_)

                        def _(g, mi=mi, lane=lane):
                            st[f"{tag}_amz{mi}"] = c.V.tok(g.memset(
                                amx[:, 2:3], 0.0))
                        VS(_)

                        for tt in range(NTT):
                            for oc in range(NOC):
                                gi = tt * NOC + oc
                                pq = gi % 2

                                def _(g, mi=mi, tt=tt, oc=oc, gi=gi, pq=pq):
                                    c.T.wait(g, st[f"{tag}_slab{mi}"],
                                             st[f"{tag}_xqd"])
                                    c.T.wait(g, st.get(f"{tag}_ev{mi}_{gi - 2}"))
                                    for kt in range(ND):
                                        mmi = g.matmul(
                                            ps_a[pq][:, :],
                                            xqb[:, kt, tt * P:(tt + 1) * P],
                                            slabs[:, kt, oc * 512:(oc + 1) * 512],
                                            start=(kt == 0), stop=(kt == ND - 1))
                                    st[f"{tag}_mm{mi}_{gi}"] = c.T.tok(mmi)
                                TS(_)

                                def _(g, mi=mi, gi=gi, pq=pq):
                                    c.V.wait(g, st[f"{tag}_mm{mi}_{gi}"])
                                    c.V.wait(g, st.get(f"{tag}_st{mi}_{gi - 2}"))
                                    st[f"{tag}_ev{mi}_{gi}"] = c.V.tok(g.tensor_copy(
                                        evb[pq][:, :], ps_a[pq][:, :]))
                                VS(_)

                                def _(g, mi=mi, gi=gi, pq=pq):
                                    c.V.wait(g, st[f"{tag}_ev{mi}_{gi}"],
                                             st[f"{tag}_amz{mi}"])
                                    c.V.wait(g, st.get(f"{tag}_amv{mi}_{gi - 1}"))
                                    t = c.V.tok(g.tensor_reduce(
                                        amx[:, 3:4], evb[pq][:, :], AX.X, AL.max,
                                        apply_absolute_value=True))
                                    c.V.wait(g, t)
                                    st[f"{tag}_amv{mi}_{gi}"] = c.V.tok(g.tensor_max(
                                        amx[:, 2:3], amx[:, 2:3], amx[:, 3:4]))
                                VS(_)

                                def _(g, mi=mi, tt=tt, oc=oc, gi=gi, pq=pq,
                                      dstd=dstd):
                                    c.G.wait(g, st[f"{tag}_ev{mi}_{gi}"])
                                    st[f"{tag}_st{mi}_{gi}"] = c.Dg.tok(g.dma_start(
                                        out=bass.AP(dstd,
                                                    tt * P * I_FFN + oc * 512,
                                                    [[I_FFN, P], [1, 512]]),
                                        in_=evb[pq][:, :]))
                                GS(_)

                        tfm = fold_max_col(amx, 2,
                                           st[f"{tag}_amv{mi}_{NTT * NOC - 1}"],
                                           f"m{mi}")

                        def _(g, mi=mi, tfm=tfm, lane=lane):
                            c.V.wait(g, tfm)
                            st[f"{tag}_mmdone{mi}"] = c.V.tok(g.tensor_copy(
                                arsb[0:1, mi:mi + 1], amx[0:1, 2:3]))
                        VS(_)

                tb = allreduce_lanes("ug", 2, st[f"{tag}_mmdone1"])

                def _(g, tb=tb):
                    c.V.wait(g, tb)
                    t = c.V.tok(g.tensor_scalar(arc[:, 5:6], colt[:, 0:1],
                                                1.0 / 32767.0, 1e-12, AL.mult, AL.max))
                    c.V.wait(g, t)
                    t = c.V.tok(g.reciprocal(arc[:, 2:3], arc[:, 5:6]))
                    c.V.wait(g, t)
                    t = c.V.tok(g.tensor_scalar(arc[:, 4:5], colt[:, 1:2],
                                                1.0 / 32767.0, 1e-12, AL.mult, AL.max))
                    c.V.wait(g, t)
                    st[f"{tag}_scug"] = c.V.tok(g.reciprocal(arc[:, 3:4],
                                                             arc[:, 4:5]))
                VS(_)
                fence()

                with ExitStack() as pb:
                    grow = [pb.enter_context(
                        nc.sbuf_tensor(f"{tag}_grow{i}", [P, I_FFN], BF16))
                        for i in range(2)]
                    t1r = pb.enter_context(
                        nc.sbuf_tensor(f"{tag}_t1r", [P, I_FFN], F32))
                    urow = pb.enter_context(
                        nc.sbuf_tensor(f"{tag}_urow", [P, I_FFN], BF16))
                    t2r = pb.enter_context(
                        nc.sbuf_tensor(f"{tag}_t2r", [P, I_FFN], F32))

                    def _(g):
                        st[f"{tag}_amz2"] = c.V.tok(g.memset(amx[:, 2:3], 0.0))
                    VS(_)

                    for tt in range(NTT):
                        pq = tt % 2

                        def _(g, tt=tt, pq=pq):
                            c.Ds.wait(g, st[f"{tag}_st1_{tt * NOC + NOC - 1}"])
                            c.Ds.wait(g, st.get(f"{tag}_g2w{tt - 2}"))
                            st[f"{tag}_gld{tt}"] = c.Ds.tok(g.dma_start(
                                out=grow[pq][:, :],
                                in_=bass.AP(gated, tt * P * I_FFN,
                                            [[I_FFN, P], [1, I_FFN]])))
                        YS(_)

                        def _(g, tt=tt, pq=pq):
                            c.V.wait(g, st[f"{tag}_gld{tt}"], st[f"{tag}_scug"],
                                     st[f"{tag}_amz2"])
                            c.V.wait(g, st.get(f"{tag}_g2v{tt - 1}"))
                            t = c.V.tok(g.tensor_scalar(
                                t1r[:, :], grow[pq][:, :], arc[:, 3:4], MAGIC,
                                AL.mult, AL.add))
                            c.V.wait(g, t)
                            st[f"{tag}_t1_{tt}"] = c.V.tok(g.tensor_scalar(
                                t1r[:, :], t1r[:, :], MAGIC, None, AL.subtract))
                        VS(_)

                        def _(g, tt=tt):
                            c.S.wait(g, st[f"{tag}_t1_{tt}"])
                            st[f"{tag}_sg_{tt}"] = c.S.tok(g.activation(
                                t2r[:, :], t1r[:, :], AF.Sigmoid,
                                scale=arc[:, 4:5]))
                        SS(_)

                        def _(g, tt=tt, pq=pq):
                            c.V.wait(g, st[f"{tag}_sg_{tt}"])
                            t = c.V.tok(g.tensor_scalar(
                                t2r[:, :], t2r[:, :], 65536.0, MAGIC,
                                AL.mult, AL.add))
                            c.V.wait(g, t)
                            t = c.V.tok(g.tensor_scalar(
                                t2r[:, :], t2r[:, :], MAGIC, 65535.0,
                                AL.subtract, AL.min))
                            c.V.wait(g, t)
                            t = c.V.tok(g.tensor_mul(grow[pq][:, :], t1r[:, :],
                                                     t2r[:, :]))
                            c.V.wait(g, t)
                            t = c.V.tok(g.tensor_reduce(
                                amx[:, 3:4], grow[pq][:, :], AX.X, AL.max,
                                apply_absolute_value=True))
                            c.V.wait(g, t)
                            st[f"{tag}_g2v{tt}"] = c.V.tok(g.tensor_max(
                                amx[:, 2:3], amx[:, 2:3], amx[:, 3:4]))
                        VS(_)

                        def _(g, tt=tt, pq=pq):
                            c.G.wait(g, st[f"{tag}_g2v{tt}"])
                            st[f"{tag}_g2w{tt}"] = c.Dg.tok(g.dma_start(
                                out=bass.AP(g2d, tt * P * I_FFN,
                                            [[I_FFN, P], [1, I_FFN]]),
                                in_=grow[pq][:, :]))
                        GS(_)

                    tf2 = fold_max_col(amx, 2, st[f"{tag}_g2v{NTT - 1}"], "g2")

                    def _(g, tf2=tf2):
                        c.V.wait(g, tf2)
                        st[f"{tag}_arw2"] = c.V.tok(g.tensor_copy(
                            arsb[0:1, 0:1], amx[0:1, 2:3]))
                    VS(_)

                    tc = allreduce_lanes("g2", 1, st[f"{tag}_arw2"])

                    def _(g, tc=tc):
                        c.V.wait(g, tc)
                        t = c.V.tok(g.tensor_scalar(colt[:, 1:2], arc[:, 4:5],
                                                    1.0 / 65536.0, None, AL.mult))
                        c.V.wait(g, t)
                        t = c.V.tok(g.tensor_mul(colt[:, 2:3], colt[:, 0:1],
                                                 colt[:, 1:2]))
                        c.V.wait(g, t)
                        t = c.V.tok(g.tensor_scalar(colt[:, 2:3], colt[:, 2:3],
                                                    1.0 / 32767.0, 1e-12,
                                                    AL.mult, AL.max))
                        c.V.wait(g, t)
                        t = c.V.tok(g.reciprocal(arc[:, 5:6], colt[:, 2:3]))
                        c.V.wait(g, t)
                        st[f"{tag}_sg2"] = c.V.tok(g.tensor_mul(
                            arc[:, 5:6], arc[:, 5:6], colt[:, 1:2]))
                    VS(_)

                    def _(g):
                        st[f"{tag}_amz3"] = c.V.tok(g.memset(amx[:, 2:3], 0.0))
                    VS(_)

                    for tt in range(NTT):
                        pq = tt % 2

                        def _(g, tt=tt, pq=pq):
                            c.Ds.wait(g, st[f"{tag}_sg2"], st[f"{tag}_g2w{NTT - 1}"])
                            c.Ds.wait(g, st.get(f"{tag}_orw{tt - 2}"))
                            g.dma_start(
                                out=grow[pq][:, :],
                                in_=bass.AP(g2d, tt * P * I_FFN,
                                            [[I_FFN, P], [1, I_FFN]])
                            ).then_inc(c.Ds.sem, 16)
                            c.Ds.n += 16
                            st[f"{tag}_c1ld{tt}"] = c.Ds.tok(g.dma_start(
                                out=urow[:, :],
                                in_=bass.AP(upd, tt * P * I_FFN,
                                            [[I_FFN, P], [1, I_FFN]])))
                        YS(_)

                        def _(g, tt=tt, pq=pq):
                            c.V.wait(g, st[f"{tag}_c1ld{tt}"], st[f"{tag}_amz3"])
                            c.V.wait(g, st.get(f"{tag}_orv{tt - 1}"))
                            t = c.V.tok(g.tensor_scalar(
                                t1r[:, :], urow[:, :], arc[:, 2:3], MAGIC,
                                AL.mult, AL.add))
                            c.V.wait(g, t)
                            t = c.V.tok(g.tensor_scalar(
                                t1r[:, :], t1r[:, :], MAGIC, None, AL.subtract))
                            c.V.wait(g, t)
                            t = c.V.tok(g.tensor_scalar(
                                t2r[:, :], grow[pq][:, :], arc[:, 5:6], MAGIC,
                                AL.mult, AL.add))
                            c.V.wait(g, t)
                            t = c.V.tok(g.tensor_scalar(
                                t2r[:, :], t2r[:, :], MAGIC, None, AL.subtract))
                            c.V.wait(g, t)
                            t = c.V.tok(g.tensor_mul(grow[pq][:, :], t1r[:, :],
                                                     t2r[:, :]))
                            c.V.wait(g, t)
                            t = c.V.tok(g.tensor_reduce(
                                amx[:, 3:4], grow[pq][:, :], AX.X, AL.max,
                                apply_absolute_value=True))
                            c.V.wait(g, t)
                            st[f"{tag}_orv{tt}"] = c.V.tok(g.tensor_max(
                                amx[:, 2:3], amx[:, 2:3], amx[:, 3:4]))
                        VS(_)

                        def _(g, tt=tt, pq=pq):
                            c.G.wait(g, st[f"{tag}_orv{tt}"])
                            st[f"{tag}_orw{tt}"] = c.Dg.tok(g.dma_start(
                                out=bass.AP(ord_, tt * P * I_FFN,
                                            [[I_FFN, P], [1, I_FFN]]),
                                in_=grow[pq][:, :]))
                        GS(_)

                    tf3 = fold_max_col(amx, 2, st[f"{tag}_orv{NTT - 1}"], "or")

                    def _(g, tf3=tf3):
                        c.V.wait(g, tf3)
                        st[f"{tag}_arw3"] = c.V.tok(g.tensor_copy(
                            arsb[0:1, 0:1], amx[0:1, 2:3]))
                    VS(_)

                    td = allreduce_lanes("or", 1, st[f"{tag}_arw3"])

                    def _(g, td=td):
                        c.V.wait(g, td)
                        t = c.V.tok(g.reciprocal(colt[:, 3:4], arc[:, 2:3]))
                        c.V.wait(g, t)
                        t = c.V.tok(g.tensor_mul(colt[:, 1:2], colt[:, 2:3],
                                                 colt[:, 3:4]))
                        c.V.wait(g, t)
                        t = c.V.tok(g.tensor_mul(colt[:, 3:4], colt[:, 0:1],
                                                 colt[:, 1:2]))
                        c.V.wait(g, t)
                        t = c.V.tok(g.tensor_scalar(colt[:, 3:4], colt[:, 3:4],
                                                    1.0 / 32767.0, 1e-12,
                                                    AL.mult, AL.max))
                        c.V.wait(g, t)
                        t = c.V.tok(g.reciprocal(arc[:, 6:7], colt[:, 3:4]))
                        c.V.wait(g, t)
                        t = c.V.tok(g.tensor_mul(arc[:, 6:7], arc[:, 6:7],
                                                 colt[:, 1:2]))
                        c.V.wait(g, t)
                        st[f"{tag}_so"] = c.V.tok(g.tensor_scalar(
                            arc[:, 7:8], colt[:, 3:4], 0.5, None, AL.mult))
                    VS(_)
                fence()

                with ExitStack() as dn:
                    wds = dn.enter_context(
                        nc.sbuf_tensor(f"{tag}_wds", [P, NI, D], BF16))
                    ofq = dn.enter_context(
                        nc.sbuf_tensor(f"{tag}_ofq", [P, NI, 512], BF16))
                    ofb = dn.enter_context(
                        nc.sbuf_tensor(f"{tag}_ofb", [P, 2, 512], BF16))
                    oft = dn.enter_context(
                        nc.sbuf_tensor(f"{tag}_oft", [P, 2, 512], F32))

                    def _(g):
                        for kt in range(NI):
                            g.dma_start(
                                out=wds[:, kt, :],
                                in_=bass.AP(wqd, kt * P, [[I_FFN, D], [1, P]]),
                                transpose=True).then_inc(c.Ds.sem, 16)
                            c.Ds.n += 16
                        st[f"{tag}_wdsld"] = (c.Ds.sem, c.Ds.n)
                    YS(_)

                    for tch in range(NT):
                        for kt in range(NI):
                            def _(g, tch=tch, kt=kt):
                                c.Ds.wait(g, st[f"{tag}_orw{NTT - 1}"])
                                if kt == 0:
                                    c.Ds.wait(g, st.get(f"{tag}_xup{tch - 1}"))
                                st[f"{tag}_ofld_{tch}_{kt}"] = c.Ds.tok(g.dma_start(
                                    out=ofb[:, kt % 2, :],
                                    in_=bass.AP(ord_, tch * 512 * I_FFN + kt * P,
                                                [[I_FFN, 512], [1, P]]),
                                    transpose=True))
                            YS(_)

                            def _(g, tch=tch, kt=kt):
                                c.V.wait(g, st[f"{tag}_ofld_{tch}_{kt}"],
                                         st[f"{tag}_so"])
                                c.V.wait(g, st.get(f"{tag}_ofq_{tch}_{kt - 2}"))
                                t = c.V.tok(g.tensor_scalar(
                                    oft[:, kt % 2, :], ofb[:, kt % 2, :],
                                    arc[:, 6:7], MAGIC, AL.mult, AL.add))
                                c.V.wait(g, t)
                                st[f"{tag}_ofqv_{tch}_{kt}"] = c.V.tok(g.tensor_scalar(
                                    ofq[:, kt, :], oft[:, kt % 2, :], MAGIC, None,
                                    AL.subtract))
                                st[f"{tag}_ofq_{tch}_{kt}"] = st[f"{tag}_ofqv_{tch}_{kt}"]
                            VS(_)

                        for dt in range(ND):
                            pq = dt % 2

                            def _(g, tch=tch, dt=dt, pq=pq):
                                c.T.wait(g, st[f"{tag}_wdsld"])
                                c.T.wait(g, st[f"{tag}_ofq_{tch}_{NI - 1}"])
                                c.T.wait(g, st.get(f"{tag}_dnx_{tch}_{dt - 2}"))
                                for kt in range(NI):
                                    mmi = g.matmul(
                                        ps_a[pq][:, :],
                                        wds[:, kt, dt * P:(dt + 1) * P],
                                        ofq[:, kt, :],
                                        start=(kt == 0), stop=(kt == NI - 1))
                                st[f"{tag}_dnmm_{tch}_{dt}"] = c.T.tok(mmi)
                            TS(_)

                            def _(g, tch=tch, dt=dt, pq=pq):
                                sl = slice(tch * 512, (tch + 1) * 512)
                                c.V.wait(g, st[f"{tag}_dnmm_{tch}_{dt}"])
                                t = c.V.tok(g.scalar_tensor_tensor(
                                    X[:, dt, sl], ps_a[pq][:, :], arc[:, 7:8],
                                    X[:, dt, sl], AL.mult, AL.add))
                                st[f"{tag}_dnx_{tch}_{dt}"] = t
                                if dt == ND - 1:
                                    st[f"{tag}_xup{tch}"] = t
                            VS(_)

                st[f"{tag}_done"] = st[f"{tag}_xup{NT - 1}"]
            fence()
            return st[f"{tag}_done"]

        def emit_l1_attn(tag, xdep):
            with ExitStack() as ls:
                def sb(n, shape, dtype):
                    return ls.enter_context(
                        nc.sbuf_tensor(f"{tag}_{n}", shape, dtype))

                colt = sb("colt", [P, 4], F32)
                acol = sb("acol", [P, 4], F32)
                aco2 = sb("aco2", [P, 4], F32)
                gq8 = sb("gq8", [P, NPT], F32)
                bo2 = sb("bo2", [P, 2], F32)
                amf = sb("amf", [P, 512], F32)
                gb = sb("gb", [P, 512], F32)
                ucc = sb("ucc", [P, 2, 512], F32)
                rstdc = sb("rstdc", [P, 512], F32)
                mrsc = sb("mrsc", [P, 512], F32)
                glocal = sb("glocal", [1, TL], F32)
                grow = sb("grow", [1, SEQ], F32)

                def bc_stats(ch, dep, key):
                    def _(g, ch=ch, dep=dep, key=key):
                        c.G.wait(g, dep)
                        g.dma_start(out=rstdc[:, :],
                                    in_=bass.AP(rows, TL + ch * 512,
                                                [[2 * TL, 1], [0, P], [1, 512]])
                                    ).then_inc(c.Dg.sem, 16)
                        c.Dg.n += 16
                        st[f"{tag}_bcs_{key}"] = c.Dg.tok(g.dma_start(
                            out=mrsc[:, :],
                            in_=bass.AP(rows, ch * 512,
                                        [[2 * TL, 1], [0, P], [1, 512]])))
                    GS(_)
                    return st[f"{tag}_bcs_{key}"]

                def fold_rowmax(dep, key, dstrow, dstw, dstoff):
                    t = dep
                    for k in [64, 32, 16, 8, 4, 2, 1]:
                        def _(g, k=k, t=t, key=key):
                            c.G.wait(g, t)
                            st[f"{tag}_pf_{key}{k}"] = c.Dg.tok(g.dma_start(
                                out=ucc[0:k, 0, :], in_=amf[k:2 * k, :]))
                        GS(_)

                        def _(g, k=k, key=key):
                            c.V.wait(g, st[f"{tag}_pf_{key}{k}"])
                            st[f"{tag}_pv_{key}{k}"] = c.V.tok(g.tensor_max(
                                amf[0:k, :], amf[0:k, :], ucc[0:k, 0, :]))
                        VS(_)
                        t = st[f"{tag}_pv_{key}{k}"]

                    def _(g, t=t, key=key, dstrow=dstrow, dstw=dstw,
                          dstoff=dstoff):
                        c.V.wait(g, t)
                        st[f"{tag}_pr_{key}"] = c.V.tok(g.tensor_copy(
                            bass.AP(dstrow, dstoff, [[dstw, 1], [1, 512]]),
                            amf[0:1, :]))
                    VS(_)
                    return st[f"{tag}_pr_{key}"]

                def walpha(wsrc, wid, nelem, rscol, acolap, key, wfp, wab,
                           acc1, dep):
                    def _(g, key=key):
                        c.V.wait(g, st.get(f"{tag}_alast"))
                        st[f"{tag}_az_{key}"] = c.V.tok(g.memset(acc1[:, 0:1],
                                                                 0.0))
                    VS(_)
                    for ti in range(ND):
                        def _(g, ti=ti, key=key, wsrc=wsrc, wid=wid, dep=dep):
                            c.Ds.wait(g, st.get(f"{tag}_wsa_{key}{ti - 1}"))
                            if ti == 0:
                                c.Ds.wait(g, dep)
                            st[f"{tag}_wsl_{key}{ti}"] = c.Ds.tok(g.dma_start(
                                out=wfp[:, 0:wid],
                                in_=bass.AP(wsrc, ti * P * wid,
                                            [[wid, P], [1, wid]])))
                        YS(_)

                        def _(g, ti=ti, key=key, wid=wid):
                            c.S.wait(g, st[f"{tag}_wsl_{key}{ti}"])
                            c.S.wait(g, st.get(f"{tag}_wsa_{key}{ti - 1}"))
                            st[f"{tag}_wsb_{key}{ti}"] = c.S.tok(g.activation(
                                wab[:, 0:wid], wfp[:, 0:wid], AF.Abs))
                        SS(_)

                        def _(g, ti=ti, key=key, wid=wid):
                            c.V.wait(g, st[f"{tag}_wsb_{key}{ti}"],
                                     st[f"{tag}_az_{key}"])
                            t = c.V.tok(g.tensor_reduce(
                                acc1[:, 1:2], wab[:, 0:wid], AX.X, AL.add))
                            c.V.wait(g, t)
                            st[f"{tag}_wsa_{key}{ti}"] = c.V.tok(g.tensor_add(
                                acc1[:, 0:1], acc1[:, 0:1], acc1[:, 1:2]))
                        VS(_)
                    t = st[f"{tag}_wsa_{key}{ND - 1}"]
                    for k in [64, 32, 16, 8, 4, 2, 1]:
                        def _(g, k=k, t=t, key=key, acc1=acc1):
                            c.G.wait(g, t)
                            st[f"{tag}_wfg_{key}{k}"] = c.Dg.tok(g.dma_start(
                                out=colt[0:k, 3:4], in_=acc1[k:2 * k, 0:1]))
                        GS(_)

                        def _(g, k=k, key=key, acc1=acc1):
                            c.V.wait(g, st[f"{tag}_wfg_{key}{k}"])
                            st[f"{tag}_wfv_{key}{k}"] = c.V.tok(g.tensor_add(
                                acc1[0:k, 0:1], acc1[0:k, 0:1], colt[0:k, 3:4]))
                        VS(_)
                        t = st[f"{tag}_wfv_{key}{k}"]

                    def _(g, t=t, key=key, acc1=acc1):
                        c.G.wait(g, t)
                        st[f"{tag}_abc_{key}"] = c.Dg.tok(g.dma_start(
                            out=colt[:, 3:4],
                            in_=bass.AP(acc1, 0, [[2, 1], [0, P], [1, 1]])))
                    GS(_)

                    def _(g, key=key, nelem=nelem, rscol=rscol, acolap=acolap):
                        c.V.wait(g, st[f"{tag}_abc_{key}"])
                        t = c.V.tok(g.tensor_scalar(
                            acolap, colt[:, 3:4], 1.0 / nelem, 1e-8,
                            AL.mult, AL.max))
                        c.V.wait(g, t)
                        st[f"{tag}_al_{key}"] = c.V.tok(
                            g.reciprocal(rscol, acolap))
                        st[f"{tag}_alast"] = st[f"{tag}_al_{key}"]
                    VS(_)
                    return st[f"{tag}_al_{key}"]

                def wtern(wsrc, wid, ntile, dstw, dst, rscol, adep, key, wfp,
                          wab):
                    for ti in range(ntile):
                        def _(g, ti=ti, key=key, wsrc=wsrc, wid=wid,
                              dstw=dstw):
                            c.Ds.wait(g, st.get(f"{tag}_tn_{key}{ti - 1}"))
                            if ti == 0:
                                c.Ds.wait(g, st[f"{tag}_wsb_{key}{ND - 1}"])
                            st[f"{tag}_tl_{key}{ti}"] = c.Ds.tok(g.dma_start(
                                out=wfp[:, 0:dstw],
                                in_=bass.AP(wsrc, ti * P * wid,
                                            [[wid, P], [1, dstw]])))
                        YS(_)

                        def _(g, ti=ti, key=key, dstw=dstw, dst=dst,
                              rscol=rscol, adep=adep):
                            c.V.wait(g, st[f"{tag}_tl_{key}{ti}"], adep)
                            t = c.V.tok(g.tensor_scalar(
                                wab[:, 0:dstw], wfp[:, 0:dstw], rscol, 0.5,
                                AL.mult, AL.is_ge))
                            c.V.wait(g, t)
                            t = c.V.tok(g.tensor_scalar(
                                wfp[:, 0:dstw], wfp[:, 0:dstw], rscol, -0.5,
                                AL.mult, AL.is_le))
                            c.V.wait(g, t)
                            st[f"{tag}_tn_{key}{ti}"] = c.V.tok(g.tensor_sub(
                                dst[:, ti, 0:dstw], wab[:, 0:dstw],
                                wfp[:, 0:dstw]))
                        VS(_)
                    return st[f"{tag}_tn_{key}{ntile - 1}"]

                emit_ln(tag, xdep, None)

                with ExitStack() as xs_:
                    xqb = xs_.enter_context(
                        nc.sbuf_tensor(f"{tag}_xqb", [P, ND, TL], BF16))

                    for ch in range(NT):
                        dep = (st[f"{tag}_u"] if ch == 0
                               else st[f"{tag}_gx{ch - 1}"])
                        bdep = bc_stats(ch, dep, f"a{ch}")
                        for dt in range(ND):
                            def _(g, ch=ch, dt=dt, bdep=bdep):
                                sl = slice(ch * 512, (ch + 1) * 512)
                                c.V.wait(g, bdep)
                                t = c.V.tok(g.tensor_mul(
                                    ucc[:, 0, :], X[:, dt, sl], rstdc[:, :]))
                                c.V.wait(g, t)
                                st[f"{tag}_uc_{ch}_{dt}"] = c.V.tok(
                                    g.tensor_sub(ucc[:, 0, :], ucc[:, 0, :],
                                                 mrsc[:, :]))
                            VS(_)

                            def _(g, ch=ch, dt=dt):
                                c.S.wait(g, st[f"{tag}_uc_{ch}_{dt}"])
                                c.S.wait(g, st.get(f"{tag}_ab_{ch}_{dt - 1}"))
                                st[f"{tag}_abs_{ch}_{dt}"] = c.S.tok(
                                    g.activation(ucc[:, 1, :], ucc[:, 0, :],
                                                 AF.Abs))
                            SS(_)

                            def _(g, ch=ch, dt=dt):
                                c.V.wait(g, st[f"{tag}_abs_{ch}_{dt}"])
                                if dt == 0:
                                    t = c.V.tok(g.tensor_copy(amf[:, :],
                                                              ucc[:, 1, :]))
                                else:
                                    t = c.V.tok(g.tensor_max(
                                        amf[:, :], amf[:, :], ucc[:, 1, :]))
                                st[f"{tag}_ab_{ch}_{dt}"] = t
                            VS(_)

                        st[f"{tag}_gx{ch}"] = fold_rowmax(
                            st[f"{tag}_ab_{ch}_{ND - 1}"], f"gx{ch}",
                            glocal, TL, ch * 512)

                    def _(g):
                        c.V.wait(g, st[f"{tag}_gx{NT - 1}"])
                        st[f"{tag}_gam"] = c.V.tok(g.tensor_scalar(
                            glocal[0:1, :], glocal[0:1, :], 1e-8, 1.0 / 127.0,
                            AL.max, AL.mult))
                    VS(_)

                    def _(g):
                        c.G.wait(g, st[f"{tag}_gam"])
                        st[f"{tag}_gsnd"] = c.Dg.tok(g.dma_start(
                            out=agg_in.ap(), in_=glocal[0:1, :]))
                        c.G.wait(g, st[f"{tag}_gsnd"])
                        cc = g.collective_compute(
                            "AllGather", AL.bypass,
                            replica_groups=[[0, 1], [2, 3], [4, 5], [6, 7]],
                            ins=[agg_in.ap().opt()], outs=[agg_out.ap().opt()])
                        st[f"{tag}_agg"] = c.CC.tok(cc)
                        c.G.wait(g, st[f"{tag}_agg"])
                        g.dma_start(out=bass.AP(grow, 0, [[SEQ, 1], [1, TL]]),
                                    in_=agg_out[0:1, :]).then_inc(c.Dg.sem, 16)
                        c.Dg.n += 16
                        st[f"{tag}_gfull"] = c.Dg.tok(g.dma_start(
                            out=bass.AP(grow, TL, [[SEQ, 1], [1, TL]]),
                            in_=agg_out[1:2, :]))
                    GS(_)

                    def _(g):
                        c.V.wait(g, st[f"{tag}_gsnd"])
                        st[f"{tag}_rg"] = c.V.tok(g.reciprocal(
                            glocal[0:1, :], glocal[0:1, :]))
                    VS(_)

                    for ch in range(NT):
                        def _(g, ch=ch):
                            c.G.wait(g, st[f"{tag}_rg"])
                            c.G.wait(g, st.get(f"{tag}_xi{ch - 1}"))
                            st[f"{tag}_rgb{ch}"] = c.Dg.tok(g.dma_start(
                                out=gb[:, :],
                                in_=bass.AP(glocal, ch * 512,
                                            [[TL, 1], [0, P], [1, 512]])))
                        GS(_)
                        bdep = bc_stats(ch, st[f"{tag}_rg"] if ch == 0
                                        else st[f"{tag}_xi{ch - 1}"], f"b{ch}")
                        for dt in range(ND):
                            def _(g, ch=ch, dt=dt, bdep=bdep):
                                sl = slice(ch * 512, (ch + 1) * 512)
                                pq = dt % 2
                                c.V.wait(g, bdep, st[f"{tag}_rgb{ch}"])
                                t = c.V.tok(g.tensor_mul(
                                    ucc[:, pq, :], X[:, dt, sl], rstdc[:, :]))
                                c.V.wait(g, t)
                                t = c.V.tok(g.tensor_sub(
                                    ucc[:, pq, :], ucc[:, pq, :], mrsc[:, :]))
                                c.V.wait(g, t)
                                t = c.V.tok(g.tensor_mul(
                                    ucc[:, pq, :], ucc[:, pq, :], gb[:, :]))
                                c.V.wait(g, t)
                                t = c.V.tok(g.tensor_scalar(
                                    xqb[:, dt, sl], ucc[:, pq, :], MAGIC,
                                    MAGIC, AL.add, AL.subtract))
                                if dt == ND - 1:
                                    st[f"{tag}_xi{ch}"] = t
                            VS(_)

                    def _(g):
                        c.G.wait(g, st[f"{tag}_xi{NT - 1}"])
                        st[f"{tag}_xsnd"] = c.Dg.tok(g.dma_start(
                            out=agx_in.ap(), in_=xqb[:, :, :]))
                        c.G.wait(g, st[f"{tag}_xsnd"])
                        cc = g.collective_compute(
                            "AllGather", AL.bypass,
                            replica_groups=[[0, 1], [2, 3], [4, 5], [6, 7]],
                            ins=[agx_in.ap().opt()], outs=[agx_out.ap().opt()])
                        st[f"{tag}_agx"] = c.CC.tok(cc)
                    GS(_)
                fence()

                def _(g):
                    g.memset(bo2[:, :], 0.0)
                    g.memset(bo2[0:DH, 0:1], 1.0)
                    st["bo2m"] = c.V.tok(g.memset(bo2[DH:P, 1:2], 1.0))
                VS(_)

                def _(g):
                    st[f"{tag}_qgld"] = c.Ds.tok(g.dma_start(
                        out=amf[0:1, 0:NHEAD], in_=qg_in[0:1, :]))
                YS(_)

                def _(g):
                    c.S.wait(g, st[f"{tag}_qgld"])
                    st[f"{tag}_qgab"] = c.S.tok(g.activation(
                        amf[0:1, 16:32], amf[0:1, 0:16], AF.Abs))
                SS(_)

                def _(g):
                    c.V.wait(g, st[f"{tag}_qgab"])
                    t = c.V.tok(g.tensor_reduce(
                        amf[0:1, 32:33], amf[0:1, 16:32], AX.X, AL.max))
                    c.V.wait(g, t)
                    st[f"{tag}_ngc"] = c.V.tok(g.tensor_scalar(
                        amf[0:1, 32:33], amf[0:1, 32:33], -8.0, None, AL.mult))
                VS(_)

                def _(g):
                    c.G.wait(g, st[f"{tag}_ngc"])
                    g.dma_start(out=acol[:, 3:4],
                                in_=bass.AP(amf, 32, [[512, 1], [0, P], [1, 1]])
                                ).then_inc(c.Dg.sem, 16)
                    c.Dg.n += 16
                    for pt in range(NPT):
                        g.dma_start(out=gq8[0:64, pt:pt + 1],
                                    in_=bass.AP(amf, 2 * pt,
                                                [[512, 1], [0, 64], [1, 1]])
                                    ).then_inc(c.Dg.sem, 16)
                        c.Dg.n += 16
                        g.dma_start(out=gq8[64:P, pt:pt + 1],
                                    in_=bass.AP(amf, 2 * pt + 1,
                                                [[512, 1], [0, 64], [1, 1]])
                                    ).then_inc(c.Dg.sem, 16)
                        c.Dg.n += 16
                    st[f"{tag}_gq8r"] = (c.Dg.sem, c.Dg.n)
                GS(_)

                def _(g):
                    c.V.wait(g, st[f"{tag}_gq8r"])
                    st[f"{tag}_gq8"] = c.V.tok(g.tensor_scalar(
                        gq8[:, :], gq8[:, :], 0.125, None, AL.mult))
                VS(_)

                cds = ExitStack()
                qs = cds.enter_context(
                    nc.sbuf_tensor(f"{tag}_qs", [P, NPT, SEQ], BF16))
                ks = cds.enter_context(
                    nc.sbuf_tensor(f"{tag}_ks", [P, SEQ], BF16))
                vT = cds.enter_context(
                    nc.sbuf_tensor(f"{tag}_vT", [P, 2, SEQ // P, DH + 1],
                                   BF16))

                bcw = ExitStack()
                wqT = bcw.enter_context(
                    nc.sbuf_tensor(f"{tag}_wqT", [P, ND, 512], BF16))
                wkT = bcw.enter_context(
                    nc.sbuf_tensor(f"{tag}_wkT", [P, ND, P], BF16))
                wvT = bcw.enter_context(
                    nc.sbuf_tensor(f"{tag}_wvT", [P, ND, P], BF16))

                with ExitStack() as ws:
                    wfp = ws.enter_context(
                        nc.sbuf_tensor(f"{tag}_wfp", [P, D], F32))
                    wab = ws.enter_context(
                        nc.sbuf_tensor(f"{tag}_wab", [P, D], F32))
                    acc1 = ws.enter_context(
                        nc.sbuf_tensor(f"{tag}_acc1", [P, 2], F32))

                    tq = walpha(wq_td, D, D * D, acol[:, 0:1], aco2[:, 0:1],
                                "q", wfp, wab, acc1, None)
                    tq2 = wtern(wq_td, D, ND, 512, wqT, acol[:, 0:1], tq, "q",
                                wfp, wab)
                    tk_ = walpha(wk_td, 2 * P, D * 2 * P, acol[:, 1:2],
                                 aco2[:, 1:2], "k", wfp, wab, acc1, tq2)
                    tk2 = wtern(wk_td, 2 * P, ND, P, wkT, acol[:, 1:2], tk_,
                                "k", wfp, wab)
                    tv_ = walpha(wv_td, 2 * P, D * 2 * P, acol[:, 2:3],
                                 aco2[:, 2:3], "v", wfp, wab, acc1, tk2)
                    st[f"{tag}_terns"] = wtern(wv_td, 2 * P, ND, P, wvT,
                                               acol[:, 2:3], tv_, "v", wfp,
                                               wab)
                fence()

                with ExitStack() as qp:
                    xsl = qp.enter_context(
                        nc.sbuf_tensor(f"{tag}_xsl", [P, ND, 512], BF16))
                    ccc = qp.enter_context(
                        nc.sbuf_tensor(f"{tag}_ccc", [P, 512], F32))
                    ssc = qp.enter_context(
                        nc.sbuf_tensor(f"{tag}_ssc", [P, 512], F32))
                    qc32 = qp.enter_context(
                        nc.sbuf_tensor(f"{tag}_qc32", [P, 512], F32))
                    qrt = qp.enter_context(
                        nc.sbuf_tensor(f"{tag}_qrt", [P, 512], F32))
                    mrm = qp.enter_context(
                        nc.sbuf_tensor(f"{tag}_mrm", [2, 512], F32))
                    rb = qp.enter_context(
                        nc.sbuf_tensor(f"{tag}_rb", [P, 512], F32))
                    vlow = qp.enter_context(
                        nc.sbuf_tensor(f"{tag}_vlow", [64, 512], F32))

                    def _(g):
                        st[f"{tag}_vones"] = c.V.tok(g.memset(vT[:, :, :, :],
                                                              1.0))
                    VS(_)

                    for tch in range(SEQT):
                        blk, lch = tch // NT, tch % NT

                        def _(g, tch=tch, blk=blk, lch=lch):
                            c.Ds.wait(g, st[f"{tag}_agx"])
                            c.Ds.wait(g, st.get(f"{tag}_vtr_{tch - 1}_7"))
                            g.dma_start(
                                out=xsl[:, :, :],
                                in_=bass.AP(agx_out,
                                            blk * P * ND * TL + lch * 512,
                                            [[ND * TL, P], [TL, ND], [1, 512]])
                            ).then_inc(c.Ds.sem, 16)
                            c.Ds.n += 16
                            g.dma_start(out=ccc[:, :],
                                        in_=bass.AP(cc_in, tch * 512,
                                                    [[SEQ, P], [1, 512]])
                                        ).then_inc(c.Ds.sem, 16)
                            c.Ds.n += 16
                            st[f"{tag}_cld{tch}"] = c.Ds.tok(g.dma_start(
                                out=ssc[:, :],
                                in_=bass.AP(ss_in, tch * 512,
                                            [[SEQ, P], [1, 512]])))
                        YS(_)

                        def _(g, tch=tch):
                            c.G.wait(g, st[f"{tag}_gfull"])
                            c.G.wait(g, st.get(f"{tag}_evv_{tch - 1}_5"))
                            st[f"{tag}_gbb{tch}"] = c.Dg.tok(g.dma_start(
                                out=gb[:, :],
                                in_=bass.AP(grow, tch * 512,
                                            [[SEQ, 1], [0, P], [1, 512]])))
                        GS(_)

                        for pi in range(6):
                            pq = (tch * 6 + pi) % 2
                            aci = [0, 0, 0, 0, 1, 2][pi]

                            def _(g, tch=tch, pi=pi, pq=pq):
                                c.T.wait(g, st[f"{tag}_terns"],
                                         st[f"{tag}_cld{tch}"])
                                c.T.wait(g, st.get(
                                    f"{tag}_evv_{tch}_{pi - 2}" if pi >= 2
                                    else f"{tag}_evv_{tch - 1}_{pi + 4}"))
                                for kt in range(ND):
                                    if pi < 4:
                                        lw = wqT[:, kt, pi * P:(pi + 1) * P]
                                    elif pi == 4:
                                        lw = wkT[:, kt, :]
                                    else:
                                        lw = wvT[:, kt, :]
                                    mmi = g.matmul(ps_a[pq][:, :], lw,
                                                   xsl[:, kt, :],
                                                   start=(kt == 0),
                                                   stop=(kt == ND - 1))
                                st[f"{tag}_mm_{tch}_{pi}"] = c.T.tok(mmi)
                            TS(_)

                            def _(g, tch=tch, pi=pi, pq=pq, aci=aci):
                                c.V.wait(g, st[f"{tag}_mm_{tch}_{pi}"],
                                         st[f"{tag}_gbb{tch}"])
                                if pi == 0:
                                    c.V.wait(g,
                                             st.get(f"{tag}_vtrm_{tch - 1}_6"),
                                             st.get(f"{tag}_vlw{tch - 1}"))
                                st[f"{tag}_evv_{tch}_{pi}"] = c.V.tok(
                                    g.scalar_tensor_tensor(
                                        qc32[:, :], ps_a[pq][:, :],
                                        aco2[:, aci:aci + 1], gb[:, :],
                                        AL.mult, AL.mult))
                            VS(_)

                            if pi < 5:
                                def _(g, tch=tch, pi=pi):
                                    c.S.wait(g, st[f"{tag}_evv_{tch}_{pi}"])
                                    c.S.wait(g, st.get(
                                        f"{tag}_rmm_{tch}_{pi - 1}" if pi
                                        else f"{tag}_rmm_{tch - 1}_4"))
                                    st[f"{tag}_sq_{tch}_{pi}"] = c.S.tok(
                                        g.activation(ucc[:, 0, :], qc32[:, :],
                                                     AF.Square))
                                SS(_)

                                def _(g, tch=tch, pi=pi):
                                    c.T.wait(g, st[f"{tag}_sq_{tch}_{pi}"],
                                             st["bo2m"])
                                    c.T.wait(g, st.get(
                                        f"{tag}_mre_{tch}_{pi - 1}" if pi
                                        else f"{tag}_mre_{tch - 1}_4"))
                                    st[f"{tag}_rmm_{tch}_{pi}"] = c.T.tok(
                                        g.matmul(ps_misc[0:2, :], bo2[:, :],
                                                 ucc[:, 0, :], start=True,
                                                 stop=True))
                                TS(_)

                                def _(g, tch=tch, pi=pi):
                                    c.V.wait(g, st[f"{tag}_rmm_{tch}_{pi}"])
                                    c.V.wait(g, st.get(
                                        f"{tag}_rbb_{tch}_{pi - 1}" if pi
                                        else f"{tag}_rbb_{tch - 1}_4"))
                                    st[f"{tag}_mre_{tch}_{pi}"] = c.V.tok(
                                        g.tensor_scalar(
                                            mrm[0:2, :], ps_misc[0:2, :],
                                            1.0 / DH, RMS_EPS, AL.mult,
                                            AL.add))
                                VS(_)

                                def _(g, tch=tch, pi=pi):
                                    c.S.wait(g, st[f"{tag}_mre_{tch}_{pi}"])
                                    t = c.S.tok(g.activation(
                                        mrm[0:2, :], mrm[0:2, :], AF.Ln))
                                    c.S.wait(g, t)
                                    st[f"{tag}_rst_{tch}_{pi}"] = c.S.tok(
                                        g.activation(mrm[0:2, :], mrm[0:2, :],
                                                     AF.Exp, scale=-0.5))
                                SS(_)

                                def _(g, tch=tch, pi=pi):
                                    c.G.wait(g, st[f"{tag}_rst_{tch}_{pi}"])
                                    c.G.wait(g, st.get(
                                        f"{tag}_rt1_{tch}_{pi - 1}" if pi
                                        else f"{tag}_rt1_{tch - 1}_4"))
                                    g.dma_start(
                                        out=rb[0:64, :],
                                        in_=bass.AP(mrm, 0,
                                                    [[512, 1], [0, 64],
                                                     [1, 512]])
                                    ).then_inc(c.Dg.sem, 16)
                                    c.Dg.n += 16
                                    st[f"{tag}_rbb_{tch}_{pi}"] = c.Dg.tok(
                                        g.dma_start(
                                            out=rb[64:P, :],
                                            in_=bass.AP(mrm, 512,
                                                        [[512, 1], [0, 64],
                                                         [1, 512]])))
                                GS(_)

                                def _(g, tch=tch, pi=pi):
                                    c.V.wait(g, st[f"{tag}_rbb_{tch}_{pi}"])
                                    t = c.V.tok(g.tensor_mul(
                                        qc32[:, :], qc32[:, :], rb[:, :]))
                                    c.V.wait(g, t)
                                    c.V.wait(g, st.get(
                                        f"{tag}_rpa_{tch}_{pi - 1}" if pi
                                        else f"{tag}_rpa_{tch - 1}_4"))
                                    t = c.V.tok(g.tensor_mul(
                                        qrt[:, :], qc32[:, :], ssc[:, :]))
                                    c.V.wait(g, t)
                                    st[f"{tag}_rt1_{tch}_{pi}"] = c.V.tok(
                                        g.tensor_mul(qc32[:, :], qc32[:, :],
                                                     ccc[:, :]))
                                VS(_)

                                def _(g, tch=tch, pi=pi):
                                    c.G.wait(g, st[f"{tag}_rt1_{tch}_{pi}"])
                                    for b2 in range(2):
                                        ba = b2 * DH
                                        g.dma_start(
                                            out=qc32[ba:ba + 32, :],
                                            in_=qrt[ba + 32:ba + DH, :],
                                            accum_op=AL.add).then_inc(
                                            c.Dg.sem, 16)
                                        c.Dg.n += 16
                                        g.dma_start(
                                            out=qc32[ba + 32:ba + DH, :],
                                            in_=qrt[ba:ba + 32, :],
                                            accum_op=AL.add).then_inc(
                                            c.Dg.sem, 16)
                                        c.Dg.n += 16
                                    st[f"{tag}_rpa_{tch}_{pi}"] = (c.Dg.sem,
                                                                   c.Dg.n)
                                GS(_)

                                def _(g, tch=tch, pi=pi):
                                    sl = slice(tch * 512, (tch + 1) * 512)
                                    c.V.wait(g, st[f"{tag}_rpa_{tch}_{pi}"],
                                             st[f"{tag}_gq8"])
                                    if pi < 4:
                                        t = c.V.tok(g.tensor_scalar(
                                            qs[:, pi, sl], qc32[:, :],
                                            gq8[:, pi:pi + 1], None, AL.mult))
                                    else:
                                        t = c.V.tok(g.tensor_copy(
                                            ks[:, sl], qc32[:, :]))
                                    st[f"{tag}_fin_{tch}_{pi}"] = t
                                VS(_)
                            else:
                                def _(g, tch=tch):
                                    c.G.wait(g, st[f"{tag}_evv_{tch}_5"])
                                    c.G.wait(g,
                                             st.get(f"{tag}_vtrm_{tch - 1}_7"))
                                    st[f"{tag}_vlw{tch}"] = c.Dg.tok(
                                        g.dma_start(out=vlow[:, :],
                                                    in_=qc32[64:P, :]))
                                GS(_)

                                for i in range(4):
                                    tk = tch * 4 + i

                                    def _(g, tch=tch, i=i):
                                        c.T.wait(g, st["idld"],
                                                 st[f"{tag}_evv_{tch}_5"])
                                        if i == 0:
                                            c.T.wait(g, st.get(
                                                f"{tag}_vtr_{tch - 1}_7"))
                                        else:
                                            c.T.wait(g, st.get(
                                                f"{tag}_vtr_{tch}_{2 * i - 1}"))
                                        st[f"{tag}_vtrm_{tch}_{2 * i}"] = (
                                            c.T.tok(g.transpose(
                                                ps_tr[:, 0:DH],
                                                qc32[0:DH, i * P:(i + 1) * P],
                                                identf[0:DH, 0:DH])))
                                    TS(_)

                                    def _(g, tch=tch, i=i, tk=tk):
                                        c.V.wait(g,
                                                 st[f"{tag}_vtrm_{tch}_{2*i}"],
                                                 st[f"{tag}_vones"])
                                        st[f"{tag}_vtr_{tch}_{2 * i}"] = (
                                            c.V.tok(g.tensor_copy(
                                                vT[:, 0, tk, 0:DH],
                                                ps_tr[:, 0:DH])))
                                    VS(_)

                                    def _(g, tch=tch, i=i):
                                        c.T.wait(g, st[f"{tag}_vlw{tch}"])
                                        c.T.wait(g,
                                                 st[f"{tag}_vtr_{tch}_{2*i}"])
                                        st[f"{tag}_vtrm_{tch}_{2 * i + 1}"] = (
                                            c.T.tok(g.transpose(
                                                ps_tr[:, 0:DH],
                                                vlow[0:DH, i * P:(i + 1) * P],
                                                identf[0:DH, 0:DH])))
                                    TS(_)

                                    def _(g, tch=tch, i=i, tk=tk):
                                        c.V.wait(
                                            g,
                                            st[f"{tag}_vtrm_{tch}_{2*i+1}"])
                                        st[f"{tag}_vtr_{tch}_{2 * i + 1}"] = (
                                            c.V.tok(g.tensor_copy(
                                                vT[:, 1, tk, 0:DH],
                                                ps_tr[:, 0:DH])))
                                    VS(_)
                fence()

                with ExitStack() as ap_:
                    mskt = ap_.enter_context(
                        nc.sbuf_tensor(f"{tag}_mskt", [P, NPT, 512], BF16))
                    esb = ap_.enter_context(
                        nc.sbuf_tensor(f"{tag}_esb", [P, 4, 512], BF16))
                    pses = [ps_a[0], ps_a[1], ps_b[1]]
                    pacc = [ps_b[0], ps_misc]
                    den = ap_.enter_context(
                        nc.sbuf_tensor(f"{tag}_den", [1, 512], F32))
                    dnb = ap_.enter_context(
                        nc.sbuf_tensor(f"{tag}_dnb", [DH, 512], F32))
                    ob = ap_.enter_context(
                        nc.sbuf_tensor(f"{tag}_ob", [DH, 512], BF16))

                    def _(g):
                        st[f"{tag}_mskld"] = c.Ds.tok(g.dma_start(
                            out=mskt[:, :, :], in_=msk_in[:, :]))
                    YS(_)

                    for h in range(8):
                        pt = h % 4
                        ro = (h // 4) * DH
                        kv = h // 4
                        for qc in range(SEQT):
                            hq = h * SEQT + qc
                            ntk = (qc + 1) * NT
                            s_base = c.S.n
                            v_base = c.V.n

                            def _(g, h=h, pt=pt, ro=ro, kv=kv, qc=qc, hq=hq,
                                  ntk=ntk, s_base=s_base, v_base=v_base):
                                c.T.wait(g, st.get(f"{tag}_onrm{hq - 2}"))
                                LA = 3
                                for step in range(ntk + LA):
                                    tk = step
                                    if tk < ntk:
                                        pq = tk % 3
                                        if tk >= 3:
                                            c.T.wait(g, (c.S.sem,
                                                         s_base + tk - 2))
                                        mmi = g.matmul(
                                            pses[pq][:, :],
                                            ks[ro:ro + DH,
                                               tk * P:(tk + 1) * P],
                                            qs[ro:ro + DH, pt,
                                               qc * 512:(qc + 1) * 512],
                                            start=True, stop=True)
                                        st[f"{tag}_smm_{h}_{qc}_{tk}"] = (
                                            c.T.tok(mmi))
                                    j = step - LA
                                    if j < 0:
                                        continue
                                    pqj = j % 3
                                    if j >= ntk - NT:
                                        c.T.wait(g, (c.V.sem,
                                                     v_base + j - (ntk - NT)
                                                     + 1))
                                    else:
                                        c.T.wait(g, (c.S.sem,
                                                     s_base + j + 1))
                                    mmi = g.matmul(pacc[hq % 2][0:DH + 1, :],
                                                   vT[:, kv, j, :],
                                                   esb[:, pqj, :],
                                                   start=(j == 0),
                                                   stop=(j == ntk - 1))
                                    t = c.T.tok(mmi)
                                    st[f"{tag}_pv_{h}_{qc}_{j}"] = t
                                    if j == ntk - 1:
                                        st[f"{tag}_pvl{hq}"] = t
                            TS(_)

                            def _(g, h=h, qc=qc, hq=hq, ntk=ntk):
                                c.S.wait(g, st.get(f"{tag}_pvl{hq - 1}"))
                                for tk in range(ntk):
                                    pq = tk % 3
                                    c.S.wait(g, st[f"{tag}_smm_{h}_{qc}_{tk}"])
                                    c.S.wait(g,
                                             st.get(f"{tag}_pv_{h}_{qc}_{tk-3}"))
                                    st[f"{tag}_es_{h}_{qc}_{tk}"] = c.S.tok(
                                        g.activation(esb[:, pq, :],
                                                     pses[pq][:, :], AF.Exp,
                                                     bias=acol[:, 3:4]))
                            SS(_)

                            def _(g, h=h, qc=qc, ntk=ntk):
                                c.V.wait(g, st[f"{tag}_mskld"])
                                for tk in range(ntk - NT, ntk):
                                    pq = tk % 3
                                    mi = tk - (ntk - NT)
                                    c.V.wait(g, st[f"{tag}_es_{h}_{qc}_{tk}"])
                                    st[f"{tag}_mk_{h}_{qc}_{tk}"] = (
                                        c.V.tok(g.tensor_mul(
                                            esb[:, pq, :], esb[:, pq, :],
                                            mskt[:, mi, :])))
                            VS(_)

                            def _(g, hq=hq):
                                c.V.wait(g, st[f"{tag}_pvl{hq}"])
                                st[f"{tag}_den{hq}"] = c.V.tok(g.reciprocal(
                                    den[0:1, :], pacc[hq % 2][DH:DH + 1, :]))
                            VS(_)

                            def _(g, hq=hq):
                                c.G.wait(g, st[f"{tag}_den{hq}"])
                                st[f"{tag}_dnb{hq}"] = c.Dg.tok(g.dma_start(
                                    out=dnb[:, :],
                                    in_=bass.AP(den, 0, [[512, 1], [0, DH],
                                                         [1, 512]])))
                            GS(_)

                            def _(g, hq=hq):
                                c.V.wait(g, st[f"{tag}_dnb{hq}"])
                                st[f"{tag}_onrm{hq}"] = c.V.tok(g.tensor_mul(
                                    ob[:, :], pacc[hq % 2][0:DH, :],
                                    dnb[:, :]))
                            VS(_)

                            def _(g, pt=pt, ro=ro, qc=qc, hq=hq):
                                c.G.wait(g, st[f"{tag}_onrm{hq}"])
                                st[f"{tag}_oev{hq}"] = c.Dg.tok(g.dma_start(
                                    out=bass.AP(y_hbm,
                                                (pt * P + ro) * SEQ + qc * 512,
                                                [[SEQ, DH], [1, 512]]),
                                    in_=ob[:, :]))
                            GS(_)
                    fence()
                bcw.close()
                cds.close()

                wpT = sb("wpT", [P, NPT, D], BF16)
                with ExitStack() as ws2:
                    wfp2 = ws2.enter_context(
                        nc.sbuf_tensor(f"{tag}_wfp2", [P, D], F32))
                    wab2 = ws2.enter_context(
                        nc.sbuf_tensor(f"{tag}_wab2", [P, D], F32))
                    acc2 = ws2.enter_context(
                        nc.sbuf_tensor(f"{tag}_acc2", [P, 2], F32))
                    tp = walpha(wp_td, D, D * D, colt[:, 2:3], aco2[:, 3:4],
                                "p", wfp2, wab2, acc2, None)
                    st[f"{tag}_wpt"] = wtern(wp_td, D, NPT, D, wpT,
                                             colt[:, 2:3], tp, "p", wfp2,
                                             wab2)
                fence()

                with ExitStack() as yp:
                    ysc = yp.enter_context(
                        nc.sbuf_tensor(f"{tag}_ysc", [P, NPT, 512], BF16))
                    yq = yp.enter_context(
                        nc.sbuf_tensor(f"{tag}_yq", [P, NPT, 512], BF16))
                    rgy = yp.enter_context(
                        nc.sbuf_tensor(f"{tag}_rgy", [1, SEQ], F32))
                    gya = yp.enter_context(
                        nc.sbuf_tensor(f"{tag}_gya", [1, SEQ], F32))
                    gyb = yp.enter_context(
                        nc.sbuf_tensor(f"{tag}_gyb", [P, 512], F32))
                    evf = yp.enter_context(
                        nc.sbuf_tensor(f"{tag}_evf", [P, 2, 512], F32))

                    for tch in range(SEQT):
                        def _(g, tch=tch):
                            c.Ds.wait(g, st.get(f"{tag}_ym{tch - 1}"))
                            st[f"{tag}_yld{tch}"] = c.Ds.tok(g.dma_start(
                                out=ysc[:, :, :],
                                in_=bass.AP(y_hbm, tch * 512,
                                            [[SEQ, P], [P * SEQ, NPT],
                                             [1, 512]])))
                        YS(_)

                        for pt in range(NPT):
                            def _(g, tch=tch, pt=pt):
                                c.S.wait(g, st[f"{tag}_yld{tch}"])
                                c.S.wait(g, st.get(f"{tag}_yab_{tch}_{pt - 2}"))
                                st[f"{tag}_yas_{tch}_{pt}"] = c.S.tok(
                                    g.activation(ucc[:, pt % 2, :],
                                                 ysc[:, pt, :], AF.Abs))
                            SS(_)

                            def _(g, tch=tch, pt=pt):
                                c.V.wait(g, st[f"{tag}_yas_{tch}_{pt}"])
                                if pt == 0:
                                    t = c.V.tok(g.tensor_copy(amf[:, :],
                                                              ucc[:, 0, :]))
                                else:
                                    t = c.V.tok(g.tensor_max(
                                        amf[:, :], amf[:, :],
                                        ucc[:, pt % 2, :]))
                                st[f"{tag}_yab_{tch}_{pt}"] = t
                            VS(_)

                        st[f"{tag}_ym{tch}"] = fold_rowmax(
                            st[f"{tag}_yab_{tch}_{NPT - 1}"], f"ym{tch}",
                            rgy, SEQ, tch * 512)

                    def _(g):
                        c.G.wait(g, st[f"{tag}_ym{SEQT - 1}"])
                        st[f"{tag}_ysnd"] = c.Dg.tok(g.dma_start(
                            out=ary_in.ap(), in_=rgy[0:1, :]))
                        c.G.wait(g, st[f"{tag}_ysnd"])
                        cc = g.collective_compute(
                            "AllReduce", AL.max,
                            replica_groups=[[0, 1], [2, 3], [4, 5], [6, 7]],
                            ins=[ary_in.ap().opt()], outs=[ary_out.ap().opt()])
                        st[f"{tag}_ary"] = c.CC.tok(cc)
                        c.G.wait(g, st[f"{tag}_ary"])
                        st[f"{tag}_yrld"] = c.Dg.tok(g.dma_start(
                            out=rgy[0:1, :], in_=ary_out.ap()))
                    GS(_)

                    def _(g):
                        c.V.wait(g, st[f"{tag}_yrld"], st[f"{tag}_wpt"])
                        t = c.V.tok(g.tensor_scalar(
                            rgy[0:1, :], rgy[0:1, :], 1e-8, 1.0 / 127.0,
                            AL.max, AL.mult))
                        c.V.wait(g, t)
                        t = c.V.tok(g.tensor_scalar(
                            gya[0:1, :], rgy[0:1, :], aco2[0:1, 3:4], None,
                            AL.mult))
                        c.V.wait(g, t)
                        st[f"{tag}_gyr"] = c.V.tok(g.reciprocal(
                            rgy[0:1, :], rgy[0:1, :]))
                    VS(_)

                    for tch in range(SEQT):
                        blk, lch = tch // NT, tch % NT

                        def _(g, tch=tch):
                            c.Ds.wait(g, st[f"{tag}_gyr"])
                            c.Ds.wait(g,
                                      st.get(f"{tag}_yqv_{tch - 1}_{NPT - 1}"))
                            st[f"{tag}_yl2{tch}"] = c.Ds.tok(g.dma_start(
                                out=ysc[:, :, :],
                                in_=bass.AP(y_hbm, tch * 512,
                                            [[SEQ, P], [P * SEQ, NPT],
                                             [1, 512]])))
                        YS(_)

                        def _(g, tch=tch):
                            c.G.wait(g, st[f"{tag}_gyr"])
                            c.G.wait(g,
                                     st.get(f"{tag}_pex_{tch - 1}_{ND - 1}"))
                            g.dma_start(out=gb[:, :],
                                        in_=bass.AP(rgy, tch * 512,
                                                    [[SEQ, 1], [0, P],
                                                     [1, 512]])
                                        ).then_inc(c.Dg.sem, 16)
                            c.Dg.n += 16
                            st[f"{tag}_gyb{tch}"] = c.Dg.tok(g.dma_start(
                                out=gyb[:, :],
                                in_=bass.AP(gya, tch * 512,
                                            [[SEQ, 1], [0, P], [1, 512]])))
                        GS(_)

                        for pt in range(NPT):
                            def _(g, tch=tch, pt=pt):
                                c.V.wait(g, st[f"{tag}_yl2{tch}"],
                                         st[f"{tag}_gyb{tch}"])
                                if pt == 0:
                                    c.V.wait(g, st.get(
                                        f"{tag}_pmm_{tch - 1}_{ND - 1}"))
                                t = c.V.tok(g.tensor_mul(
                                    ucc[:, pt % 2, :], ysc[:, pt, :],
                                    gb[:, :]))
                                c.V.wait(g, t)
                                st[f"{tag}_yqv_{tch}_{pt}"] = c.V.tok(
                                    g.tensor_scalar(
                                        yq[:, pt, :], ucc[:, pt % 2, :],
                                        MAGIC, MAGIC, AL.add, AL.subtract))
                            VS(_)

                        for dt in range(ND):
                            pq = dt % 2

                            def _(g, tch=tch, dt=dt, pq=pq):
                                c.T.wait(g, st[f"{tag}_wpt"],
                                         st[f"{tag}_yqv_{tch}_{NPT - 1}"])
                                c.T.wait(g, st.get(
                                    f"{tag}_pev_{tch}_{dt - 2}" if dt >= 2
                                    else f"{tag}_pev_{tch - 1}_{dt + 6}"))
                                for kt in range(NPT):
                                    mmi = g.matmul(
                                        ps_a[pq][:, :],
                                        wpT[:, kt, dt * P:(dt + 1) * P],
                                        yq[:, kt, :],
                                        start=(kt == 0), stop=(kt == NPT - 1))
                                st[f"{tag}_pmm_{tch}_{dt}"] = c.T.tok(mmi)
                            TS(_)

                            def _(g, tch=tch, dt=dt, pq=pq):
                                c.V.wait(g, st[f"{tag}_pmm_{tch}_{dt}"])
                                c.V.wait(g, st.get(
                                    f"{tag}_pex_{tch}_{dt - 2}" if dt >= 2
                                    else f"{tag}_pex_{tch - 1}_{dt + 6}"))
                                st[f"{tag}_pev_{tch}_{dt}"] = c.V.tok(
                                    g.tensor_mul(evf[:, pq, :],
                                                 ps_a[pq][:, :], gyb[:, :]))
                            VS(_)

                            def _(g, tch=tch, dt=dt, pq=pq, blk=blk,
                                  lch=lch):
                                c.G.wait(g, st[f"{tag}_pev_{tch}_{dt}"])
                                st[f"{tag}_pex_{tch}_{dt}"] = c.Dg.tok(
                                    g.dma_start(
                                        out=bass.AP(
                                            pps_hbm,
                                            blk * D * TL + dt * P * TL
                                            + lch * 512,
                                            [[TL, P], [1, 512]]),
                                        in_=evf[:, pq, :]))
                            GS(_)

                    def _(g):
                        c.G.wait(g, st[f"{tag}_pex_{SEQT - 1}_{ND - 1}"])
                        cc = g.collective_compute(
                            "ReduceScatter", AL.add,
                            replica_groups=[[0, 1], [2, 3], [4, 5], [6, 7]],
                            ins=[pps_hbm.ap().opt()],
                            outs=[rsy_out.ap().opt()])
                        st[f"{tag}_rs"] = c.CC.tok(cc)
                    GS(_)

                    for ch in range(NT):
                        for dt in range(ND):
                            idx = ch * ND + dt
                            pq = idx % 2

                            def _(g, dt=dt, ch=ch, idx=idx, pq=pq):
                                c.Ds.wait(g, st[f"{tag}_rs"])
                                c.Ds.wait(g, st.get(f"{tag}_xf_{idx - 2}"))
                                st[f"{tag}_rl_{idx}"] = c.Ds.tok(g.dma_start(
                                    out=evf[:, pq, :],
                                    in_=bass.AP(rsy_out,
                                                dt * P * TL + ch * 512,
                                                [[TL, P], [1, 512]])))
                            YS(_)

                            def _(g, dt=dt, ch=ch, idx=idx, pq=pq):
                                sl = slice(ch * 512, (ch + 1) * 512)
                                c.V.wait(g, st[f"{tag}_rl_{idx}"])
                                st[f"{tag}_xf_{idx}"] = c.V.tok(g.tensor_add(
                                    X[:, dt, sl], X[:, dt, sl],
                                    evf[:, pq, :]))
                            VS(_)

                    st[f"{tag}_done"] = st[f"{tag}_xf_{NT * ND - 1}"]
            fence()
            return st[f"{tag}_done"]

        dep = st["xld"]
        if stage >= 1:
            dep = emit_l0_ffn_sublayer("f0", dep)
        if stage >= 2:
            dep = emit_mamba("m0", dep)
        if stage >= 3:
            dep = emit_l0_ffn_sublayer("f1", dep)
        if stage >= 4:
            dep = emit_l1_ffn("q0", dep, True)
        if stage >= 5:
            dep = emit_l1_attn("a1", dep)
        if stage >= 6:
            dep = emit_l1_ffn("q1", dep, False)

        def _(g):
            c.Ds.wait(g, dep)
            t = c.Ds.tok(g.dma_start(
                out=bass.AP(xout, 0, [[TL, P], [P * TL, ND], [1, TL]]),
                in_=X[:, :, :]))
            c.Ds.wait(g, t)
        YS(_)

    return nc


def _prep_inputs(inputs):
    f32 = np.float32
    x = np.asarray(inputs["L0_attn_u"], f32)
    ident = np.eye(P, dtype=f32)
    w1 = np.ascontiguousarray(np.asarray(inputs["L0_ffn_W1"], f32))
    w2 = np.ascontiguousarray(np.asarray(inputs["L0_ffn_W2"], f32))
    wrv = np.ascontiguousarray(np.asarray(inputs["L0_ffn_Wr"], f32))
    br = np.asarray(inputs["L0_ffn_br"], f32).reshape(NE, 1)
    b1 = np.asarray(inputs["L0_ffn_b1"], f32)
    b1tv = np.ascontiguousarray(b1.reshape(NE, NH, P).transpose(0, 2, 1))
    b2 = np.ascontiguousarray(np.asarray(inputs["L0_ffn_b2"], f32))
    wdt = np.ascontiguousarray(np.asarray(inputs["L0_attn_W_dt"], f32).T)
    wb = np.ascontiguousarray(np.asarray(inputs["L0_attn_W_B"], f32).T)
    wc = np.ascontiguousarray(np.asarray(inputs["L0_attn_W_C"], f32).T)
    bdt = np.ascontiguousarray(
        np.asarray(inputs["L0_attn_b_dt"], f32).reshape(ND, P).T)
    loga = np.ascontiguousarray(
        np.asarray(inputs["L0_attn_log_A"], f32).reshape(ND, P, NS))
    dpar = np.ascontiguousarray(
        np.asarray(inputs["L0_attn_D_param"], f32).reshape(ND, P).T)
    wg = np.ascontiguousarray(np.asarray(inputs["L1_ffn_w_gate"], f32))
    wu = np.ascontiguousarray(np.asarray(inputs["L1_ffn_w_up"], f32))
    wd = np.ascontiguousarray(np.asarray(inputs["L1_ffn_w_down"], f32))
    wq = np.asarray(inputs["L1_attn_Wq"], f32)
    wk = np.asarray(inputs["L1_attn_Wk"], f32)
    wv = np.asarray(inputs["L1_attn_Wv"], f32)
    wp = np.asarray(inputs["L1_attn_Wp"], f32)
    qg = np.asarray(inputs["L1_attn_q_gain"], f32).reshape(-1)
    invf = 1.0 / 10000.0 ** (np.arange(0, DH, 2, dtype=np.float64) / DH)
    fr = np.outer(invf, np.arange(SEQ, dtype=np.float64))
    cosr, sinr = np.cos(fr), np.sin(fr)
    cct = np.ascontiguousarray(np.tile(cosr, (4, 1)).astype(f32))
    sst = np.ascontiguousarray(
        np.concatenate([-sinr, sinr, -sinr, sinr], axis=0).astype(f32))
    jj = np.arange(512)
    pp = np.arange(P)
    mskb = np.ascontiguousarray(np.stack(
        [(jj[None, :] >= (mi * P + pp)[:, None]) for mi in range(4)],
        axis=1).reshape(P, 4 * 512).astype(ml_dtypes.bfloat16))
    per_core = []
    attn_pc = []
    for half in range(2):
        myh = [8 * half + pt + 4 * ab for pt in range(4) for ab in range(2)]
        oth = [8 * (1 - half) + pt + 4 * ab for pt in range(4)
               for ab in range(2)]
        csel = np.concatenate(
            [np.arange(h * DH, (h + 1) * DH) for h in myh + oth])
        mykv = [2 * half, 2 * half + 1]
        okv = [2 * (1 - half), 2 * (1 - half) + 1]
        ksel = np.concatenate(
            [np.arange(k * DH, (k + 1) * DH) for k in mykv + okv])
        attn_pc.append({
            "wq_td": np.ascontiguousarray(wq.T[:, csel]),
            "wk_td": np.ascontiguousarray(wk.T[:, ksel]),
            "wv_td": np.ascontiguousarray(wv.T[:, ksel]),
            "wp_td": np.ascontiguousarray(wp.T[csel, :]),
            "qg_in": np.ascontiguousarray(
                qg[myh + oth].reshape(1, NHEAD)),
        })
    for core in range(NCORES):
        b, half = core // 2, core % 2
        xs = np.ascontiguousarray(x[b, half * TL:(half + 1) * TL, :].T)
        hm = np.full((P, 1), float(half), f32)
        per_core.append({
            "xin": xs, "ident_in": ident, "halfm_in": hm,
            "l0w1": w1, "l0w2": w2, "l0wr": wrv, "l0br": br,
            "l0b1t": b1tv, "l0b2": b2,
            "wdt_t": wdt, "wb_t": wb, "wc_t": wc,
            "bdt_d": bdt, "loga_d": loga, "dpar_d": dpar,
            "wg_d": wg, "wu_d": wu, "wd_d": wd,
            "cc_in": cct, "ss_in": sst, "msk_in": mskb,
            **attn_pc[half],
        })
    return per_core


def kernel(**inputs):
    per_core = _prep_inputs(inputs)
    nc = build(BUILD_STAGE)
    res = run_bass_kernel_spmd(nc, per_core, core_ids=list(range(NCORES)))
    out = np.empty((4, SEQ, D), np.float32)
    for core in range(NCORES):
        b, half = core // 2, core % 2
        out[b, half * TL:(half + 1) * TL, :] = res.results[core]["xout"].T
    return out
